# version 12
# baseline (speedup 1.0000x reference)
import numpy as np
from contextlib import ExitStack

import concourse.bass as bass
import concourse.mybir as mybir
from concourse.bass_utils import run_bass_kernel_spmd

F32 = mybir.dt.float32
BF16 = mybir.dt.bfloat16
FP8 = mybir.dt.float8e4
I32 = mybir.dt.int32
AF = mybir.ActivationFunctionType
OP = mybir.AluOpType
AX = mybir.AxisListType

D = 1024
T = 2048
NDT = 8
NTT = 16
NQC = 4
H = 16
DH = 64
NG = 2
MOE_E = 8
LN_EPS = 1e-5
FQ_EPS = float(np.finfo(np.float32).eps)
AM_EPS = 1e-6
BIG = 1e30

ALL_PHASES = ["ln1", "attn0", "proj0", "ln2", "router", "moe",
              "ln3", "attn1", "proj1", "ln4", "ffn1"]


class Sem:
    def __init__(self, nc, stk, name):
        self.h = stk.enter_context(nc.semaphore(name))
        self.v = 0

    def inc(self, instr, n=1):
        instr.then_inc(self.h, n)
        self.v += n
        return self.v


class MK:
    def __init__(self, phases=None, dump=None):
        self.dump = dump
        self.dump_layer = 0
        self.phases = ALL_PHASES if phases is None else phases
        self.nc = bass.Bass("TRN2", target_bir_lowering=False)
        self.stk = ExitStack()
        self.pool = []

    def sb(self, name, shape, dtype):
        return self.stk.enter_context(self.nc.sbuf_tensor(name, shape, dtype))

    def palloc(self, name, shape, dtype):
        cm = self.nc.sbuf_tensor(name, shape, dtype)
        h = cm.__enter__()
        self.pool.append(cm)
        return h

    def pfree(self, n=None):
        k = len(self.pool) if n is None else n
        for _ in range(k):
            self.pool.pop().__exit__(None, None, None)

    def barrier(self):
        nc, s = self.nc, self.s_bar
        for eng in (nc.tensor, nc.vector, nc.scalar, nc.sync):
            eng.drain()
            eng.sem_inc(s.h, 1)
            s.v += 1
        nc.gpsimd.sem_inc(s.h, 1)
        s.v += 1
        for eng in (nc.tensor, nc.vector, nc.scalar, nc.gpsimd, nc.sync):
            eng.wait_ge(s.h, s.v)

    def bank(self, i, n=1):
        return self.PS[:, 512 * i:512 * (i + n)]

    def build(self):
        nc = self.nc
        s = self.stk
        di = lambda name, shape: nc.dram_tensor(name, shape, F32, kind="ExternalInput").ap()
        self.xT = di("xT", [D, T])
        self.qkvt = di("qkvt", [24, 128, 8, 128])
        self.qkvb = di("qkvb", [128, 24])
        self.wot = di("wot", [8, 128, 8, 128])
        self.wob = di("wob", [128, 8])
        self.rwt = di("rwt", [128, 8, 8])
        self.w1t = di("w1t", [MOE_E, 16, 128, 8, 256])
        self.w2t = di("w2t", [MOE_E, 8, 128, 4, 1024])
        self.l1w = di("l1w", [4, D, D])
        self.l1wt = di("l1wt", [4, D, D])
        self.l1b = di("l1b", [128, 32])
        self.wfct = di("wfct", [D, 4096])
        self.lnwb = di("lnwb", [128, 64])
        self.wprt = di("wprt", [4096, D])
        self.eye_in = di("eye", [128, 128])
        self.outT = nc.dram_tensor("outT", [D, T], F32, kind="ExternalOutput").ap()

        self.X = self.sb("X", [128, NDT, T], F32)
        self.XN = self.sb("XN", [128, NDT, T], BF16)
        self.SM0 = self.sb("SM0", [128, 2048], F32)
        self.ONES1 = self.sb("ONES1", [128, 128], F32)
        self.ONESB = self.sb("ONESB", [128, 2], BF16)
        self.ONE2F = self.sb("ONE2F", [128, 2], F32)
        self.EYE = self.sb("EYE", [128, 128], F32)
        self.QKVB = self.sb("QKVB", [128, 24], F32)
        self.QKVBV = self.sb("QKVBV", [128, 8], BF16)
        self.WOBS = self.sb("WOBS", [128, 8], F32)
        self.L1BS = self.sb("L1BS", [128, 32], F32)
        self.L1BV = self.sb("L1BV", [128, 8], BF16)
        self.LNW = self.sb("LNW", [128, 32], F32)
        self.LNWB16 = self.sb("LNWB16", [128, 32], BF16)
        self.LNBB16 = self.sb("LNBB16", [128, 32], BF16)
        self.NEGM8 = self.sb("NEGM8", [128, 4], F32)
        self.RW = self.sb("RW", [128, 8, 8], BF16)
        self.SCT = self.sb("SCT", [8, 128], F32)
        self.SCC = self.sb("SCC", [128, 8], F32)
        self.SMLL = self.sb("SMLL", [128, 4], F32)
        self.WEB = self.sb("WEB", [128, 8], F32)
        self.INVGC = self.sb("INVGC", [128, 2], F32)
        self.PBIA = self.sb("PBIA", [128, 16], F32)
        self.T1 = self.sb("T1", [128, 2, 512], F32)
        self.EPSC = self.sb("EPSC", [128, 1], F32)
        self.SQG = None
        self.PS = s.enter_context(nc.psum_tensor("PS", [128, 4096], F32))

        self.s_ld = Sem(nc, s, "s_ld")
        self.s_gw = Sem(nc, s, "s_gw")
        self.s_st = Sem(nc, s, "s_st")
        self.s_tp = Sem(nc, s, "s_tp")
        self.s_sc = Sem(nc, s, "s_sc")
        self.s_vc = Sem(nc, s, "s_vc")
        self.s_bar = Sem(nc, s, "s_bar")
        self.s_wr = [Sem(nc, s, f"s_wr{i}") for i in range(2)]
        self.s_w1 = [Sem(nc, s, f"s_w1{i}") for i in range(3)]
        self.s_w2 = [Sem(nc, s, f"s_w2{i}") for i in range(4)]
        self.s_q = [Sem(nc, s, f"s_q{i}") for i in range(2)]
        self.s_gr = Sem(nc, s, "s_gr")

        ph = self.phases
        self.p_load()
        self.barrier()
        steps = [("ln1", self.p_ln, ()), ("attn0", self.p_attn, (0,)),
                 ("proj0", self.p_outproj, (0,)), ("ln2", self.p_ln, ()),
                 ("router", self.p_router, ()), ("moe", self.p_moe, ()),
                 ("ln3", self.p_ln, ()), ("attn1", self.p_attn, (1,)),
                 ("proj1", self.p_outproj, (1,)), ("ln4", self.p_ln, ()),
                 ("ffn1", self.p_ffn1, ())]
        prev = None
        for name, fn, args in steps:
            if name not in ph:
                continue
            fuse_next = name in ("proj0", "proj1") and \
                (("ln2" in ph) if name == "proj0" else ("ln4" in ph))
            if fuse_next:
                fn(*args, keep_pool=True)
            elif name in ("ln2", "ln4") and prev in ("proj0", "proj1"):
                fn(dep_tok=self.proj_tok)
                self.barrier()
                self.pfree()
            else:
                fn(*args)
                self.barrier()
            prev = name
        self.p_store()
        self.pfree()
        self.stk.close()
        return self.nc

    def p_load(self):
        nc = self.nc
        for dt in range(NDT):
            self.s_ld.inc(nc.sync.dma_start(self.X[:, dt, :], self.xT[dt * 128:(dt + 1) * 128, :]), 16)
        for dst, src in ((self.QKVB, self.qkvb[:]), (self.WOBS, self.wob[:]),
                         (self.L1BS, self.l1b[:]), (self.LNW, self.lnwb[:, 0:32]),
                         (self.EYE, self.eye_in[:])):
            self.s_ld.inc(nc.sync.dma_start(dst[:], src), 16)
        self.s_gw.inc(nc.gpsimd.dma_start(self.RW[:], self.rwt[:]), 16)
        self.s_gw.inc(nc.gpsimd.dma_start(self.LNWB16[:], self.lnwb[:, 0:32]), 16)
        self.s_gw.inc(nc.gpsimd.dma_start(self.LNBB16[:], self.lnwb[:, 32:64]), 16)
        nc.vector.wait_ge(self.s_ld.h, self.s_ld.v)
        nc.vector.wait_ge(self.s_gw.h, self.s_gw.v)
        nc.vector.memset(self.ONES1[:], 1.0)
        nc.vector.memset(self.EPSC[:], LN_EPS)
        nc.vector.memset(self.ONESB[:], 1.0)
        nc.vector.memset(self.ONE2F[:], 1.0)
        nc.vector.tensor_copy(self.QKVBV[:], self.QKVB[:, 16:24])
        nc.vector.tensor_copy(self.L1BV[:], self.L1BS[:, 16:24])

    def p_ln(self, dep_tok=None):
        self._ln_n = getattr(self, "_ln_n", 0) + 1
        """XN = (X - mu) * rsqrt(var+eps) in bf16, two 1024-token half-passes.
        PSUM per half: sum(x) banks4-5, sum(x^2) banks6-7; then MUB banks4-5,
        RB banks6-7 broadcasts.  dep_tok[dt] (s_vc) gates reading X[:, dt]."""
        nc, X, XN, SM0, PS = self.nc, self.X, self.XN, self.SM0, self.PS
        XSQ = self.palloc(f"XSQ{self._ln_n}", [128, 2, 512], F32)
        LNTS = self.palloc(f"LNTS{self._ln_n}", [128, 2, 512], F32)
        for half in range(2):
            base = half * 1024
            t_sq0 = self.s_tp.v
            sq_tok = []
            for q2 in range(2):
                for dt in range(NDT):
                    i = q2 * NDT + dt
                    if i >= 2:
                        nc.scalar.wait_ge(self.s_tp.h, t_sq0 + (i - 1))
                    if dep_tok is not None and dep_tok[dt] is not None and half == 0 and q2 == 0:
                        nc.scalar.wait_ge(self.s_vc.h, dep_tok[dt])
                    ins = nc.scalar.activation(XSQ[:, i % 2, :],
                                               X[:, dt, base + q2 * 512: base + (q2 + 1) * 512], AF.Square)
                    sq_tok.append(self.s_sc.inc(ins))
            for q2 in range(2):
                for dt in range(NDT):
                    i = q2 * NDT + dt
                    if dep_tok is not None and dep_tok[dt] is not None:
                        nc.tensor.wait_ge(self.s_vc.h, dep_tok[dt])
                    nc.tensor.matmul(PS[0:1, 2048 + q2 * 512:2560 + q2 * 512], self.ONE2F[:, 0:1],
                                     X[:, dt, base + q2 * 512: base + (q2 + 1) * 512],
                                     start=(dt == 0), stop=(dt == NDT - 1))
                    nc.tensor.wait_ge(self.s_sc.h, sq_tok[i])
                    ins = nc.tensor.matmul(PS[0:1, 3072 + q2 * 512:3584 + q2 * 512], self.ONE2F[:, 0:1],
                                           XSQ[:, i % 2, :], start=(dt == 0), stop=(dt == NDT - 1))
                    self.s_tp.inc(ins)
            t_stats = self.s_tp.v
            nc.vector.wait_ge(self.s_tp.h, t_stats)
            nc.vector.tensor_scalar(SM0[0:1, 0:1024], PS[0:1, 2048:3072], 1.0 / D, None, op0=OP.mult)
            nc.vector.tensor_scalar(SM0[0:1, 1024:2048], PS[0:1, 3072:4096], 1.0 / D, None, op0=OP.mult)
            nc.vector.drain()
            nc.vector.tensor_tensor(PS[0:1, 2048:3072], SM0[0:1, 0:1024], SM0[0:1, 0:1024], op=OP.mult)
            nc.vector.drain()
            v_var = self.s_vc.inc(nc.vector.tensor_tensor(
                SM0[0:1, 1024:2048], SM0[0:1, 1024:2048], PS[0:1, 2048:3072], op=OP.subtract))
            nc.scalar.wait_ge(self.s_vc.h, v_var)
            s_sqrt = self.s_sc.inc(nc.scalar.activation(SM0[0:1, 1024:2048], SM0[0:1, 1024:2048],
                                                        AF.Sqrt, bias=self.EPSC[0:1, 0:1]))
            nc.vector.wait_ge(self.s_sc.h, s_sqrt)
            v_rstd = self.s_vc.inc(nc.vector.reciprocal(SM0[0:1, 1024:2048], SM0[0:1, 1024:2048]))
            nc.tensor.wait_ge(self.s_vc.h, v_rstd)
            t_bc = None
            for off in (0, 512):
                nc.tensor.matmul(PS[:, 2048 + off:2560 + off], self.ONES1[0:1, 0:128],
                                 SM0[0:1, off:off + 512], start=True, stop=True)
                ins = nc.tensor.matmul(PS[:, 3072 + off:3584 + off], self.ONES1[0:1, 0:128],
                                       SM0[0:1, 1024 + off:1536 + off], start=True, stop=True)
                t_bc = self.s_tp.inc(ins)
            nc.vector.wait_ge(self.s_tp.h, t_bc)
            lastv = None
            for dt in range(NDT):
                for q2 in range(2):
                    sl = slice(base + q2 * 512, base + (q2 + 1) * 512)
                    nc.vector.tensor_tensor(LNTS[:, q2, :], X[:, dt, sl],
                                            PS[:, 2048 + q2 * 512:2560 + q2 * 512], op=OP.subtract)
                    nc.vector.drain()
                    ins = nc.vector.tensor_tensor(XN[:, dt, sl], LNTS[:, q2, :],
                                                  PS[:, 3072 + q2 * 512:3584 + q2 * 512], op=OP.mult)
                    lastv = self.s_vc.inc(ins)
            nc.tensor.wait_ge(self.s_vc.h, lastv)
        self.pfree(2)

    def p_attn(self, layer):
        nc, PS, XN = self.nc, self.PS, self.XN
        O = self.palloc(f"O{layer}", [128, NDT, T], BF16)
        QT = self.palloc(f"QT{layer}", [128, 4, T], BF16)
        KT = self.palloc(f"KT{layer}", [128, 4, T], BF16)
        V = self.palloc(f"V{layer}", [128, NTT, 512], BF16)
        EB = self.palloc(f"EB{layer}", [128, 2, 1024], BF16)
        SQB = EB[:, :, :].rearrange("p a b -> p (a b)")
        self.DIBS = self.T1[:, 1, :]
        if layer == 0:
            WR = self.palloc("WR", [128, 2, 8, 128], BF16)
        else:
            self.QI = self.palloc("QIa", [128, 256], I32)
            WQ = self.palloc("WQ", [128, 8, 512], BF16)
            STG = self.palloc("STG", [128, 2, 512], F32)
        self.o_tok = [[None] * NQC for _ in range(NDT)]

        slot_free = [None, None]
        for g in range(NG):
            nc.tensor.wait_ge(self.s_sc.h, self.s_sc.v)
            nc.tensor.wait_ge(self.s_vc.h, self.s_vc.v)
            if layer == 0:
                chunks = []
                for m in range(3):
                    for ftl in range(4):
                        chunks.append((m, ftl, m * 8 + g * 4 + ftl))
                for ci, (m, ftl, fc) in enumerate(chunks):
                    sl = ci % 2
                    if slot_free[sl] is not None:
                        nc.gpsimd.wait_ge(self.s_tp.h, slot_free[sl])
                    sem = self.s_wr[sl]
                    gw = (sem, sem.inc(nc.gpsimd.dma_start(WR[:, sl, :, :], self.qkvt[fc]), 16))
                    self.emit_qkv_chunk(layer, g, m, ftl, WR[:, sl, :, :], gw, QT, KT, V)
                    slot_free[sl] = self.s_tp.v
            else:
                for m in range(3):
                    self.quant_l1(m, g, WQ, STG)
                    self.emit_qkv_chunk(layer, g, m, None, WQ, None, QT, KT, V)
            self.emit_mhat(QT, KT, SQB)
            for lt in range(4):
                for qc in range(NQC):
                    self.emit_attn_pair(layer, g, lt, qc, QT, KT, V, EB, O)
        if self.dump in ("QK", "V"):
            pass
        else:
            self.pfree(len(self.pool) - 1)

    def emit_qkv_chunk(self, layer, g, m, ftl, W, gw_tok, QT, KT, V):
        nc, PS, XN = self.nc, self.PS, self.XN
        ftls = [ftl] if layer == 0 else list(range(4))
        last_tok = None
        for lf in ftls:
            wsl = (lambda dt: W[:, dt, lf * 128:(lf + 1) * 128]) if layer == 1 else \
                  (lambda dt: W[:, dt, :])
            fglob = (m * 8 if layer == 0 else m * 8) + g * 4 + lf
            if m < 2:
                dst = QT if m == 0 else KT
                bias = self.QKVB[:, fglob:fglob + 1] if layer == 0 else \
                    self.L1BS[:, m * 8 + g * 4 + lf: m * 8 + g * 4 + lf + 1]
                for qc in range(NQC):
                    bk = self.bank(qc % 2)
                    if self.qkv_bank_tok[qc % 2] is not None:
                        nc.tensor.wait_ge(self.s_vc.h, self.qkv_bank_tok[qc % 2])
                    if gw_tok is not None and qc == 0:
                        nc.tensor.wait_ge(gw_tok[0].h, gw_tok[1])
                    for dt in range(NDT):
                        ins = nc.tensor.matmul(bk, wsl(dt), XN[:, dt, qc * 512:(qc + 1) * 512],
                                               start=(dt == 0), stop=(dt == NDT - 1))
                    t = self.s_tp.inc(ins)
                    nc.vector.wait_ge(self.s_tp.h, t)
                    ins = nc.vector.tensor_scalar(dst[:, lf, qc * 512:(qc + 1) * 512], bk,
                                                  bias, None, op0=OP.add)
                    self.qkv_bank_tok[qc % 2] = self.s_vc.inc(ins)
                    last_tok = self.qkv_bank_tok[qc % 2]
            else:
                width = 128 if layer == 0 else 512
                fcol = lf * 128 if layer == 0 else 0
                mov = (lambda dt: W[:, dt, :]) if layer == 0 else (lambda dt: W[:, dt, 0:512])
                for tt in range(NTT):
                    bk = self.bank(2 + tt % 2)[:, 0:width]
                    if self.qkv_bank_tok[2 + tt % 2] is not None:
                        nc.tensor.wait_ge(self.s_vc.h, self.qkv_bank_tok[2 + tt % 2])
                    if gw_tok is not None and tt == 0:
                        nc.tensor.wait_ge(gw_tok[0].h, gw_tok[1])
                    for dt in range(NDT):
                        ins = nc.tensor.matmul(bk, XN[:, dt, tt * 128:(tt + 1) * 128], mov(dt),
                                               start=(dt == 0), stop=(dt == NDT - 1))
                    t = self.s_tp.inc(ins)
                    nc.vector.wait_ge(self.s_tp.h, t)
                    ins = nc.vector.tensor_copy(V[:, tt, fcol:fcol + width], bk)
                    self.qkv_bank_tok[2 + tt % 2] = self.s_vc.inc(ins)
                    last_tok = self.qkv_bank_tok[2 + tt % 2]
                if layer == 1:
                    break
        return last_tok

    def quant_l1(self, m, g, WQ, STG):
        nc, PS = self.nc, self.PS
        nc.vector.wait_ge(self.s_tp.h, self.s_tp.v)
        nc.vector.wait_ge(self.s_gr.h, self.s_gr.v)
        for ft in range(4):
            fr = (g * 4 + ft) * 128
            stgv = STG[:, :, :].rearrange("p a b -> p (a b)")
            nc.sync.wait_ge(self.s_vc.h, self.s_vc.v)
            ld = self.s_q[0].inc(nc.sync.dma_start(stgv, self.l1w[m, fr:fr + 128, :]), 16)
            nc.vector.wait_ge(self.s_q[0].h, ld)
            self.s_vc.inc(nc.vector.reduce_max(self.SCC[:, ft:ft + 1], stgv, axis=AX.X,
                                               apply_absolute_value=True))
        nc.vector.drain()
        nc.vector.tensor_scalar(self.SCC[:, 0:4], self.SCC[:, 0:4], 1.0 / 127.0, FQ_EPS,
                                op0=OP.mult, op1=OP.max)
        nc.vector.drain()
        v0 = self.s_vc.inc(nc.vector.reciprocal(self.SCC[:, 4:8], self.SCC[:, 0:4]))
        nc.tensor.wait_ge(self.s_vc.h, v0)
        t0 = self.s_tp.inc(nc.tensor.transpose(PS[0:8, 1024:1152], self.SCC[:, 0:8], self.EYE[:]))
        nc.scalar.wait_ge(self.s_tp.h, t0)
        s0 = self.s_sc.inc(nc.scalar.copy(self.SCT[:], PS[0:8, 1024:1152]))
        nc.sync.wait_ge(self.s_sc.h, s0)
        nc.sync.wait_ge(self.s_tp.h, self.s_tp.v)
        srow = self.s_gr.inc(nc.sync.dma_start(self.SM0[0:1, 1024:2048], self.SCT[0:8, :]), 16)
        nc.tensor.wait_ge(self.s_gr.h, srow)
        nc.tensor.matmul(self.bank(4), self.ONES1[0:1, 0:128],
                         self.SM0[0:1, 1024:1536], start=True, stop=True)
        ins = nc.tensor.matmul(self.bank(5), self.ONES1[0:1, 0:128],
                               self.SM0[0:1, 1536:2048], start=True, stop=True)
        t1 = self.s_tp.inc(ins)
        nc.vector.wait_ge(self.s_tp.h, t1)
        for dt in range(NDT):
            nc.sync.wait_ge(self.s_vc.h, self.s_vc.v)
            ld = self.s_q[dt % 2].inc(nc.sync.dma_start(STG[:, dt % 2, 0:512],
                                                 self.l1wt[m, dt * 128:(dt + 1) * 128, g * 512:(g + 1) * 512]), 16)
            nc.vector.wait_ge(self.s_q[dt % 2].h, ld)
            for hc in range(2):
                hs = slice(hc * 256, (hc + 1) * 256)
                nc.vector.tensor_tensor(self.T1[:, 0, hs], STG[:, dt % 2, hs], self.bank(5)[:, hs], op=OP.mult)
                nc.vector.drain()
                nc.vector.tensor_copy(self.QI[:], self.T1[:, 0, hs])
                nc.vector.drain()
                nc.vector.tensor_scalar(self.T1[:, 1, hs], self.QI[:], -127.0, 127.0, op0=OP.max, op1=OP.min)
                nc.vector.drain()
                ins = nc.vector.tensor_tensor(WQ[:, dt, hs], self.T1[:, 1, hs], self.bank(4)[:, hs], op=OP.mult)
            self.s_vc.inc(ins)
        nc.tensor.wait_ge(self.s_vc.h, self.s_vc.v)

    def emit_mhat(self, QT, KT, SQB):
        nc, PS, SM0 = self.nc, self.PS, self.SM0
        for si, SRC in ((0, QT), (1, KT)):
            for lt in range(4):
                vs = self.s_vc.inc(nc.vector.tensor_tensor(SQB[:], SRC[:, lt, :], SRC[:, lt, :], op=OP.mult))
                nc.tensor.wait_ge(self.s_vc.h, vs)
                for half in range(2):
                    lh = lt * 2 + half
                    pb = (lh % 2) * 4
                    if self.mh_tok[lh % 2] is not None:
                        nc.tensor.wait_ge(self.s_vc.h, self.mh_tok[lh % 2])
                    for qc in range(NQC):
                        ins = nc.tensor.matmul(PS[0:1, (pb + qc) * 512:(pb + qc + 1) * 512],
                                               self.ONESB[half * 64:half * 64 + 64, 0:1],
                                               SQB[half * 64:half * 64 + 64, qc * 512:(qc + 1) * 512],
                                               start=True, stop=True)
                    t = self.s_tp.inc(ins)
                    nc.vector.wait_ge(self.s_tp.h, t)
                    ins = nc.vector.reduce_max(SM0[0:1, si * 8 + lh:si * 8 + lh + 1],
                                               PS[0:1, pb * 512:(pb + 4) * 512], axis=AX.X)
                    self.mh_tok[lh % 2] = self.s_vc.inc(ins)
        nc.vector.drain()
        v1 = self.s_vc.inc(nc.vector.tensor_tensor(SM0[0:1, 16:24], SM0[0:1, 0:8], SM0[0:1, 8:16], op=OP.mult))
        nc.scalar.wait_ge(self.s_vc.h, v1)
        s1 = self.s_sc.inc(nc.scalar.activation(SM0[0:1, 24:32], SM0[0:1, 16:24], AF.Sqrt))
        nc.vector.wait_ge(self.s_sc.h, s1)
        nc.vector.tensor_scalar(SM0[0:1, 32:40], SM0[0:1, 24:32], -0.125, -1.0, op0=OP.mult, op1=OP.add)
        ev = SM0[0:1, 32:40].rearrange("p (a b) -> p a b", b=2)
        nc.vector.drain()
        v2 = self.s_vc.inc(nc.vector.tensor_tensor(SM0[0:1, 40:44], ev[:, :, 0], ev[:, :, 1], op=OP.min))
        nc.tensor.wait_ge(self.s_vc.h, v2)
        t2 = self.s_tp.inc(nc.tensor.matmul(PS[:, 1024:1028], self.ONES1[0:1, 0:128],
                                            SM0[0:1, 40:44], start=True, stop=True))
        nc.scalar.wait_ge(self.s_tp.h, t2)
        sc = self.s_sc.inc(nc.scalar.copy(self.NEGM8[:], PS[:, 1024:1028]))
        nc.tensor.wait_ge(self.s_sc.h, sc)

    def emit_attn_pair(self, layer, g, lt, qc, QT, KT, V, EB, O):
        nc, PS = self.nc, self.PS
        ha, hb = 2 * lt, 2 * lt + 1
        qsl = slice(qc * 512, (qc + 1) * 512)
        score_tok = [None] * 16
        exp_tok = [None] * 16
        av_tok = [None] * 16
        for kt in range(17):
            if kt < 16:
                r = kt % 2
                sview = PS[:, r * 1024:(r + 1) * 1024]
                if self.sb_tok[r] is not None:
                    nc.tensor.wait_ge(self.s_sc.h, self.sb_tok[r])
                ksl = slice(kt * 128, (kt + 1) * 128)
                nc.tensor.matmul(sview[:, 0:512], KT[0:64, lt, ksl], QT[0:64, lt, qsl], start=True, stop=True)
                ins = nc.tensor.matmul(sview[:, 512:1024], KT[64:128, lt, ksl], QT[64:128, lt, qsl],
                                       start=True, stop=True)
                score_tok[kt] = self.s_tp.inc(ins)
                nc.scalar.wait_ge(self.s_tp.h, score_tok[kt])
                if kt >= 2:
                    nc.scalar.wait_ge(self.s_tp.h, av_tok[kt - 2])
                ins = nc.scalar.activation(EB[:, kt % 2, :], sview, AF.Exp,
                                           bias=self.NEGM8[:, lt:lt + 1], scale=0.125)
                exp_tok[kt] = self.s_sc.inc(ins)
                self.sb_tok[r] = exp_tok[kt]
            if kt >= 1:
                j = kt - 1
                er = j % 2
                nc.tensor.wait_ge(self.s_sc.h, exp_tok[j])
                if j == 0 and self.av_free is not None:
                    nc.tensor.wait_ge(self.s_vc.h, self.av_free)
                st, sp = (j == 0), (j == 15)
                vsl = V[:, j, :]
                nc.tensor.matmul(self.bank(4)[0:64, :], vsl[:, ha * 64:ha * 64 + 64], EB[:, er, 0:512],
                                 start=st, stop=sp, tile_position=(0, 0))
                nc.tensor.matmul(self.bank(4)[64:128, :], vsl[:, hb * 64:hb * 64 + 64], EB[:, er, 512:1024],
                                 start=st, stop=sp, tile_position=(0, 64))
                nc.tensor.matmul(PS[0:1, 3072:3584], self.ONESB[:, 0:1], EB[:, er, 0:512],
                                 start=st, stop=sp, tile_position=(0, 0))
                ins = nc.tensor.matmul(PS[32:33, 3072:3584], self.ONESB[:, 0:1], EB[:, er, 512:1024],
                                       start=st, stop=sp, tile_position=(0, 32))
                av_tok[j] = self.s_tp.inc(ins)
        nc.scalar.wait_ge(self.s_tp.h, av_tok[15])
        nc.scalar.copy(self.SM0[0:1, 1024:1536], PS[0:1, 3072:3584])
        sden = self.s_sc.inc(nc.scalar.copy(self.SM0[32:33, 1536:2048], PS[32:33, 3072:3584]))
        nc.sync.wait_ge(self.s_sc.h, sden)
        self.s_gr.inc(nc.sync.dma_start(self.SCC[:, 0:4], self.SM0[0:1, 1024:1536]), 16)
        g1 = self.s_gr.inc(nc.sync.dma_start(self.SCC[:, 4:8], self.SM0[32:33, 1536:2048]), 16)
        nc.vector.wait_ge(self.s_gr.h, g1)
        vrc = self.s_vc.inc(nc.vector.reciprocal(self.SCC[:, 0:8], self.SCC[:, 0:8]))
        nc.sync.wait_ge(self.s_vc.h, vrc)
        self.s_gr.inc(nc.sync.dma_start(self.SM0[0:1, 1024:1536], self.SCC[:, 0:4]), 16)
        vr = self.s_gr.inc(nc.sync.dma_start(self.SM0[0:1, 1536:2048], self.SCC[:, 4:8]), 16)
        nc.tensor.wait_ge(self.s_gr.h, vr)
        nc.tensor.matmul(self.bank(7)[0:64, :], self.ONES1[0:1, 0:64], self.SM0[0:1, 1024:1536],
                         start=True, stop=True, tile_position=(0, 0))
        tb = self.s_tp.inc(nc.tensor.matmul(self.bank(7)[64:128, :], self.ONES1[0:1, 0:64],
                                            self.SM0[0:1, 1536:2048], start=True, stop=True,
                                            tile_position=(0, 64)))
        nc.vector.wait_ge(self.s_tp.h, tb)
        nc.vector.tensor_copy(self.DIBS[:], self.bank(7))
        nc.vector.drain()
        ins = nc.vector.tensor_tensor(O[:, g * 4 + lt, qsl], self.bank(4), self.DIBS[:], op=OP.mult)
        self.av_free = self.s_vc.inc(ins)
        self.o_tok[g * 4 + lt][qc] = self.av_free

    def p_outproj(self, layer, keep_pool=False):
        nc, PS = self.nc, self.PS
        self.proj_tok = [None] * NDT
        O = self.pool_handles[f"O{layer}"]
        if layer == 0:
            WR = self.palloc("WOR", [128, 2, 8, 128], BF16)
        else:
            self.QI = self.palloc("QIp", [128, 256], I32)
            WQ = self.palloc("WPQ", [128, 8, 512], BF16)
            STG = self.palloc("STGP", [128, 2, 512], F32)
        bv16 = self.QKVBV if layer == 0 else self.L1BV
        slot_free = [None, None]
        for g in range(NG):
            if layer == 1:
                self.quant_l1(3, g, WQ, STG)
            for lf in range(4):
                ft = g * 4 + lf
                if layer == 0:
                    sl = ft % 2
                    if slot_free[sl] is not None:
                        nc.gpsimd.wait_ge(self.s_tp.h, slot_free[sl])
                    sem = self.s_wr[sl]
                    gw = (sem, sem.inc(nc.gpsimd.dma_start(WR[:, sl, :, :], self.wot[ft]), 16))
                    wsl = lambda dt: WR[:, sl, dt, :]
                else:
                    gw = None
                    wsl = lambda dt: WQ[:, dt, lf * 128:(lf + 1) * 128]
                if gw is not None:
                    nc.tensor.wait_ge(gw[0].h, gw[1])
                for dt in range(NDT):
                    ins = nc.tensor.matmul(PS[:, 1024 + ft:1024 + ft + 1], wsl(dt), bv16[:, dt:dt + 1],
                                           start=(dt == 0), stop=(dt == NDT - 1))
                tb = self.s_tp.inc(ins)
                nc.vector.wait_ge(self.s_tp.h, tb)
                base_b = self.WOBS[:, ft:ft + 1] if layer == 0 else self.L1BS[:, 24 + ft:25 + ft]
                vb = self.s_vc.inc(nc.vector.scalar_tensor_tensor(
                    self.WEB[:, ft:ft + 1], PS[:, 1024 + ft:1024 + ft + 1], 1.0, base_b,
                    op0=OP.mult, op1=OP.add))
                for qc in range(NQC):
                    r = qc % 2
                    if self.op_bank_tok[r] is not None:
                        nc.tensor.wait_ge(self.s_vc.h, self.op_bank_tok[r])
                    for dt in range(NDT):
                        nc.tensor.wait_ge(self.s_vc.h, self.o_tok[dt][qc] or 0)
                        ins = nc.tensor.matmul(self.bank(r), wsl(dt), O[:, dt, qc * 512:(qc + 1) * 512],
                                               start=(dt == 0), stop=(dt == NDT - 1))
                    t = self.s_tp.inc(ins)
                    nc.vector.wait_ge(self.s_tp.h, t)
                    nc.vector.wait_ge(self.s_vc.h, vb)
                    nc.vector.tensor_scalar(self.T1[:, r, :], self.bank(r), self.WEB[:, ft:ft + 1],
                                            None, op0=OP.add)
                    nc.vector.drain()
                    ins = nc.vector.tensor_tensor(self.X[:, ft, qc * 512:(qc + 1) * 512],
                                                  self.X[:, ft, qc * 512:(qc + 1) * 512],
                                                  self.T1[:, r, :], op=OP.add)
                    self.op_bank_tok[r] = self.s_vc.inc(ins)
                self.proj_tok[ft] = self.op_bank_tok[(NQC - 1) % 2]
                slot_free_tok = self.s_tp.v
                if layer == 0:
                    slot_free[ft % 2] = slot_free_tok
        if not keep_pool:
            self.pfree()

    def p_router(self):
        nc, PS = self.nc, self.PS
        self.LGT = self.palloc("LGT", [128, 16, 8], F32)
        for tt in range(NTT):
            if tt >= 1:
                nc.tensor.wait_ge(self.s_sc.h, self.rt_tok[0])
            for dt in range(NDT):
                ins = nc.tensor.matmul(PS[:, 0:8], self.XN[:, dt, tt * 128:(tt + 1) * 128],
                                       self.RW[:, dt, :], start=(dt == 0), stop=(dt == NDT - 1))
            t = self.s_tp.inc(ins)
            nc.scalar.wait_ge(self.s_tp.h, t)
            self.rt_tok[0] = self.s_sc.inc(nc.scalar.copy(self.LGT[:, tt, :], PS[:, 0:8]))
        for tt in range(NTT):
            L = self.LGT[:, tt, :]
            nc.vector.wait_ge(self.s_sc.h, self.rt_tok[0])
            v = self.s_vc.inc(nc.vector.reduce_max(self.SMLL[:, 0:1], L, axis=AX.X, negate=True))
            nc.scalar.wait_ge(self.s_vc.h, v)
            s = self.s_sc.inc(nc.scalar.activation(L, L, AF.Exp, bias=self.SMLL[:, 0:1],
                                                   accum_out=self.SMLL[:, 1:2]))
            nc.vector.wait_ge(self.s_sc.h, s)
            nc.vector.reciprocal(self.SMLL[:, 1:2], self.SMLL[:, 1:2])
            nc.vector.drain()
            nc.vector.tensor_scalar(L, L, self.SMLL[:, 1:2], None, op0=OP.mult)
            nc.vector.drain()
            nc.vector.reduce_max(self.SMLL[:, 0:1], L, axis=AX.X)
            nc.vector.drain()
            nc.vector.tensor_scalar(self.T1[:, 0, 0:8], L, self.SMLL[:, 0:1], None, op0=OP.is_ge)
            nc.vector.drain()
            nc.vector.scalar_tensor_tensor(self.T1[:, 1, 0:8], self.T1[:, 0, 0:8], -BIG, L,
                                           op0=OP.mult, op1=OP.add)
            nc.vector.drain()
            nc.vector.reduce_max(self.SMLL[:, 2:3], self.T1[:, 1, 0:8], axis=AX.X)
            nc.vector.drain()
            nc.vector.tensor_scalar(self.T1[:, 0, 0:8], L, self.SMLL[:, 2:3], None, op0=OP.is_ge)
            nc.vector.drain()
            vg = self.s_vc.inc(nc.vector.tensor_tensor(L, L, self.T1[:, 0, 0:8], op=OP.mult))
            nc.tensor.wait_ge(self.s_vc.h, vg)
            tt2 = self.s_tp.inc(nc.tensor.transpose(PS[0:8, 512:640], L, self.EYE[:]))
            nc.vector.wait_ge(self.s_tp.h, tt2)
            self.s_vc.inc(nc.vector.tensor_copy(self.SM0[0:8, tt * 128:(tt + 1) * 128], PS[0:8, 512:640]))
        self.pfree(1)

    def p_moe(self):
        nc, PS, XN, X = self.nc, self.PS, self.XN, self.X
        HG = self.palloc("HG", [128, 32, 1024], BF16)
        W1R = self.palloc("W1R", [128, 3, 8, 256], BF16)
        W2R = self.palloc("W2R", [128, 4, 2, 1024], BF16)
        self.GB = self.palloc("GBm", [128, 1024], F32)
        w1_free = [None] * 3
        w2_free = [None] * 4
        bank_free = [None] * 8
        gb_tok = None
        gelu_tok = [None] * 32
        for c in range(2):
            csl = slice(c * 1024, (c + 1) * 1024)
            for e in range(MOE_E):
                if gb_tok is not None:
                    nc.sync.wait_ge(self.s_vc.h, gb_tok)
                grow = self.s_gr.inc(nc.sync.dma_start(
                    self.GB[0:1, 0:1024], self.SM0[e:e + 1, c * 1024:(c + 1) * 1024]), 16)
                for b in (4, 5):
                    if bank_free[b] is not None:
                        nc.tensor.wait_ge(self.s_vc.h, bank_free[b])
                nc.tensor.wait_ge(self.s_gr.h, grow)
                ins = None
                for half in range(2):
                    ins = nc.tensor.matmul(self.bank(4 + half), self.ONES1[0:1, 0:128],
                                           self.GB[0:1, half * 512:(half + 1) * 512],
                                           start=True, stop=True)
                tgb = self.s_tp.inc(ins)
                nc.scalar.wait_ge(self.s_tp.h, tgb)
                if gb_tok is not None:
                    nc.scalar.wait_ge(self.s_vc.h, gb_tok)
                sgb = self.s_sc.inc(nc.scalar.copy(self.GB[:], self.bank(4, 2)))
                for ht in range(32):
                    sl = (ht // 2) % 3
                    if ht % 2 == 0:
                        if w1_free[sl] is not None:
                            nc.gpsimd.wait_ge(self.s_tp.h, w1_free[sl])
                        self.w1_tok[sl] = self.s_w1[sl].inc(
                            nc.gpsimd.dma_start(W1R[:, sl, :, :], self.w1t[e, ht // 2]), 16)
                    pb = (ht % 2) * 2
                    if ht >= 2 and gelu_tok[ht - 2] is not None:
                        nc.tensor.wait_ge(self.s_sc.h, gelu_tok[ht - 2])
                    for b in (pb, pb + 1):
                        if bank_free[b] is not None:
                            nc.tensor.wait_ge(self.s_vc.h, bank_free[b])
                            bank_free[b] = None
                    nc.tensor.wait_ge(self.s_w1[sl].h, self.w1_tok[sl])
                    for nh in range(2):
                        for dt in range(NDT):
                            ins = nc.tensor.matmul(self.bank(pb + nh),
                                                   W1R[:, sl, dt, (ht % 2) * 128:(ht % 2) * 128 + 128],
                                                   XN[:, dt, c * 1024 + nh * 512: c * 1024 + (nh + 1) * 512],
                                                   start=(dt == 0), stop=(dt == NDT - 1))
                    t = self.s_tp.inc(ins)
                    if ht % 2 == 1:
                        w1_free[sl] = t
                    nc.scalar.wait_ge(self.s_tp.h, t)
                    ins = nc.scalar.activation(HG[:, ht, :], self.bank(pb, 2), AF.Gelu)
                    gelu_tok[ht] = self.s_sc.inc(ins)
                hg_done = gelu_tok[31]
                for nh in range(2):
                    for ht in range(32):
                        sl2 = (ht // 2) % 4
                        if ht % 2 == 0:
                            if w2_free[sl2] is not None:
                                nc.gpsimd.wait_ge(self.s_tp.h, w2_free[sl2])
                            hq = ht // 4
                            self.w2_tok[sl2] = self.s_w2[sl2].inc(
                                nc.gpsimd.dma_start(W2R[:, sl2, :, :],
                                                    self.w2t[e, hq, :, (ht % 4) // 2 * 2:(ht % 4) // 2 * 2 + 2, :]), 16)
                        if ht == 0:
                            nc.tensor.wait_ge(self.s_sc.h, hg_done)
                            nc.tensor.wait_ge(self.s_sc.h, sgb)
                            for b in range(8):
                                if bank_free[b] is not None:
                                    nc.tensor.wait_ge(self.s_vc.h, bank_free[b])
                                    bank_free[b] = None
                            nc.tensor.wait_ge(self.s_w2[sl2].h, self.w2_tok[sl2])
                        elif ht % 2 == 0:
                            nc.tensor.wait_ge(self.s_w2[sl2].h, self.w2_tok[sl2])
                        for dt in range(NDT):
                            ins = nc.tensor.matmul(self.bank(dt), W2R[:, sl2, ht % 2, dt * 128:(dt + 1) * 128],
                                                   HG[:, ht, nh * 512:(nh + 1) * 512],
                                                   start=(ht == 0), stop=(ht == 31))
                            if ht == 31:
                                tok = self.s_tp.inc(ins)
                                nc.vector.wait_ge(self.s_tp.h, tok)
                                nc.vector.tensor_tensor(self.T1[:, dt % 2, :], self.bank(dt),
                                                        self.GB[:, nh * 512:(nh + 1) * 512], op=OP.mult)
                                nc.vector.drain()
                                xsl = slice(c * 1024 + nh * 512, c * 1024 + (nh + 1) * 512)
                                ins2 = nc.vector.tensor_tensor(X[:, dt, xsl], X[:, dt, xsl],
                                                               self.T1[:, dt % 2, :], op=OP.add)
                                bank_free[dt] = self.s_vc.inc(ins2)
                        if ht == 31:
                            w2_free[sl2] = self.s_tp.v
                        elif ht % 2 == 1:
                            w2_free[sl2] = self.s_tp.inc(ins)
                gb_tok = bank_free[7]
        self.pfree()

    def p_ffn1(self):
        nc, PS, XN, X = self.nc, self.PS, self.XN, self.X
        WQ1 = self.palloc("WQ1", [128, 8, 4096], FP8)
        WQ2 = self.palloc("WQ2", [128, 32, 1024], FP8)
        self.RBS = self.palloc("RBSf", [128, 256], F32)
        self.MBS = self.palloc("MBSf", [128, 256], F32)
        LNC = self.palloc("LNC", [128, 32, 2], BF16)
        G = self.palloc("G", [128, 32, 256], BF16)
        SQGF = self.palloc("SQGF", [128, 4, 256], BF16)
        QSC = self.palloc("QSC", [128, 2, 512], F32)
        STGF = self.palloc("STGF", [128, 2, 1024], F32)
        nc.vector.tensor_copy(LNC[:, :, 0], self.LNBB16[:])
        nc.vector.tensor_copy(LNC[:, :, 1], self.LNWB16[:])
        slot_tok = [None, None]
        self.wq1_co = {}
        for wi, (src, ntile, width, WQ) in ((1, (self.wprt, 32, 1024, WQ2)),
                                            (0, (self.wfct, 8, 4096, WQ1))):
            nc.vector.memset(self.SMLL[:, 3:4], 0.0)
            wu = min(1024, width)
            units = [(it, co) for co in range(0, width, wu) for it in range(ntile)]
            nhalf = len(units)
            for ih, (it, co) in enumerate(units):
                sl = ih % 2
                if slot_tok[sl] is not None:
                    nc.sync.wait_ge(self.s_vc.h, slot_tok[sl])
                ld = self.s_q[sl].inc(nc.sync.dma_start(
                    STGF[:, sl, 0:wu], src[it * 128:it * 128 + 128, co:co + wu]), 16)
                nc.vector.wait_ge(self.s_q[sl].h, ld)
                nc.vector.reduce_sum(self.SMLL[:, 2:3], STGF[:, sl, 0:wu], axis=AX.X,
                                     apply_absolute_value=True)
                nc.vector.drain()
                v = nc.vector.tensor_tensor(self.SMLL[:, 3:4], self.SMLL[:, 3:4],
                                            self.SMLL[:, 2:3], op=OP.add)
                slot_tok[sl] = self.s_vc.inc(v)
                nc.vector.drain()
            vsum = slot_tok[(nhalf - 1) % 2]
            nc.tensor.wait_ge(self.s_vc.h, vsum)
            tg = self.s_tp.inc(nc.tensor.matmul(PS[0:1, 1024:1025], self.ONE2F[:, 0:1],
                                                self.SMLL[:, 3:4], start=True, stop=True))
            nc.vector.wait_ge(self.s_tp.h, tg)
            nc.vector.tensor_scalar(self.SM0[0:1, 48:49], PS[0:1, 1024:1025], 1.0 / (4096.0 * 1024.0),
                                    AM_EPS, op0=OP.mult, op1=OP.add)
            nc.vector.drain()
            vg = self.s_vc.inc(nc.vector.reciprocal(self.SM0[0:1, 48:49], self.SM0[0:1, 48:49]))
            nc.tensor.wait_ge(self.s_vc.h, vg)
            tb = self.s_tp.inc(nc.tensor.matmul(PS[:, 1024:1025], self.ONES1[0:1, 0:128],
                                                self.SM0[0:1, 48:49], start=True, stop=True))
            nc.scalar.wait_ge(self.s_tp.h, tb)
            sg = self.s_sc.inc(nc.scalar.copy(self.INVGC[:, wi:wi + 1], PS[:, 1024:1025]))
            nc.vector.wait_ge(self.s_sc.h, sg)
            for ih, (it, co) in enumerate(units):
                sl = ih % 2
                if slot_tok[sl] is not None:
                    nc.sync.wait_ge(self.s_vc.h, slot_tok[sl])
                ld = self.s_q[sl].inc(nc.sync.dma_start(
                    STGF[:, sl, 0:wu], src[it * 128:it * 128 + 128, co:co + wu]), 16)
                nc.vector.wait_ge(self.s_q[sl].h, ld)
                for ch in range(wu // 512):
                    cs = slice(ch * 512, (ch + 1) * 512)
                    nc.vector.drain()
                    nc.vector.tensor_scalar(QSC[:, 0, :], STGF[:, sl, cs], self.INVGC[:, wi:wi + 1], 0.5,
                                            op0=OP.mult, op1=OP.is_gt)
                    nc.vector.tensor_scalar(QSC[:, 1, :], STGF[:, sl, cs], self.INVGC[:, wi:wi + 1], -0.5,
                                            op0=OP.mult, op1=OP.is_lt)
                    nc.vector.drain()
                    col = co + ch * 512
                    v = nc.vector.tensor_tensor(WQ[:, it, col:col + 512], QSC[:, 0, :], QSC[:, 1, :],
                                                op=OP.subtract)
                    slot_tok[sl] = self.s_vc.inc(v)
                if ih == nhalf - 1 or (wi == 0 and ih == nhalf // 2 - 1):
                    self.wq1_co[(wi, ih)] = self.s_vc.v
            if wi == 1:
                wq2_done = self.s_vc.v
        vq_done = self.s_vc.v
        wq1_half = self.wq1_co.get((0, len(units) // 2 - 1), vq_done)
        nc.tensor.wait_ge(self.s_vc.h, wq2_done)
        for dt in range(NDT):
            for ht in range(32):
                ins = nc.tensor.matmul(PS[:, 1536 + 2 * dt:1538 + 2 * dt],
                                       WQ2[:, ht, dt * 128:(dt + 1) * 128], LNC[:, ht, :],
                                       start=(ht == 0), stop=(ht == 31))
        tpb = self.s_tp.inc(ins)
        nc.vector.wait_ge(self.s_tp.h, tpb)
        vpb = self.s_vc.inc(nc.vector.tensor_copy(self.PBIA[:], PS[:, 1536:1552]))
        self.pfree(2)
        W = 256
        gelu_tok = [None] * 32
        sq_tok = [None] * 32
        fold_tok = [None] * 32
        bank_free = [None] * 8
        for c in range(T // W):
            csl = slice(c * W, (c + 1) * W)
            stat_free = [None] * 4
            if c == 0:
                nc.tensor.wait_ge(self.s_vc.h, wq1_half)
            def emit_stats(ht):
                nc.tensor.wait_ge(self.s_vc.h, sq_tok[ht])
                if ht == 0:
                    for b in (4, 5):
                        if bank_free[b] is not None:
                            nc.tensor.wait_ge(self.s_vc.h, bank_free[b])
                            bank_free[b] = None
                nc.tensor.matmul(PS[0:1, 2048:2048 + W], self.ONESB[:, 0:1], G[:, ht, :],
                                 start=(ht == 0), stop=(ht == 31))
                ins = nc.tensor.matmul(PS[0:1, 2560:2560 + W], self.ONESB[:, 0:1], SQGF[:, ht % 4, :],
                                       start=(ht == 0), stop=(ht == 31))
                st2 = self.s_tp.inc(ins)
                stat_free[ht % 4] = st2
                nc.vector.wait_ge(self.s_tp.h, st2)
                fold_tok[ht] = self.s_vc.inc(nc.vector.tensor_scalar(
                    G[:, ht, :], G[:, ht, :], self.LNW[:, ht:ht + 1], None, op0=OP.mult))
            for ht in range(32):
                pb = ht % 2
                if ht >= 2 and gelu_tok[ht - 2] is not None:
                    nc.tensor.wait_ge(self.s_sc.h, gelu_tok[ht - 2])
                if bank_free[pb] is not None:
                    nc.tensor.wait_ge(self.s_vc.h, bank_free[pb])
                    bank_free[pb] = None
                if c == 0 and ht == 16:
                    nc.tensor.wait_ge(self.s_vc.h, vq_done)
                for dt in range(NDT):
                    ins = nc.tensor.matmul(self.bank(pb)[:, 0:W], WQ1[:, dt, ht * 128:(ht + 1) * 128],
                                           XN[:, dt, csl], start=(dt == 0), stop=(dt == NDT - 1))
                t = self.s_tp.inc(ins)
                nc.scalar.wait_ge(self.s_tp.h, t)
                gelu_tok[ht] = self.s_sc.inc(nc.scalar.activation(G[:, ht, :], self.bank(pb)[:, 0:W], AF.Gelu))
                nc.vector.wait_ge(self.s_sc.h, gelu_tok[ht])
                if stat_free[ht % 4] is not None:
                    nc.vector.wait_ge(self.s_tp.h, stat_free[ht % 4])
                sq_tok[ht] = self.s_vc.inc(nc.vector.tensor_tensor(
                    SQGF[:, ht % 4, :], G[:, ht, :], G[:, ht, :], op=OP.mult))
                if ht >= 2:
                    emit_stats(ht - 2)
            emit_stats(30)
            emit_stats(31)
            nc.vector.tensor_scalar(self.SM0[0:1, 0:W], PS[0:1, 2048:2048 + W], 1.0 / 4096.0, None, op0=OP.mult)
            nc.vector.tensor_scalar(self.SM0[0:1, 512:512 + W], PS[0:1, 2560:2560 + W], 1.0 / 4096.0,
                                    None, op0=OP.mult)
            nc.vector.drain()
            nc.vector.tensor_tensor(PS[0:1, 2048:2048 + W], self.SM0[0:1, 0:W], self.SM0[0:1, 0:W], op=OP.mult)
            nc.vector.drain()
            vv = self.s_vc.inc(nc.vector.tensor_tensor(self.SM0[0:1, 512:512 + W], self.SM0[0:1, 512:512 + W],
                                                       PS[0:1, 2048:2048 + W], op=OP.subtract))
            nc.scalar.wait_ge(self.s_vc.h, vv)
            ss = self.s_sc.inc(nc.scalar.activation(self.SM0[0:1, 512:512 + W], self.SM0[0:1, 512:512 + W],
                                                    AF.Sqrt, bias=self.EPSC[0:1, 0:1]))
            nc.vector.wait_ge(self.s_sc.h, ss)
            nc.vector.reciprocal(self.SM0[0:1, 512:512 + W], self.SM0[0:1, 512:512 + W])
            nc.vector.drain()
            vr = self.s_vc.inc(nc.vector.scalar_tensor_tensor(self.SM0[0:1, 1024:1024 + W], self.SM0[0:1, 0:W],
                                                              -1.0, self.SM0[0:1, 512:512 + W],
                                                              op0=OP.mult, op1=OP.mult))
            nc.tensor.wait_ge(self.s_vc.h, vr)
            for b in (6, 7):
                if bank_free[b] is not None:
                    nc.tensor.wait_ge(self.s_vc.h, bank_free[b])
                    bank_free[b] = None
            nc.tensor.matmul(self.bank(6)[:, 0:W], self.ONES1[0:1, 0:128], self.SM0[0:1, 512:512 + W],
                             start=True, stop=True)
            tb2 = self.s_tp.inc(nc.tensor.matmul(self.bank(7)[:, 0:W], self.ONES1[0:1, 0:128],
                                                 self.SM0[0:1, 1024:1024 + W], start=True, stop=True))
            nc.scalar.wait_ge(self.s_tp.h, tb2)
            nc.scalar.copy(self.RBS[:, 0:W], self.bank(6)[:, 0:W])
            srb = self.s_sc.inc(nc.scalar.copy(self.MBS[:, 0:W], self.bank(7)[:, 0:W]))
            for ht in range(32):
                if ht == 0:
                    nc.tensor.wait_ge(self.s_vc.h, fold_tok[31])
                    nc.tensor.wait_ge(self.s_vc.h, vpb)
                    nc.tensor.wait_ge(self.s_sc.h, srb)
                    for b in range(8):
                        if bank_free[b] is not None:
                            nc.tensor.wait_ge(self.s_vc.h, bank_free[b])
                            bank_free[b] = None
                for dt in range(NDT):
                    ins = nc.tensor.matmul(self.bank(dt)[:, 0:W], WQ2[:, ht, dt * 128:(dt + 1) * 128],
                                           G[:, ht, :], start=(ht == 0), stop=(ht == 31))
                    if ht == 31:
                        tok = self.s_tp.inc(ins)
                        nc.vector.wait_ge(self.s_tp.h, tok)
                        nc.vector.tensor_tensor(self.T1[:, dt % 2, 0:W], self.bank(dt)[:, 0:W],
                                                self.RBS[:, 0:W], op=OP.mult)
                        nc.vector.drain()
                        nc.vector.scalar_tensor_tensor(self.T1[:, dt % 2, 256:256 + W], self.MBS[:, 0:W],
                                                       self.PBIA[:, 2 * dt + 1:2 * dt + 2], self.T1[:, dt % 2, 0:W],
                                                       op0=OP.mult, op1=OP.add)
                        nc.vector.drain()
                        nc.vector.tensor_scalar(self.T1[:, dt % 2, 0:W], self.T1[:, dt % 2, 256:256 + W],
                                                self.PBIA[:, 2 * dt:2 * dt + 1], None, op0=OP.add)
                        nc.vector.drain()
                        ins2 = nc.vector.tensor_tensor(X[:, dt, csl], X[:, dt, csl],
                                                       self.T1[:, dt % 2, 0:W], op=OP.add)
                        bank_free[dt] = self.s_vc.inc(ins2)
            self.mm2_tok = self.s_tp.v
        self.pfree()

    def p_store(self):
        nc = self.nc
        for se in ([self.s_tp, self.s_sc, self.s_vc, self.s_gw, self.s_ld, self.s_gr]
                   + self.s_wr + self.s_w1 + self.s_w2 + self.s_q):
            nc.sync.wait_ge(se.h, se.v)
            nc.gpsimd.wait_ge(se.h, se.v)
        if self.dump is None:
            for dt in range(NDT):
                self.s_st.inc(nc.sync.dma_start(self.outT[dt * 128:(dt + 1) * 128, :], self.X[:, dt, :]), 16)
            nc.sync.wait_ge(self.s_st.h, self.s_st.v)
        elif self.dump == "QK":
            for i, src in enumerate((self.pool_handles[f"QT{self.dump_layer}"], self.pool_handles[f"KT{self.dump_layer}"])):
                for lt in range(4):
                    self.s_gw.inc(nc.gpsimd.dma_start(
                        self.outT[(i * 4 + lt) * 128:(i * 4 + lt + 1) * 128, :], src[:, lt, :]), 16)
            nc.gpsimd.wait_ge(self.s_gw.h, self.s_gw.v)
            nc.sync.wait_ge(self.s_gw.h, self.s_gw.v)
        elif self.dump == "V":
            V = self.pool_handles[f"V{self.dump_layer}"]
            for i in range(4):
                for j in range(4):
                    self.s_gw.inc(nc.gpsimd.dma_start(
                        self.outT[i * 128:(i + 1) * 128, j * 512:(j + 1) * 512], V[:, i * 4 + j, :]), 16)
            nc.gpsimd.wait_ge(self.s_gw.h, self.s_gw.v)
            nc.sync.wait_ge(self.s_gw.h, self.s_gw.v)
        else:
            src = {"XN": self.XN}.get(self.dump) if self.dump in ("XN",) else self.pool_handles[self.dump]
            for dt in range(NDT):
                self.s_gw.inc(nc.gpsimd.dma_start(self.outT[dt * 128:(dt + 1) * 128, :], src[:, dt, :]), 16)
            nc.gpsimd.wait_ge(self.s_gw.h, self.s_gw.v)
            nc.sync.wait_ge(self.s_gw.h, self.s_gw.v)


def build_program(phases=None, dump=None):
    mk = MK(phases, dump)
    mk.qkv_bank_tok = [None] * 4
    mk.mh_tok = [None, None]
    mk.sb_tok = [None, None]
    mk.av_free = None
    mk.op_bank_tok = [None, None]
    mk.rt_tok = [None]
    mk.w1_tok = [None] * 3
    mk.w2_tok = [None] * 4
    mk.pool_handles = {}
    mk.mm2_tok = None
    _orig = mk.palloc

    def palloc2(name, shape, dtype):
        h = _orig(name, shape, dtype)
        mk.pool_handles[name] = h
        return h
    mk.palloc = palloc2
    nc = mk.build()
    return nc


def host_inputs(b_x, ins):
    d = {}
    d["xT"] = np.ascontiguousarray(b_x.T)
    qkv = ins["L0_attn_qkv_w"]
    d["qkvt"] = np.ascontiguousarray(qkv.reshape(24, 128, 8, 128).transpose(0, 3, 2, 1))
    d["qkvb"] = np.ascontiguousarray(ins["L0_attn_qkv_b"].reshape(24, 128).T)
    wo = ins["L0_attn_out_w"]
    d["wot"] = np.ascontiguousarray(wo.reshape(8, 128, 8, 128).transpose(0, 3, 2, 1))
    d["wob"] = np.ascontiguousarray(ins["L0_attn_out_b"].reshape(8, 128).T)
    d["rwt"] = np.ascontiguousarray(ins["L0_ffn_router_w"].T.reshape(8, 128, 8).transpose(1, 0, 2))
    w1 = ins["L0_ffn_w1"]
    d["w1t"] = np.ascontiguousarray(w1.transpose(0, 2, 1).reshape(8, 8, 128, 16, 256).transpose(0, 3, 2, 1, 4))
    w2 = ins["L0_ffn_w2"]
    d["w2t"] = np.ascontiguousarray(w2.transpose(0, 2, 1).reshape(8, 8, 4, 128, 1024).transpose(0, 1, 3, 2, 4))
    l1 = [ins["L1_attn_wq"], ins["L1_attn_wk"], ins["L1_attn_wv"], ins["L1_attn_wp"]]
    d["l1w"] = np.ascontiguousarray(np.stack(l1))
    d["l1wt"] = np.ascontiguousarray(np.stack([w.T for w in l1]))
    bb = [ins["L1_attn_bq"], ins["L1_attn_bk"], ins["L1_attn_bv"], ins["L1_attn_bp"]]
    d["l1b"] = np.ascontiguousarray(np.concatenate([b.reshape(8, 128).T for b in bb], axis=1))
    d["wfct"] = np.ascontiguousarray(ins["L1_ffn_W_fc"].T)
    d["lnwb"] = np.ascontiguousarray(np.concatenate(
        [ins["L1_ffn_ln_w"].reshape(32, 128).T, ins["L1_ffn_ln_b"].reshape(32, 128).T], axis=1))
    d["wprt"] = np.ascontiguousarray(ins["L1_ffn_W_proj"].T)
    d["eye"] = np.eye(128, dtype=np.float32)
    return {k: v.astype(np.float32, copy=False) for k, v in d.items()}


def kernel(**inputs):
    x = inputs["L0_attn_x"]
    nc = build_program()
    in_maps = [host_inputs(x[b], inputs) for b in range(8)]
    res = run_bass_kernel_spmd(nc, in_maps, core_ids=list(range(8)))
    out = np.stack([np.ascontiguousarray(r["outT"].T) for r in res.results])
    return out.astype(np.float32)


# version 14
# speedup vs baseline: 1.0006x; 1.0006x over previous
import numpy as np
from contextlib import ExitStack

import concourse.bass as bass
import concourse.mybir as mybir
from concourse.bass_utils import run_bass_kernel_spmd

F32 = mybir.dt.float32
BF16 = mybir.dt.bfloat16
FP8 = mybir.dt.float8e4
I32 = mybir.dt.int32
AF = mybir.ActivationFunctionType
OP = mybir.AluOpType
AX = mybir.AxisListType

D = 1024
T = 2048
NDT = 8
NTT = 16
NQC = 4
H = 16
DH = 64
NG = 2
MOE_E = 8
LN_EPS = 1e-5
FQ_EPS = float(np.finfo(np.float32).eps)
AM_EPS = 1e-6
BIG = 1e30

ALL_PHASES = ["ln1", "attn0", "proj0", "ln2", "router", "moe",
              "ln3", "attn1", "proj1", "ln4", "ffn1"]


class Sem:
    def __init__(self, nc, stk, name):
        self.h = stk.enter_context(nc.semaphore(name))
        self.v = 0

    def inc(self, instr, n=1):
        instr.then_inc(self.h, n)
        self.v += n
        return self.v


class MK:
    def __init__(self, phases=None, dump=None):
        self.dump = dump
        self.dump_layer = 0
        self.phases = ALL_PHASES if phases is None else phases
        self.nc = bass.Bass("TRN2", target_bir_lowering=False)
        self.stk = ExitStack()
        self.pool = []

    def sb(self, name, shape, dtype):
        return self.stk.enter_context(self.nc.sbuf_tensor(name, shape, dtype))

    def palloc(self, name, shape, dtype):
        cm = self.nc.sbuf_tensor(name, shape, dtype)
        h = cm.__enter__()
        self.pool.append(cm)
        return h

    def pfree(self, n=None):
        k = len(self.pool) if n is None else n
        for _ in range(k):
            self.pool.pop().__exit__(None, None, None)

    def barrier(self):
        nc, s = self.nc, self.s_bar
        for eng in (nc.tensor, nc.vector, nc.scalar, nc.sync):
            eng.drain()
            eng.sem_inc(s.h, 1)
            s.v += 1
        nc.gpsimd.sem_inc(s.h, 1)
        s.v += 1
        for eng in (nc.tensor, nc.vector, nc.scalar, nc.gpsimd, nc.sync):
            eng.wait_ge(s.h, s.v)

    def bank(self, i, n=1):
        return self.PS[:, 512 * i:512 * (i + n)]

    def build(self):
        nc = self.nc
        s = self.stk
        di = lambda name, shape: nc.dram_tensor(name, shape, F32, kind="ExternalInput").ap()
        self.xT = di("xT", [D, T])
        self.qkvt = di("qkvt", [24, 128, 8, 128])
        self.qkvb = di("qkvb", [128, 24])
        self.wot = di("wot", [8, 128, 8, 128])
        self.wob = di("wob", [128, 8])
        self.rwt = di("rwt", [128, 8, 8])
        self.w1t = di("w1t", [MOE_E, 16, 128, 8, 256])
        self.w2t = di("w2t", [MOE_E, 8, 128, 4, 1024])
        self.l1w = di("l1w", [4, D, D])
        self.l1wt = di("l1wt", [4, D, D])
        self.l1b = di("l1b", [128, 32])
        self.wfct = di("wfct", [D, 4096])
        self.lnwb = di("lnwb", [128, 64])
        self.wprt = di("wprt", [4096, D])
        self.eye_in = di("eye", [128, 128])
        self.outT = nc.dram_tensor("outT", [D, T], F32, kind="ExternalOutput").ap()

        self.X = self.sb("X", [128, NDT, T], F32)
        self.XN = self.sb("XN", [128, NDT, T], BF16)
        self.SM0 = self.sb("SM0", [128, 2048], F32)
        self.ONES1 = self.sb("ONES1", [128, 128], F32)
        self.ONESB = self.sb("ONESB", [128, 2], BF16)
        self.ONE2F = self.sb("ONE2F", [128, 2], F32)
        self.EYE = self.sb("EYE", [128, 128], F32)
        self.QKVB = self.sb("QKVB", [128, 24], F32)
        self.QKVBV = self.sb("QKVBV", [128, 8], BF16)
        self.WOBS = self.sb("WOBS", [128, 8], F32)
        self.L1BS = self.sb("L1BS", [128, 32], F32)
        self.L1BV = self.sb("L1BV", [128, 8], BF16)
        self.LNW = self.sb("LNW", [128, 32], F32)
        self.LNWB16 = self.sb("LNWB16", [128, 32], BF16)
        self.LNBB16 = self.sb("LNBB16", [128, 32], BF16)
        self.NEGM8 = self.sb("NEGM8", [128, 4], F32)
        self.RW = self.sb("RW", [128, 8, 8], BF16)
        self.SCT = self.sb("SCT", [8, 128], F32)
        self.SCC = self.sb("SCC", [128, 8], F32)
        self.SMLL = self.sb("SMLL", [128, 4], F32)
        self.WEB = self.sb("WEB", [128, 8], F32)
        self.INVGC = self.sb("INVGC", [128, 2], F32)
        self.PBIA = self.sb("PBIA", [128, 16], F32)
        self.T1 = self.sb("T1", [128, 2, 512], F32)
        self.EPSC = self.sb("EPSC", [128, 1], F32)
        self.SQG = None
        self.PS = s.enter_context(nc.psum_tensor("PS", [128, 4096], F32))

        self.s_ld = Sem(nc, s, "s_ld")
        self.s_gw = Sem(nc, s, "s_gw")
        self.s_st = Sem(nc, s, "s_st")
        self.s_tp = Sem(nc, s, "s_tp")
        self.s_sc = Sem(nc, s, "s_sc")
        self.s_vc = Sem(nc, s, "s_vc")
        self.s_bar = Sem(nc, s, "s_bar")
        self.s_wr = [Sem(nc, s, f"s_wr{i}") for i in range(2)]
        self.s_w1 = [Sem(nc, s, f"s_w1{i}") for i in range(3)]
        self.s_w2 = [Sem(nc, s, f"s_w2{i}") for i in range(4)]
        self.s_q = [Sem(nc, s, f"s_q{i}") for i in range(2)]
        self.s_gr = Sem(nc, s, "s_gr")

        ph = self.phases
        self.p_load()
        self.barrier()
        steps = [("ln1", self.p_ln, ()), ("attn0", self.p_attn, (0,)),
                 ("proj0", self.p_outproj, (0,)), ("ln2", self.p_ln, ()),
                 ("router", self.p_router, ()), ("moe", self.p_moe, ()),
                 ("ln3", self.p_ln, ()), ("attn1", self.p_attn, (1,)),
                 ("proj1", self.p_outproj, (1,)), ("ln4", self.p_ln, ()),
                 ("ffn1", self.p_ffn1, ())]
        for name, fn, args in steps:
            if name in ph:
                fn(*args)
                self.barrier()
        self.p_store()
        self.pfree()
        self.stk.close()
        return self.nc

    def p_load(self):
        nc = self.nc
        for dt in range(NDT):
            self.s_ld.inc(nc.sync.dma_start(self.X[:, dt, :], self.xT[dt * 128:(dt + 1) * 128, :]), 16)
        for dst, src in ((self.QKVB, self.qkvb[:]), (self.WOBS, self.wob[:]),
                         (self.L1BS, self.l1b[:]), (self.LNW, self.lnwb[:, 0:32]),
                         (self.EYE, self.eye_in[:])):
            self.s_ld.inc(nc.sync.dma_start(dst[:], src), 16)
        self.s_gw.inc(nc.gpsimd.dma_start(self.RW[:], self.rwt[:]), 16)
        self.s_gw.inc(nc.gpsimd.dma_start(self.LNWB16[:], self.lnwb[:, 0:32]), 16)
        self.s_gw.inc(nc.gpsimd.dma_start(self.LNBB16[:], self.lnwb[:, 32:64]), 16)
        nc.vector.wait_ge(self.s_ld.h, self.s_ld.v)
        nc.vector.wait_ge(self.s_gw.h, self.s_gw.v)
        nc.vector.memset(self.ONES1[:], 1.0)
        nc.vector.memset(self.EPSC[:], LN_EPS)
        nc.vector.memset(self.ONESB[:], 1.0)
        nc.vector.memset(self.ONE2F[:], 1.0)
        nc.vector.tensor_copy(self.QKVBV[:], self.QKVB[:, 16:24])
        nc.vector.tensor_copy(self.L1BV[:], self.L1BS[:, 16:24])

    def p_ln(self):
        self._ln_n = getattr(self, "_ln_n", 0) + 1
        """XN = (X - mu) * rsqrt(var+eps) in bf16, two 1024-token half-passes.
        PSUM per half: sum(x) banks0-1, sum(x^2) banks4-5; then MUB banks0-1,
        RB banks2-3 broadcasts."""
        nc, X, XN, SM0, PS = self.nc, self.X, self.XN, self.SM0, self.PS
        XSQ = self.palloc(f"XSQ{self._ln_n}", [128, 2, 512], F32)
        LNTS = self.palloc(f"LNTS{self._ln_n}", [128, 2, 512], F32)
        for half in range(2):
            base = half * 1024
            t_sq0 = self.s_tp.v
            sq_tok = []
            for q2 in range(2):
                for dt in range(NDT):
                    i = q2 * NDT + dt
                    if i >= 2:
                        nc.scalar.wait_ge(self.s_tp.h, t_sq0 + (i - 1))
                    ins = nc.scalar.activation(XSQ[:, i % 2, :],
                                               X[:, dt, base + q2 * 512: base + (q2 + 1) * 512], AF.Square)
                    sq_tok.append(self.s_sc.inc(ins))
            for q2 in range(2):
                for dt in range(NDT):
                    i = q2 * NDT + dt
                    nc.tensor.matmul(PS[0:1, q2 * 512:(q2 + 1) * 512], self.ONE2F[:, 0:1],
                                     X[:, dt, base + q2 * 512: base + (q2 + 1) * 512],
                                     start=(dt == 0), stop=(dt == NDT - 1))
                    nc.tensor.wait_ge(self.s_sc.h, sq_tok[i])
                    ins = nc.tensor.matmul(PS[0:1, 2048 + q2 * 512:2048 + (q2 + 1) * 512], self.ONE2F[:, 0:1],
                                           XSQ[:, i % 2, :], start=(dt == 0), stop=(dt == NDT - 1))
                    self.s_tp.inc(ins)
            t_stats = self.s_tp.v
            nc.vector.wait_ge(self.s_tp.h, t_stats)
            nc.vector.tensor_scalar(SM0[0:1, 0:1024], PS[0:1, 0:1024], 1.0 / D, None, op0=OP.mult)
            nc.vector.tensor_scalar(SM0[0:1, 1024:2048], PS[0:1, 2048:3072], 1.0 / D, None, op0=OP.mult)
            nc.vector.drain()
            nc.vector.tensor_tensor(PS[0:1, 0:1024], SM0[0:1, 0:1024], SM0[0:1, 0:1024], op=OP.mult)
            nc.vector.drain()
            v_var = self.s_vc.inc(nc.vector.tensor_tensor(
                SM0[0:1, 1024:2048], SM0[0:1, 1024:2048], PS[0:1, 0:1024], op=OP.subtract))
            nc.scalar.wait_ge(self.s_vc.h, v_var)
            s_sqrt = self.s_sc.inc(nc.scalar.activation(SM0[0:1, 1024:2048], SM0[0:1, 1024:2048],
                                                        AF.Sqrt, bias=self.EPSC[0:1, 0:1]))
            nc.vector.wait_ge(self.s_sc.h, s_sqrt)
            v_rstd = self.s_vc.inc(nc.vector.reciprocal(SM0[0:1, 1024:2048], SM0[0:1, 1024:2048]))
            nc.tensor.wait_ge(self.s_vc.h, v_rstd)
            t_bc = None
            for off in (0, 512):
                nc.tensor.matmul(PS[:, off:off + 512], self.ONES1[0:1, 0:128],
                                 SM0[0:1, off:off + 512], start=True, stop=True)
                ins = nc.tensor.matmul(PS[:, 1024 + off:1536 + off], self.ONES1[0:1, 0:128],
                                       SM0[0:1, 1024 + off:1536 + off], start=True, stop=True)
                t_bc = self.s_tp.inc(ins)
            nc.vector.wait_ge(self.s_tp.h, t_bc)
            lastv = None
            for dt in range(NDT):
                for q2 in range(2):
                    sl = slice(base + q2 * 512, base + (q2 + 1) * 512)
                    nc.vector.tensor_tensor(LNTS[:, q2, :], X[:, dt, sl],
                                            PS[:, q2 * 512:(q2 + 1) * 512], op=OP.subtract)
                    nc.vector.drain()
                    ins = nc.vector.tensor_tensor(XN[:, dt, sl], LNTS[:, q2, :],
                                                  PS[:, 1024 + q2 * 512:1536 + q2 * 512], op=OP.mult)
                    lastv = self.s_vc.inc(ins)
            nc.tensor.wait_ge(self.s_vc.h, lastv)
        self.pfree(2)

    def p_attn(self, layer):
        nc, PS, XN = self.nc, self.PS, self.XN
        O = self.palloc(f"O{layer}", [128, NDT, T], BF16)
        QT = self.palloc(f"QT{layer}", [128, 4, T], BF16)
        KT = self.palloc(f"KT{layer}", [128, 4, T], BF16)
        V = self.palloc(f"V{layer}", [128, NTT, 512], BF16)
        EB = self.palloc(f"EB{layer}", [128, 2, 1024], BF16)
        SQB = EB[:, :, :].rearrange("p a b -> p (a b)")
        self.DIBS = self.T1[:, 1, :]
        if layer == 0:
            WR = self.palloc("WR", [128, 2, 8, 128], BF16)
        else:
            self.QI = self.palloc("QIa", [128, 256], I32)
            WQ = self.palloc("WQ", [128, 8, 512], BF16)
            STG = self.palloc("STG", [128, 2, 512], F32)
        self.o_tok = [[None] * NQC for _ in range(NDT)]

        slot_free = [None, None]
        for g in range(NG):
            nc.tensor.wait_ge(self.s_sc.h, self.s_sc.v)
            nc.tensor.wait_ge(self.s_vc.h, self.s_vc.v)
            if layer == 0:
                chunks = []
                for m in range(3):
                    for ftl in range(4):
                        chunks.append((m, ftl, m * 8 + g * 4 + ftl))
                for ci, (m, ftl, fc) in enumerate(chunks):
                    sl = ci % 2
                    if slot_free[sl] is not None:
                        nc.gpsimd.wait_ge(self.s_tp.h, slot_free[sl])
                    sem = self.s_wr[sl]
                    gw = (sem, sem.inc(nc.gpsimd.dma_start(WR[:, sl, :, :], self.qkvt[fc]), 16))
                    self.emit_qkv_chunk(layer, g, m, ftl, WR[:, sl, :, :], gw, QT, KT, V)
                    slot_free[sl] = self.s_tp.v
            else:
                for m in range(3):
                    self.quant_l1(m, g, WQ, STG)
                    self.emit_qkv_chunk(layer, g, m, None, WQ, None, QT, KT, V)
            self.emit_mhat(QT, KT, SQB)
            for lt in range(4):
                for qc in range(NQC):
                    self.emit_attn_pair(layer, g, lt, qc, QT, KT, V, EB, O)
        if self.dump in ("QK", "V"):
            pass
        else:
            self.pfree(len(self.pool) - 1)

    def emit_qkv_chunk(self, layer, g, m, ftl, W, gw_tok, QT, KT, V):
        nc, PS, XN = self.nc, self.PS, self.XN
        ftls = [ftl] if layer == 0 else list(range(4))
        last_tok = None
        for lf in ftls:
            wsl = (lambda dt: W[:, dt, lf * 128:(lf + 1) * 128]) if layer == 1 else \
                  (lambda dt: W[:, dt, :])
            fglob = (m * 8 if layer == 0 else m * 8) + g * 4 + lf
            if m < 2:
                dst = QT if m == 0 else KT
                bias = self.QKVB[:, fglob:fglob + 1] if layer == 0 else \
                    self.L1BS[:, m * 8 + g * 4 + lf: m * 8 + g * 4 + lf + 1]
                for qc in range(NQC):
                    bk = self.bank(qc % 2)
                    if self.qkv_bank_tok[qc % 2] is not None:
                        nc.tensor.wait_ge(self.s_vc.h, self.qkv_bank_tok[qc % 2])
                    if gw_tok is not None and qc == 0:
                        nc.tensor.wait_ge(gw_tok[0].h, gw_tok[1])
                    for dt in range(NDT):
                        ins = nc.tensor.matmul(bk, wsl(dt), XN[:, dt, qc * 512:(qc + 1) * 512],
                                               start=(dt == 0), stop=(dt == NDT - 1))
                    t = self.s_tp.inc(ins)
                    nc.vector.wait_ge(self.s_tp.h, t)
                    ins = nc.vector.tensor_scalar(dst[:, lf, qc * 512:(qc + 1) * 512], bk,
                                                  bias, None, op0=OP.add)
                    self.qkv_bank_tok[qc % 2] = self.s_vc.inc(ins)
                    last_tok = self.qkv_bank_tok[qc % 2]
            else:
                width = 128 if layer == 0 else 512
                fcol = lf * 128 if layer == 0 else 0
                mov = (lambda dt: W[:, dt, :]) if layer == 0 else (lambda dt: W[:, dt, 0:512])
                for tt in range(NTT):
                    bk = self.bank(2 + tt % 2)[:, 0:width]
                    if self.qkv_bank_tok[2 + tt % 2] is not None:
                        nc.tensor.wait_ge(self.s_vc.h, self.qkv_bank_tok[2 + tt % 2])
                    if gw_tok is not None and tt == 0:
                        nc.tensor.wait_ge(gw_tok[0].h, gw_tok[1])
                    for dt in range(NDT):
                        ins = nc.tensor.matmul(bk, XN[:, dt, tt * 128:(tt + 1) * 128], mov(dt),
                                               start=(dt == 0), stop=(dt == NDT - 1))
                    t = self.s_tp.inc(ins)
                    nc.vector.wait_ge(self.s_tp.h, t)
                    ins = nc.vector.tensor_copy(V[:, tt, fcol:fcol + width], bk)
                    self.qkv_bank_tok[2 + tt % 2] = self.s_vc.inc(ins)
                    last_tok = self.qkv_bank_tok[2 + tt % 2]
                if layer == 1:
                    break
        return last_tok

    def quant_l1(self, m, g, WQ, STG):
        nc, PS = self.nc, self.PS
        nc.vector.wait_ge(self.s_tp.h, self.s_tp.v)
        nc.vector.wait_ge(self.s_gr.h, self.s_gr.v)
        for ft in range(4):
            fr = (g * 4 + ft) * 128
            stgv = STG[:, :, :].rearrange("p a b -> p (a b)")
            nc.sync.wait_ge(self.s_vc.h, self.s_vc.v)
            nc.sync.wait_ge(self.s_q[0].h, self.s_q[0].v)
            ld = self.s_q[0].inc(nc.sync.dma_start(stgv, self.l1w[m, fr:fr + 128, :]), 16)
            nc.vector.wait_ge(self.s_q[0].h, ld)
            self.s_vc.inc(nc.vector.reduce_max(self.SCC[:, ft:ft + 1], stgv, axis=AX.X,
                                               apply_absolute_value=True))
        nc.vector.drain()
        nc.vector.tensor_scalar(self.SCC[:, 0:4], self.SCC[:, 0:4], 1.0 / 127.0, FQ_EPS,
                                op0=OP.mult, op1=OP.max)
        nc.vector.drain()
        v0 = self.s_vc.inc(nc.vector.reciprocal(self.SCC[:, 4:8], self.SCC[:, 0:4]))
        nc.tensor.wait_ge(self.s_vc.h, v0)
        t0 = self.s_tp.inc(nc.tensor.transpose(PS[0:8, 1024:1152], self.SCC[:, 0:8], self.EYE[:]))
        nc.scalar.wait_ge(self.s_tp.h, t0)
        s0 = self.s_sc.inc(nc.scalar.copy(self.SCT[:], PS[0:8, 1024:1152]))
        nc.sync.wait_ge(self.s_sc.h, s0)
        nc.sync.wait_ge(self.s_tp.h, self.s_tp.v)
        srow = self.s_gr.inc(nc.sync.dma_start(self.SM0[0:1, 1024:2048], self.SCT[0:8, :]), 16)
        nc.tensor.wait_ge(self.s_gr.h, srow)
        nc.tensor.matmul(self.bank(4), self.ONES1[0:1, 0:128],
                         self.SM0[0:1, 1024:1536], start=True, stop=True)
        ins = nc.tensor.matmul(self.bank(5), self.ONES1[0:1, 0:128],
                               self.SM0[0:1, 1536:2048], start=True, stop=True)
        t1 = self.s_tp.inc(ins)
        nc.vector.wait_ge(self.s_tp.h, t1)
        for dt in range(NDT):
            nc.sync.wait_ge(self.s_vc.h, self.s_vc.v)
            nc.sync.wait_ge(self.s_q[dt % 2].h, self.s_q[dt % 2].v)
            ld = self.s_q[dt % 2].inc(nc.sync.dma_start(STG[:, dt % 2, 0:512],
                                                 self.l1wt[m, dt * 128:(dt + 1) * 128, g * 512:(g + 1) * 512]), 16)
            nc.vector.wait_ge(self.s_q[dt % 2].h, ld)
            for hc in range(2):
                hs = slice(hc * 256, (hc + 1) * 256)
                nc.vector.tensor_tensor(self.T1[:, 0, hs], STG[:, dt % 2, hs], self.bank(5)[:, hs], op=OP.mult)
                nc.vector.drain()
                nc.vector.tensor_copy(self.QI[:], self.T1[:, 0, hs])
                nc.vector.drain()
                nc.vector.tensor_scalar(self.T1[:, 1, hs], self.QI[:], -127.0, 127.0, op0=OP.max, op1=OP.min)
                nc.vector.drain()
                ins = nc.vector.tensor_tensor(WQ[:, dt, hs], self.T1[:, 1, hs], self.bank(4)[:, hs], op=OP.mult)
            self.s_vc.inc(ins)
        nc.tensor.wait_ge(self.s_vc.h, self.s_vc.v)

    def emit_mhat(self, QT, KT, SQB):
        nc, PS, SM0 = self.nc, self.PS, self.SM0
        for si, SRC in ((0, QT), (1, KT)):
            for lt in range(4):
                vs = self.s_vc.inc(nc.vector.tensor_tensor(SQB[:], SRC[:, lt, :], SRC[:, lt, :], op=OP.mult))
                nc.tensor.wait_ge(self.s_vc.h, vs)
                for half in range(2):
                    lh = lt * 2 + half
                    pb = (lh % 2) * 4
                    if self.mh_tok[lh % 2] is not None:
                        nc.tensor.wait_ge(self.s_vc.h, self.mh_tok[lh % 2])
                    for qc in range(NQC):
                        ins = nc.tensor.matmul(PS[0:1, (pb + qc) * 512:(pb + qc + 1) * 512],
                                               self.ONESB[half * 64:half * 64 + 64, 0:1],
                                               SQB[half * 64:half * 64 + 64, qc * 512:(qc + 1) * 512],
                                               start=True, stop=True)
                    t = self.s_tp.inc(ins)
                    nc.vector.wait_ge(self.s_tp.h, t)
                    ins = nc.vector.reduce_max(SM0[0:1, si * 8 + lh:si * 8 + lh + 1],
                                               PS[0:1, pb * 512:(pb + 4) * 512], axis=AX.X)
                    self.mh_tok[lh % 2] = self.s_vc.inc(ins)
        nc.vector.drain()
        v1 = self.s_vc.inc(nc.vector.tensor_tensor(SM0[0:1, 16:24], SM0[0:1, 0:8], SM0[0:1, 8:16], op=OP.mult))
        nc.scalar.wait_ge(self.s_vc.h, v1)
        s1 = self.s_sc.inc(nc.scalar.activation(SM0[0:1, 24:32], SM0[0:1, 16:24], AF.Sqrt))
        nc.vector.wait_ge(self.s_sc.h, s1)
        nc.vector.tensor_scalar(SM0[0:1, 32:40], SM0[0:1, 24:32], -0.125, -1.0, op0=OP.mult, op1=OP.add)
        ev = SM0[0:1, 32:40].rearrange("p (a b) -> p a b", b=2)
        nc.vector.drain()
        v2 = self.s_vc.inc(nc.vector.tensor_tensor(SM0[0:1, 40:44], ev[:, :, 0], ev[:, :, 1], op=OP.min))
        nc.tensor.wait_ge(self.s_vc.h, v2)
        t2 = self.s_tp.inc(nc.tensor.matmul(PS[:, 1024:1028], self.ONES1[0:1, 0:128],
                                            SM0[0:1, 40:44], start=True, stop=True))
        nc.scalar.wait_ge(self.s_tp.h, t2)
        sc = self.s_sc.inc(nc.scalar.copy(self.NEGM8[:], PS[:, 1024:1028]))
        nc.tensor.wait_ge(self.s_sc.h, sc)

    def emit_attn_pair(self, layer, g, lt, qc, QT, KT, V, EB, O):
        nc, PS = self.nc, self.PS
        ha, hb = 2 * lt, 2 * lt + 1
        qsl = slice(qc * 512, (qc + 1) * 512)
        score_tok = [None] * 16
        exp_tok = [None] * 16
        av_tok = [None] * 16
        for kt in range(17):
            if kt < 16:
                r = kt % 2
                sview = PS[:, r * 1024:(r + 1) * 1024]
                if self.sb_tok[r] is not None:
                    nc.tensor.wait_ge(self.s_sc.h, self.sb_tok[r])
                ksl = slice(kt * 128, (kt + 1) * 128)
                nc.tensor.matmul(sview[:, 0:512], KT[0:64, lt, ksl], QT[0:64, lt, qsl], start=True, stop=True)
                ins = nc.tensor.matmul(sview[:, 512:1024], KT[64:128, lt, ksl], QT[64:128, lt, qsl],
                                       start=True, stop=True)
                score_tok[kt] = self.s_tp.inc(ins)
                nc.scalar.wait_ge(self.s_tp.h, score_tok[kt])
                if kt >= 2:
                    nc.scalar.wait_ge(self.s_tp.h, av_tok[kt - 2])
                ins = nc.scalar.activation(EB[:, kt % 2, :], sview, AF.Exp,
                                           bias=self.NEGM8[:, lt:lt + 1], scale=0.125)
                exp_tok[kt] = self.s_sc.inc(ins)
                self.sb_tok[r] = exp_tok[kt]
            if kt >= 1:
                j = kt - 1
                er = j % 2
                nc.tensor.wait_ge(self.s_sc.h, exp_tok[j])
                if j == 0 and self.av_free is not None:
                    nc.tensor.wait_ge(self.s_vc.h, self.av_free)
                st, sp = (j == 0), (j == 15)
                vsl = V[:, j, :]
                nc.tensor.matmul(self.bank(4)[0:64, :], vsl[:, ha * 64:ha * 64 + 64], EB[:, er, 0:512],
                                 start=st, stop=sp, tile_position=(0, 0))
                nc.tensor.matmul(self.bank(4)[64:128, :], vsl[:, hb * 64:hb * 64 + 64], EB[:, er, 512:1024],
                                 start=st, stop=sp, tile_position=(0, 64))
                nc.tensor.matmul(PS[0:1, 3072:3584], self.ONESB[:, 0:1], EB[:, er, 0:512],
                                 start=st, stop=sp, tile_position=(0, 0))
                ins = nc.tensor.matmul(PS[32:33, 3072:3584], self.ONESB[:, 0:1], EB[:, er, 512:1024],
                                       start=st, stop=sp, tile_position=(0, 32))
                av_tok[j] = self.s_tp.inc(ins)
        nc.scalar.wait_ge(self.s_tp.h, av_tok[15])
        nc.scalar.copy(self.SM0[0:1, 1024:1536], PS[0:1, 3072:3584])
        sden = self.s_sc.inc(nc.scalar.copy(self.SM0[32:33, 1536:2048], PS[32:33, 3072:3584]))
        nc.sync.wait_ge(self.s_sc.h, sden)
        self.s_gr.inc(nc.sync.dma_start(self.SCC[:, 0:4], self.SM0[0:1, 1024:1536]), 16)
        g1 = self.s_gr.inc(nc.sync.dma_start(self.SCC[:, 4:8], self.SM0[32:33, 1536:2048]), 16)
        nc.vector.wait_ge(self.s_gr.h, g1)
        vrc = self.s_vc.inc(nc.vector.reciprocal(self.SCC[:, 0:8], self.SCC[:, 0:8]))
        nc.sync.wait_ge(self.s_vc.h, vrc)
        self.s_gr.inc(nc.sync.dma_start(self.SM0[0:1, 1024:1536], self.SCC[:, 0:4]), 16)
        vr = self.s_gr.inc(nc.sync.dma_start(self.SM0[0:1, 1536:2048], self.SCC[:, 4:8]), 16)
        nc.tensor.wait_ge(self.s_gr.h, vr)
        nc.tensor.matmul(self.bank(7)[0:64, :], self.ONES1[0:1, 0:64], self.SM0[0:1, 1024:1536],
                         start=True, stop=True, tile_position=(0, 0))
        tb = self.s_tp.inc(nc.tensor.matmul(self.bank(7)[64:128, :], self.ONES1[0:1, 0:64],
                                            self.SM0[0:1, 1536:2048], start=True, stop=True,
                                            tile_position=(0, 64)))
        nc.vector.wait_ge(self.s_tp.h, tb)
        nc.vector.tensor_copy(self.DIBS[:], self.bank(7))
        nc.vector.drain()
        ins = nc.vector.tensor_tensor(O[:, g * 4 + lt, qsl], self.bank(4), self.DIBS[:], op=OP.mult)
        self.av_free = self.s_vc.inc(ins)
        self.o_tok[g * 4 + lt][qc] = self.av_free

    def p_outproj(self, layer):
        nc, PS = self.nc, self.PS
        O = self.pool_handles[f"O{layer}"]
        if layer == 0:
            WR = self.palloc("WOR", [128, 2, 8, 128], BF16)
        else:
            self.QI = self.palloc("QIp", [128, 256], I32)
            WQ = self.palloc("WPQ", [128, 8, 512], BF16)
            STG = self.palloc("STGP", [128, 2, 512], F32)
        bv16 = self.QKVBV if layer == 0 else self.L1BV
        slot_free = [None, None]
        for g in range(NG):
            if layer == 1:
                self.quant_l1(3, g, WQ, STG)
            for lf in range(4):
                ft = g * 4 + lf
                if layer == 0:
                    sl = ft % 2
                    if slot_free[sl] is not None:
                        nc.gpsimd.wait_ge(self.s_tp.h, slot_free[sl])
                    sem = self.s_wr[sl]
                    gw = (sem, sem.inc(nc.gpsimd.dma_start(WR[:, sl, :, :], self.wot[ft]), 16))
                    wsl = lambda dt: WR[:, sl, dt, :]
                else:
                    gw = None
                    wsl = lambda dt: WQ[:, dt, lf * 128:(lf + 1) * 128]
                if gw is not None:
                    nc.tensor.wait_ge(gw[0].h, gw[1])
                for dt in range(NDT):
                    ins = nc.tensor.matmul(PS[:, 1024 + ft:1024 + ft + 1], wsl(dt), bv16[:, dt:dt + 1],
                                           start=(dt == 0), stop=(dt == NDT - 1))
                tb = self.s_tp.inc(ins)
                nc.vector.wait_ge(self.s_tp.h, tb)
                base_b = self.WOBS[:, ft:ft + 1] if layer == 0 else self.L1BS[:, 24 + ft:25 + ft]
                vb = self.s_vc.inc(nc.vector.scalar_tensor_tensor(
                    self.WEB[:, ft:ft + 1], PS[:, 1024 + ft:1024 + ft + 1], 1.0, base_b,
                    op0=OP.mult, op1=OP.add))
                for qc in range(NQC):
                    r = qc % 2
                    if self.op_bank_tok[r] is not None:
                        nc.tensor.wait_ge(self.s_vc.h, self.op_bank_tok[r])
                    for dt in range(NDT):
                        nc.tensor.wait_ge(self.s_vc.h, self.o_tok[dt][qc] or 0)
                        ins = nc.tensor.matmul(self.bank(r), wsl(dt), O[:, dt, qc * 512:(qc + 1) * 512],
                                               start=(dt == 0), stop=(dt == NDT - 1))
                    t = self.s_tp.inc(ins)
                    nc.vector.wait_ge(self.s_tp.h, t)
                    nc.vector.wait_ge(self.s_vc.h, vb)
                    nc.vector.tensor_scalar(self.T1[:, r, :], self.bank(r), self.WEB[:, ft:ft + 1],
                                            None, op0=OP.add)
                    nc.vector.drain()
                    ins = nc.vector.tensor_tensor(self.X[:, ft, qc * 512:(qc + 1) * 512],
                                                  self.X[:, ft, qc * 512:(qc + 1) * 512],
                                                  self.T1[:, r, :], op=OP.add)
                    self.op_bank_tok[r] = self.s_vc.inc(ins)
                slot_free_tok = self.s_tp.v
                if layer == 0:
                    slot_free[ft % 2] = slot_free_tok
        self.pfree()

    def p_router(self):
        nc, PS = self.nc, self.PS
        self.LGT = self.palloc("LGT", [128, 16, 8], F32)
        rt_rot = [None, None]
        for tt in range(NTT):
            r = tt % 2
            if rt_rot[r] is not None:
                nc.tensor.wait_ge(self.s_sc.h, rt_rot[r])
            for dt in range(NDT):
                ins = nc.tensor.matmul(PS[:, r * 512:r * 512 + 8], self.XN[:, dt, tt * 128:(tt + 1) * 128],
                                       self.RW[:, dt, :], start=(dt == 0), stop=(dt == NDT - 1))
            t = self.s_tp.inc(ins)
            nc.scalar.wait_ge(self.s_tp.h, t)
            rt_rot[r] = self.s_sc.inc(nc.scalar.copy(self.LGT[:, tt, :], PS[:, r * 512:r * 512 + 8]))
        self.rt_tok[0] = rt_rot[(NTT - 1) % 2]
        for tt in range(NTT):
            L = self.LGT[:, tt, :]
            nc.vector.wait_ge(self.s_sc.h, self.rt_tok[0])
            v = self.s_vc.inc(nc.vector.reduce_max(self.SMLL[:, 0:1], L, axis=AX.X, negate=True))
            nc.scalar.wait_ge(self.s_vc.h, v)
            s = self.s_sc.inc(nc.scalar.activation(L, L, AF.Exp, bias=self.SMLL[:, 0:1],
                                                   accum_out=self.SMLL[:, 1:2]))
            nc.vector.wait_ge(self.s_sc.h, s)
            nc.vector.reciprocal(self.SMLL[:, 1:2], self.SMLL[:, 1:2])
            nc.vector.drain()
            nc.vector.tensor_scalar(L, L, self.SMLL[:, 1:2], None, op0=OP.mult)
            nc.vector.drain()
            nc.vector.reduce_max(self.SMLL[:, 0:1], L, axis=AX.X)
            nc.vector.drain()
            nc.vector.tensor_scalar(self.T1[:, 0, 0:8], L, self.SMLL[:, 0:1], None, op0=OP.is_ge)
            nc.vector.drain()
            nc.vector.scalar_tensor_tensor(self.T1[:, 1, 0:8], self.T1[:, 0, 0:8], -BIG, L,
                                           op0=OP.mult, op1=OP.add)
            nc.vector.drain()
            nc.vector.reduce_max(self.SMLL[:, 2:3], self.T1[:, 1, 0:8], axis=AX.X)
            nc.vector.drain()
            nc.vector.tensor_scalar(self.T1[:, 0, 0:8], L, self.SMLL[:, 2:3], None, op0=OP.is_ge)
            nc.vector.drain()
            vg = self.s_vc.inc(nc.vector.tensor_tensor(L, L, self.T1[:, 0, 0:8], op=OP.mult))
            nc.tensor.wait_ge(self.s_vc.h, vg)
            tt2 = self.s_tp.inc(nc.tensor.transpose(PS[0:8, 512:640], L, self.EYE[:]))
            nc.vector.wait_ge(self.s_tp.h, tt2)
            self.s_vc.inc(nc.vector.tensor_copy(self.SM0[0:8, tt * 128:(tt + 1) * 128], PS[0:8, 512:640]))
        self.pfree(1)

    def p_moe(self):
        nc, PS, XN, X = self.nc, self.PS, self.XN, self.X
        HG = self.palloc("HG", [128, 32, 1024], BF16)
        W1R = self.palloc("W1R", [128, 3, 8, 256], BF16)
        W2R = self.palloc("W2R", [128, 4, 2, 1024], BF16)
        self.GB = self.palloc("GBm", [128, 1024], F32)
        w1_free = [None] * 3
        w2_free = [None] * 4
        bank_free = [None] * 8
        gb_tok = None
        gelu_tok = [None] * 32
        for c in range(2):
            csl = slice(c * 1024, (c + 1) * 1024)
            for e in range(MOE_E):
                if gb_tok is not None:
                    nc.sync.wait_ge(self.s_vc.h, gb_tok)
                grow = self.s_gr.inc(nc.sync.dma_start(
                    self.GB[0:1, 0:1024], self.SM0[e:e + 1, c * 1024:(c + 1) * 1024]), 16)
                for b in (4, 5):
                    if bank_free[b] is not None:
                        nc.tensor.wait_ge(self.s_vc.h, bank_free[b])
                nc.tensor.wait_ge(self.s_gr.h, grow)
                ins = None
                for half in range(2):
                    ins = nc.tensor.matmul(self.bank(4 + half), self.ONES1[0:1, 0:128],
                                           self.GB[0:1, half * 512:(half + 1) * 512],
                                           start=True, stop=True)
                tgb = self.s_tp.inc(ins)
                nc.scalar.wait_ge(self.s_tp.h, tgb)
                if gb_tok is not None:
                    nc.scalar.wait_ge(self.s_vc.h, gb_tok)
                sgb = self.s_sc.inc(nc.scalar.copy(self.GB[:], self.bank(4, 2)))
                for ht in range(32):
                    sl = (ht // 2) % 3
                    if ht % 2 == 0:
                        if w1_free[sl] is not None:
                            nc.gpsimd.wait_ge(self.s_tp.h, w1_free[sl])
                        self.w1_tok[sl] = self.s_w1[sl].inc(
                            nc.gpsimd.dma_start(W1R[:, sl, :, :], self.w1t[e, ht // 2]), 16)
                    pb = (ht % 2) * 2
                    if ht >= 2 and gelu_tok[ht - 2] is not None:
                        nc.tensor.wait_ge(self.s_sc.h, gelu_tok[ht - 2])
                    for b in (pb, pb + 1):
                        if bank_free[b] is not None:
                            nc.tensor.wait_ge(self.s_vc.h, bank_free[b])
                            bank_free[b] = None
                    nc.tensor.wait_ge(self.s_w1[sl].h, self.w1_tok[sl])
                    for nh in range(2):
                        for dt in range(NDT):
                            ins = nc.tensor.matmul(self.bank(pb + nh),
                                                   W1R[:, sl, dt, (ht % 2) * 128:(ht % 2) * 128 + 128],
                                                   XN[:, dt, c * 1024 + nh * 512: c * 1024 + (nh + 1) * 512],
                                                   start=(dt == 0), stop=(dt == NDT - 1))
                    t = self.s_tp.inc(ins)
                    if ht % 2 == 1:
                        w1_free[sl] = t
                    nc.scalar.wait_ge(self.s_tp.h, t)
                    ins = nc.scalar.activation(HG[:, ht, :], self.bank(pb, 2), AF.Gelu)
                    gelu_tok[ht] = self.s_sc.inc(ins)
                hg_done = gelu_tok[31]
                for nh in range(2):
                    for ht in range(32):
                        sl2 = (ht // 2) % 4
                        if ht % 2 == 0:
                            if w2_free[sl2] is not None:
                                nc.gpsimd.wait_ge(self.s_tp.h, w2_free[sl2])
                            hq = ht // 4
                            self.w2_tok[sl2] = self.s_w2[sl2].inc(
                                nc.gpsimd.dma_start(W2R[:, sl2, :, :],
                                                    self.w2t[e, hq, :, (ht % 4) // 2 * 2:(ht % 4) // 2 * 2 + 2, :]), 16)
                        if ht == 0:
                            nc.tensor.wait_ge(self.s_sc.h, hg_done)
                            nc.tensor.wait_ge(self.s_sc.h, sgb)
                            for b in range(8):
                                if bank_free[b] is not None:
                                    nc.tensor.wait_ge(self.s_vc.h, bank_free[b])
                                    bank_free[b] = None
                            nc.tensor.wait_ge(self.s_w2[sl2].h, self.w2_tok[sl2])
                        elif ht % 2 == 0:
                            nc.tensor.wait_ge(self.s_w2[sl2].h, self.w2_tok[sl2])
                        for dt in range(NDT):
                            ins = nc.tensor.matmul(self.bank(dt), W2R[:, sl2, ht % 2, dt * 128:(dt + 1) * 128],
                                                   HG[:, ht, nh * 512:(nh + 1) * 512],
                                                   start=(ht == 0), stop=(ht == 31))
                            if ht == 31:
                                tok = self.s_tp.inc(ins)
                                nc.vector.wait_ge(self.s_tp.h, tok)
                                nc.vector.tensor_tensor(self.T1[:, dt % 2, :], self.bank(dt),
                                                        self.GB[:, nh * 512:(nh + 1) * 512], op=OP.mult)
                                nc.vector.drain()
                                xsl = slice(c * 1024 + nh * 512, c * 1024 + (nh + 1) * 512)
                                ins2 = nc.vector.tensor_tensor(X[:, dt, xsl], X[:, dt, xsl],
                                                               self.T1[:, dt % 2, :], op=OP.add)
                                bank_free[dt] = self.s_vc.inc(ins2)
                        if ht == 31:
                            w2_free[sl2] = self.s_tp.v
                        elif ht % 2 == 1:
                            w2_free[sl2] = self.s_tp.inc(ins)
                gb_tok = bank_free[7]
        self.pfree()

    def p_ffn1(self):
        nc, PS, XN, X = self.nc, self.PS, self.XN, self.X
        WQ1 = self.palloc("WQ1", [128, 8, 4096], FP8)
        WQ2 = self.palloc("WQ2", [128, 32, 1024], FP8)
        self.RBS = self.palloc("RBSf", [128, 256], F32)
        self.MBS = self.palloc("MBSf", [128, 256], F32)
        LNC = self.palloc("LNC", [128, 32, 2], BF16)
        G = self.palloc("G", [128, 32, 256], BF16)
        SQGF = self.palloc("SQGF", [128, 4, 256], BF16)
        QSC = self.palloc("QSC", [128, 2, 512], F32)
        STGF = self.palloc("STGF", [128, 2, 1024], F32)
        nc.vector.tensor_copy(LNC[:, :, 0], self.LNBB16[:])
        nc.vector.tensor_copy(LNC[:, :, 1], self.LNWB16[:])
        slot_tok = [None, None]
        self.wq1_co = {}
        for wi, (src, ntile, width, WQ) in ((1, (self.wprt, 32, 1024, WQ2)),
                                            (0, (self.wfct, 8, 4096, WQ1))):
            nc.vector.memset(self.SMLL[:, 3:4], 0.0)
            wu = min(1024, width)
            units = [(it, co) for co in range(0, width, wu) for it in range(ntile)]
            nhalf = len(units)
            for ih, (it, co) in enumerate(units):
                sl = ih % 2
                if slot_tok[sl] is not None:
                    nc.sync.wait_ge(self.s_vc.h, slot_tok[sl])
                nc.sync.wait_ge(self.s_q[sl].h, self.s_q[sl].v)
                ld = self.s_q[sl].inc(nc.sync.dma_start(
                    STGF[:, sl, 0:wu], src[it * 128:it * 128 + 128, co:co + wu]), 16)
                nc.vector.wait_ge(self.s_q[sl].h, ld)
                nc.vector.reduce_sum(self.SMLL[:, 2:3], STGF[:, sl, 0:wu], axis=AX.X,
                                     apply_absolute_value=True)
                nc.vector.drain()
                v = nc.vector.tensor_tensor(self.SMLL[:, 3:4], self.SMLL[:, 3:4],
                                            self.SMLL[:, 2:3], op=OP.add)
                slot_tok[sl] = self.s_vc.inc(v)
                nc.vector.drain()
            vsum = slot_tok[(nhalf - 1) % 2]
            nc.tensor.wait_ge(self.s_vc.h, vsum)
            tg = self.s_tp.inc(nc.tensor.matmul(PS[0:1, 1024:1025], self.ONE2F[:, 0:1],
                                                self.SMLL[:, 3:4], start=True, stop=True))
            nc.vector.wait_ge(self.s_tp.h, tg)
            nc.vector.tensor_scalar(self.SM0[0:1, 48:49], PS[0:1, 1024:1025], 1.0 / (4096.0 * 1024.0),
                                    AM_EPS, op0=OP.mult, op1=OP.add)
            nc.vector.drain()
            vg = self.s_vc.inc(nc.vector.reciprocal(self.SM0[0:1, 48:49], self.SM0[0:1, 48:49]))
            nc.tensor.wait_ge(self.s_vc.h, vg)
            tb = self.s_tp.inc(nc.tensor.matmul(PS[:, 1024:1025], self.ONES1[0:1, 0:128],
                                                self.SM0[0:1, 48:49], start=True, stop=True))
            nc.scalar.wait_ge(self.s_tp.h, tb)
            sg = self.s_sc.inc(nc.scalar.copy(self.INVGC[:, wi:wi + 1], PS[:, 1024:1025]))
            nc.vector.wait_ge(self.s_sc.h, sg)
            for ih, (it, co) in enumerate(units):
                sl = ih % 2
                if slot_tok[sl] is not None:
                    nc.sync.wait_ge(self.s_vc.h, slot_tok[sl])
                nc.sync.wait_ge(self.s_q[sl].h, self.s_q[sl].v)
                ld = self.s_q[sl].inc(nc.sync.dma_start(
                    STGF[:, sl, 0:wu], src[it * 128:it * 128 + 128, co:co + wu]), 16)
                nc.vector.wait_ge(self.s_q[sl].h, ld)
                for ch in range(wu // 512):
                    cs = slice(ch * 512, (ch + 1) * 512)
                    nc.vector.drain()
                    nc.vector.tensor_scalar(QSC[:, 0, :], STGF[:, sl, cs], self.INVGC[:, wi:wi + 1], 0.5,
                                            op0=OP.mult, op1=OP.is_gt)
                    nc.vector.tensor_scalar(QSC[:, 1, :], STGF[:, sl, cs], self.INVGC[:, wi:wi + 1], -0.5,
                                            op0=OP.mult, op1=OP.is_lt)
                    nc.vector.drain()
                    col = co + ch * 512
                    v = nc.vector.tensor_tensor(WQ[:, it, col:col + 512], QSC[:, 0, :], QSC[:, 1, :],
                                                op=OP.subtract)
                    slot_tok[sl] = self.s_vc.inc(v)
                if ih == nhalf - 1 or (wi == 0 and ih == nhalf // 2 - 1):
                    self.wq1_co[(wi, ih)] = self.s_vc.v
            if wi == 1:
                wq2_done = self.s_vc.v
        vq_done = self.s_vc.v
        wq1_half = self.wq1_co.get((0, len(units) // 2 - 1), vq_done)
        nc.tensor.wait_ge(self.s_vc.h, wq2_done)
        for dt in range(NDT):
            for ht in range(32):
                ins = nc.tensor.matmul(PS[:, 1536 + 2 * dt:1538 + 2 * dt],
                                       WQ2[:, ht, dt * 128:(dt + 1) * 128], LNC[:, ht, :],
                                       start=(ht == 0), stop=(ht == 31))
        tpb = self.s_tp.inc(ins)
        nc.vector.wait_ge(self.s_tp.h, tpb)
        vpb = self.s_vc.inc(nc.vector.tensor_copy(self.PBIA[:], PS[:, 1536:1552]))
        self.pfree(2)
        W = 256
        gelu_tok = [None] * 32
        sq_tok = [None] * 32
        fold_tok = [None] * 32
        bank_free = [None] * 8
        for c in range(T // W):
            csl = slice(c * W, (c + 1) * W)
            stat_free = [None] * 4
            if c == 0:
                nc.tensor.wait_ge(self.s_vc.h, wq1_half)
            def emit_stats(ht):
                nc.tensor.wait_ge(self.s_vc.h, sq_tok[ht])
                if ht == 0:
                    for b in (4, 5):
                        if bank_free[b] is not None:
                            nc.tensor.wait_ge(self.s_vc.h, bank_free[b])
                            bank_free[b] = None
                nc.tensor.matmul(PS[0:1, 2048:2048 + W], self.ONESB[:, 0:1], G[:, ht, :],
                                 start=(ht == 0), stop=(ht == 31))
                ins = nc.tensor.matmul(PS[0:1, 2560:2560 + W], self.ONESB[:, 0:1], SQGF[:, ht % 4, :],
                                       start=(ht == 0), stop=(ht == 31))
                st2 = self.s_tp.inc(ins)
                stat_free[ht % 4] = st2
                nc.vector.wait_ge(self.s_tp.h, st2)
                fold_tok[ht] = self.s_vc.inc(nc.vector.tensor_scalar(
                    G[:, ht, :], G[:, ht, :], self.LNW[:, ht:ht + 1], None, op0=OP.mult))
            for ht in range(32):
                pb = ht % 2
                if ht >= 2 and gelu_tok[ht - 2] is not None:
                    nc.tensor.wait_ge(self.s_sc.h, gelu_tok[ht - 2])
                if bank_free[pb] is not None:
                    nc.tensor.wait_ge(self.s_vc.h, bank_free[pb])
                    bank_free[pb] = None
                if c == 0 and ht == 16:
                    nc.tensor.wait_ge(self.s_vc.h, vq_done)
                for dt in range(NDT):
                    ins = nc.tensor.matmul(self.bank(pb)[:, 0:W], WQ1[:, dt, ht * 128:(ht + 1) * 128],
                                           XN[:, dt, csl], start=(dt == 0), stop=(dt == NDT - 1))
                t = self.s_tp.inc(ins)
                nc.scalar.wait_ge(self.s_tp.h, t)
                gelu_tok[ht] = self.s_sc.inc(nc.scalar.activation(G[:, ht, :], self.bank(pb)[:, 0:W], AF.Gelu))
                nc.vector.wait_ge(self.s_sc.h, gelu_tok[ht])
                if stat_free[ht % 4] is not None:
                    nc.vector.wait_ge(self.s_tp.h, stat_free[ht % 4])
                sq_tok[ht] = self.s_vc.inc(nc.vector.tensor_tensor(
                    SQGF[:, ht % 4, :], G[:, ht, :], G[:, ht, :], op=OP.mult))
                if ht >= 2:
                    emit_stats(ht - 2)
            emit_stats(30)
            emit_stats(31)
            nc.vector.tensor_scalar(self.SM0[0:1, 0:W], PS[0:1, 2048:2048 + W], 1.0 / 4096.0, None, op0=OP.mult)
            nc.vector.tensor_scalar(self.SM0[0:1, 512:512 + W], PS[0:1, 2560:2560 + W], 1.0 / 4096.0,
                                    None, op0=OP.mult)
            nc.vector.drain()
            nc.vector.tensor_tensor(PS[0:1, 2048:2048 + W], self.SM0[0:1, 0:W], self.SM0[0:1, 0:W], op=OP.mult)
            nc.vector.drain()
            vv = self.s_vc.inc(nc.vector.tensor_tensor(self.SM0[0:1, 512:512 + W], self.SM0[0:1, 512:512 + W],
                                                       PS[0:1, 2048:2048 + W], op=OP.subtract))
            nc.scalar.wait_ge(self.s_vc.h, vv)
            ss = self.s_sc.inc(nc.scalar.activation(self.SM0[0:1, 512:512 + W], self.SM0[0:1, 512:512 + W],
                                                    AF.Sqrt, bias=self.EPSC[0:1, 0:1]))
            nc.vector.wait_ge(self.s_sc.h, ss)
            nc.vector.reciprocal(self.SM0[0:1, 512:512 + W], self.SM0[0:1, 512:512 + W])
            nc.vector.drain()
            vr = self.s_vc.inc(nc.vector.scalar_tensor_tensor(self.SM0[0:1, 1024:1024 + W], self.SM0[0:1, 0:W],
                                                              -1.0, self.SM0[0:1, 512:512 + W],
                                                              op0=OP.mult, op1=OP.mult))
            nc.tensor.wait_ge(self.s_vc.h, vr)
            for b in (6, 7):
                if bank_free[b] is not None:
                    nc.tensor.wait_ge(self.s_vc.h, bank_free[b])
                    bank_free[b] = None
            nc.tensor.matmul(self.bank(6)[:, 0:W], self.ONES1[0:1, 0:128], self.SM0[0:1, 512:512 + W],
                             start=True, stop=True)
            tb2 = self.s_tp.inc(nc.tensor.matmul(self.bank(7)[:, 0:W], self.ONES1[0:1, 0:128],
                                                 self.SM0[0:1, 1024:1024 + W], start=True, stop=True))
            nc.scalar.wait_ge(self.s_tp.h, tb2)
            nc.scalar.copy(self.RBS[:, 0:W], self.bank(6)[:, 0:W])
            srb = self.s_sc.inc(nc.scalar.copy(self.MBS[:, 0:W], self.bank(7)[:, 0:W]))
            for ht in range(32):
                if ht == 0:
                    nc.tensor.wait_ge(self.s_vc.h, fold_tok[31])
                    nc.tensor.wait_ge(self.s_vc.h, vpb)
                    nc.tensor.wait_ge(self.s_sc.h, srb)
                    for b in range(8):
                        if bank_free[b] is not None:
                            nc.tensor.wait_ge(self.s_vc.h, bank_free[b])
                            bank_free[b] = None
                for dt in range(NDT):
                    ins = nc.tensor.matmul(self.bank(dt)[:, 0:W], WQ2[:, ht, dt * 128:(dt + 1) * 128],
                                           G[:, ht, :], start=(ht == 0), stop=(ht == 31))
                    if ht == 31:
                        tok = self.s_tp.inc(ins)
                        nc.vector.wait_ge(self.s_tp.h, tok)
                        nc.vector.tensor_tensor(self.T1[:, dt % 2, 0:W], self.bank(dt)[:, 0:W],
                                                self.RBS[:, 0:W], op=OP.mult)
                        nc.vector.drain()
                        nc.vector.scalar_tensor_tensor(self.T1[:, dt % 2, 256:256 + W], self.MBS[:, 0:W],
                                                       self.PBIA[:, 2 * dt + 1:2 * dt + 2], self.T1[:, dt % 2, 0:W],
                                                       op0=OP.mult, op1=OP.add)
                        nc.vector.drain()
                        nc.vector.tensor_scalar(self.T1[:, dt % 2, 0:W], self.T1[:, dt % 2, 256:256 + W],
                                                self.PBIA[:, 2 * dt:2 * dt + 1], None, op0=OP.add)
                        nc.vector.drain()
                        ins2 = nc.vector.tensor_tensor(X[:, dt, csl], X[:, dt, csl],
                                                       self.T1[:, dt % 2, 0:W], op=OP.add)
                        bank_free[dt] = self.s_vc.inc(ins2)
            self.mm2_tok = self.s_tp.v
        self.pfree()

    def p_store(self):
        nc = self.nc
        for se in ([self.s_tp, self.s_sc, self.s_vc, self.s_gw, self.s_ld, self.s_gr]
                   + self.s_wr + self.s_w1 + self.s_w2 + self.s_q):
            nc.sync.wait_ge(se.h, se.v)
            nc.gpsimd.wait_ge(se.h, se.v)
        if self.dump is None:
            for dt in range(NDT):
                self.s_st.inc(nc.sync.dma_start(self.outT[dt * 128:(dt + 1) * 128, :], self.X[:, dt, :]), 16)
            nc.sync.wait_ge(self.s_st.h, self.s_st.v)
        elif self.dump == "QK":
            for i, src in enumerate((self.pool_handles[f"QT{self.dump_layer}"], self.pool_handles[f"KT{self.dump_layer}"])):
                for lt in range(4):
                    self.s_gw.inc(nc.gpsimd.dma_start(
                        self.outT[(i * 4 + lt) * 128:(i * 4 + lt + 1) * 128, :], src[:, lt, :]), 16)
            nc.gpsimd.wait_ge(self.s_gw.h, self.s_gw.v)
            nc.sync.wait_ge(self.s_gw.h, self.s_gw.v)
        elif self.dump == "V":
            V = self.pool_handles[f"V{self.dump_layer}"]
            for i in range(4):
                for j in range(4):
                    self.s_gw.inc(nc.gpsimd.dma_start(
                        self.outT[i * 128:(i + 1) * 128, j * 512:(j + 1) * 512], V[:, i * 4 + j, :]), 16)
            nc.gpsimd.wait_ge(self.s_gw.h, self.s_gw.v)
            nc.sync.wait_ge(self.s_gw.h, self.s_gw.v)
        else:
            src = {"XN": self.XN}.get(self.dump) if self.dump in ("XN",) else self.pool_handles[self.dump]
            for dt in range(NDT):
                self.s_gw.inc(nc.gpsimd.dma_start(self.outT[dt * 128:(dt + 1) * 128, :], src[:, dt, :]), 16)
            nc.gpsimd.wait_ge(self.s_gw.h, self.s_gw.v)
            nc.sync.wait_ge(self.s_gw.h, self.s_gw.v)


def build_program(phases=None, dump=None):
    mk = MK(phases, dump)
    mk.qkv_bank_tok = [None] * 4
    mk.mh_tok = [None, None]
    mk.sb_tok = [None, None]
    mk.av_free = None
    mk.op_bank_tok = [None, None]
    mk.rt_tok = [None]
    mk.w1_tok = [None] * 3
    mk.w2_tok = [None] * 4
    mk.pool_handles = {}
    mk.mm2_tok = None
    _orig = mk.palloc

    def palloc2(name, shape, dtype):
        h = _orig(name, shape, dtype)
        mk.pool_handles[name] = h
        return h
    mk.palloc = palloc2
    nc = mk.build()
    return nc


def host_inputs(b_x, ins):
    d = {}
    d["xT"] = np.ascontiguousarray(b_x.T)
    qkv = ins["L0_attn_qkv_w"]
    d["qkvt"] = np.ascontiguousarray(qkv.reshape(24, 128, 8, 128).transpose(0, 3, 2, 1))
    d["qkvb"] = np.ascontiguousarray(ins["L0_attn_qkv_b"].reshape(24, 128).T)
    wo = ins["L0_attn_out_w"]
    d["wot"] = np.ascontiguousarray(wo.reshape(8, 128, 8, 128).transpose(0, 3, 2, 1))
    d["wob"] = np.ascontiguousarray(ins["L0_attn_out_b"].reshape(8, 128).T)
    d["rwt"] = np.ascontiguousarray(ins["L0_ffn_router_w"].T.reshape(8, 128, 8).transpose(1, 0, 2))
    w1 = ins["L0_ffn_w1"]
    d["w1t"] = np.ascontiguousarray(w1.transpose(0, 2, 1).reshape(8, 8, 128, 16, 256).transpose(0, 3, 2, 1, 4))
    w2 = ins["L0_ffn_w2"]
    d["w2t"] = np.ascontiguousarray(w2.transpose(0, 2, 1).reshape(8, 8, 4, 128, 1024).transpose(0, 1, 3, 2, 4))
    l1 = [ins["L1_attn_wq"], ins["L1_attn_wk"], ins["L1_attn_wv"], ins["L1_attn_wp"]]
    d["l1w"] = np.ascontiguousarray(np.stack(l1))
    d["l1wt"] = np.ascontiguousarray(np.stack([w.T for w in l1]))
    bb = [ins["L1_attn_bq"], ins["L1_attn_bk"], ins["L1_attn_bv"], ins["L1_attn_bp"]]
    d["l1b"] = np.ascontiguousarray(np.concatenate([b.reshape(8, 128).T for b in bb], axis=1))
    d["wfct"] = np.ascontiguousarray(ins["L1_ffn_W_fc"].T)
    d["lnwb"] = np.ascontiguousarray(np.concatenate(
        [ins["L1_ffn_ln_w"].reshape(32, 128).T, ins["L1_ffn_ln_b"].reshape(32, 128).T], axis=1))
    d["wprt"] = np.ascontiguousarray(ins["L1_ffn_W_proj"].T)
    d["eye"] = np.eye(128, dtype=np.float32)
    return {k: v.astype(np.float32, copy=False) for k, v in d.items()}


def kernel(**inputs):
    x = inputs["L0_attn_x"]
    nc = build_program()
    in_maps = [host_inputs(x[b], inputs) for b in range(8)]
    res = run_bass_kernel_spmd(nc, in_maps, core_ids=list(range(8)))
    out = np.stack([np.ascontiguousarray(r["outT"].T) for r in res.results])
    return out.astype(np.float32)


# version 16 (final)
# speedup vs baseline: 1.0043x; 1.0038x over previous
import numpy as np
from contextlib import ExitStack

import concourse.bass as bass
import concourse.mybir as mybir
from concourse.bass_utils import run_bass_kernel_spmd

F32 = mybir.dt.float32
BF16 = mybir.dt.bfloat16
FP8 = mybir.dt.float8e4
I32 = mybir.dt.int32
AF = mybir.ActivationFunctionType
OP = mybir.AluOpType
AX = mybir.AxisListType

D = 1024
T = 2048
NDT = 8
NTT = 16
NQC = 4
H = 16
DH = 64
NG = 2
MOE_E = 8
LN_EPS = 1e-5
FQ_EPS = float(np.finfo(np.float32).eps)
AM_EPS = 1e-6
BIG = 1e30

ALL_PHASES = ["ln1", "attn0", "proj0", "ln2", "router", "moe",
              "ln3", "attn1", "proj1", "ln4", "ffn1"]


class Sem:
    def __init__(self, nc, stk, name):
        self.h = stk.enter_context(nc.semaphore(name))
        self.v = 0

    def inc(self, instr, n=1):
        instr.then_inc(self.h, n)
        self.v += n
        return self.v


class MK:
    def __init__(self, phases=None, dump=None):
        self.dump = dump
        self.dump_layer = 0
        self.phases = ALL_PHASES if phases is None else phases
        self.nc = bass.Bass("TRN2", target_bir_lowering=False)
        self.stk = ExitStack()
        self.pool = []

    def sb(self, name, shape, dtype):
        return self.stk.enter_context(self.nc.sbuf_tensor(name, shape, dtype))

    def palloc(self, name, shape, dtype):
        cm = self.nc.sbuf_tensor(name, shape, dtype)
        h = cm.__enter__()
        self.pool.append(cm)
        return h

    def pfree(self, n=None):
        k = len(self.pool) if n is None else n
        for _ in range(k):
            self.pool.pop().__exit__(None, None, None)

    def barrier(self):
        nc, s = self.nc, self.s_bar
        for eng in (nc.tensor, nc.vector, nc.scalar, nc.sync):
            eng.drain()
            eng.sem_inc(s.h, 1)
            s.v += 1
        nc.gpsimd.sem_inc(s.h, 1)
        s.v += 1
        for eng in (nc.tensor, nc.vector, nc.scalar, nc.gpsimd, nc.sync):
            eng.wait_ge(s.h, s.v)

    def bank(self, i, n=1):
        return self.PS[:, 512 * i:512 * (i + n)]

    def build(self):
        nc = self.nc
        s = self.stk
        di = lambda name, shape: nc.dram_tensor(name, shape, F32, kind="ExternalInput").ap()
        self.xT = di("xT", [D, T])
        self.qkvt = di("qkvt", [24, 128, 8, 128])
        self.qkvb = di("qkvb", [128, 24])
        self.wot = di("wot", [8, 128, 8, 128])
        self.wob = di("wob", [128, 8])
        self.rwt = di("rwt", [128, 8, 8])
        self.w1t = di("w1t", [MOE_E, 16, 128, 8, 256])
        self.w2t = di("w2t", [MOE_E, 8, 128, 4, 1024])
        self.l1w = di("l1w", [4, D, D])
        self.l1wt = di("l1wt", [4, D, D])
        self.l1b = di("l1b", [128, 32])
        self.wfct = di("wfct", [D, 4096])
        self.lnwb = di("lnwb", [128, 64])
        self.wprt = di("wprt", [4096, D])
        self.eye_in = di("eye", [128, 128])
        self.outT = nc.dram_tensor("outT", [D, T], F32, kind="ExternalOutput").ap()

        self.X = self.sb("X", [128, NDT, T], F32)
        self.XN = self.sb("XN", [128, NDT, T], BF16)
        self.SM0 = self.sb("SM0", [128, 2048], F32)
        self.ONES1 = self.sb("ONES1", [128, 128], F32)
        self.ONESB = self.sb("ONESB", [128, 2], BF16)
        self.ONE2F = self.sb("ONE2F", [128, 2], F32)
        self.EYE = self.sb("EYE", [128, 128], F32)
        self.QKVB = self.sb("QKVB", [128, 24], F32)
        self.QKVBV = self.sb("QKVBV", [128, 8], BF16)
        self.WOBS = self.sb("WOBS", [128, 8], F32)
        self.L1BS = self.sb("L1BS", [128, 32], F32)
        self.L1BV = self.sb("L1BV", [128, 8], BF16)
        self.LNW = self.sb("LNW", [128, 32], F32)
        self.LNWB16 = self.sb("LNWB16", [128, 32], BF16)
        self.LNBB16 = self.sb("LNBB16", [128, 32], BF16)
        self.NEGM8 = self.sb("NEGM8", [128, 4], F32)
        self.RW = self.sb("RW", [128, 8, 8], BF16)
        self.SCT = self.sb("SCT", [8, 128], F32)
        self.SCC = self.sb("SCC", [128, 8], F32)
        self.SMLL = self.sb("SMLL", [128, 4], F32)
        self.WEB = self.sb("WEB", [128, 8], F32)
        self.INVGC = self.sb("INVGC", [128, 2], F32)
        self.PBIA = self.sb("PBIA", [128, 16], F32)
        self.T1 = self.sb("T1", [128, 2, 512], F32)
        self.EPSC = self.sb("EPSC", [128, 1], F32)
        self.SQG = None
        self.PS = s.enter_context(nc.psum_tensor("PS", [128, 4096], F32))

        self.s_ld = Sem(nc, s, "s_ld")
        self.s_gw = Sem(nc, s, "s_gw")
        self.s_st = Sem(nc, s, "s_st")
        self.s_tp = Sem(nc, s, "s_tp")
        self.s_sc = Sem(nc, s, "s_sc")
        self.s_vc = Sem(nc, s, "s_vc")
        self.s_bar = Sem(nc, s, "s_bar")
        self.s_wr = [Sem(nc, s, f"s_wr{i}") for i in range(2)]
        self.s_w1 = [Sem(nc, s, f"s_w1{i}") for i in range(3)]
        self.s_w2 = [Sem(nc, s, f"s_w2{i}") for i in range(4)]
        self.s_q = [Sem(nc, s, f"s_q{i}") for i in range(2)]
        self.s_gr = Sem(nc, s, "s_gr")

        ph = self.phases
        self.p_load()
        self.barrier()
        steps = [("ln1", self.p_ln, ()), ("attn0", self.p_attn, (0,)),
                 ("proj0", self.p_outproj, (0,)), ("ln2", self.p_ln, ()),
                 ("router", self.p_router, ()), ("moe", self.p_moe, ()),
                 ("ln3", self.p_ln, ()), ("attn1", self.p_attn, (1,)),
                 ("proj1", self.p_outproj, (1,)), ("ln4", self.p_ln, ()),
                 ("ffn1", self.p_ffn1, ())]
        for name, fn, args in steps:
            if name in ph:
                fn(*args)
                self.barrier()
        self.p_store()
        self.pfree()
        self.stk.close()
        return self.nc

    def p_load(self):
        nc = self.nc
        for dt in range(NDT):
            self.s_ld.inc(nc.sync.dma_start(self.X[:, dt, :], self.xT[dt * 128:(dt + 1) * 128, :]), 16)
        for dst, src in ((self.QKVB, self.qkvb[:]), (self.WOBS, self.wob[:]),
                         (self.L1BS, self.l1b[:]), (self.LNW, self.lnwb[:, 0:32]),
                         (self.EYE, self.eye_in[:])):
            self.s_ld.inc(nc.sync.dma_start(dst[:], src), 16)
        self.s_gw.inc(nc.gpsimd.dma_start(self.RW[:], self.rwt[:]), 16)
        self.s_gw.inc(nc.gpsimd.dma_start(self.LNWB16[:], self.lnwb[:, 0:32]), 16)
        self.s_gw.inc(nc.gpsimd.dma_start(self.LNBB16[:], self.lnwb[:, 32:64]), 16)
        nc.vector.wait_ge(self.s_ld.h, self.s_ld.v)
        nc.vector.wait_ge(self.s_gw.h, self.s_gw.v)
        nc.vector.memset(self.ONES1[:], 1.0)
        nc.vector.memset(self.EPSC[:], LN_EPS)
        nc.vector.memset(self.ONESB[:], 1.0)
        nc.vector.memset(self.ONE2F[:], 1.0)
        nc.vector.tensor_copy(self.QKVBV[:], self.QKVB[:, 16:24])
        nc.vector.tensor_copy(self.L1BV[:], self.L1BS[:, 16:24])

    def p_ln(self):
        self._ln_n = getattr(self, "_ln_n", 0) + 1
        """XN = (X - mu) * rsqrt(var+eps) in bf16, two 1024-token half-passes.
        PSUM per half: sum(x) banks0-1, sum(x^2) banks4-5; then MUB banks0-1,
        RB banks2-3 broadcasts."""
        nc, X, XN, SM0, PS = self.nc, self.X, self.XN, self.SM0, self.PS
        XSQ = self.palloc(f"XSQ{self._ln_n}", [128, 2, 512], F32)
        LNTS = self.palloc(f"LNTS{self._ln_n}", [128, 2, 512], F32)
        for half in range(2):
            base = half * 1024
            t_sq0 = self.s_tp.v
            sq_tok = []
            for q2 in range(2):
                for dt in range(NDT):
                    i = q2 * NDT + dt
                    if i >= 2:
                        nc.scalar.wait_ge(self.s_tp.h, t_sq0 + (i - 1))
                    ins = nc.scalar.activation(XSQ[:, i % 2, :],
                                               X[:, dt, base + q2 * 512: base + (q2 + 1) * 512], AF.Square)
                    sq_tok.append(self.s_sc.inc(ins))
            for q2 in range(2):
                for dt in range(NDT):
                    i = q2 * NDT + dt
                    nc.tensor.matmul(PS[0:1, q2 * 512:(q2 + 1) * 512], self.ONE2F[:, 0:1],
                                     X[:, dt, base + q2 * 512: base + (q2 + 1) * 512],
                                     start=(dt == 0), stop=(dt == NDT - 1))
                    nc.tensor.wait_ge(self.s_sc.h, sq_tok[i])
                    ins = nc.tensor.matmul(PS[0:1, 2048 + q2 * 512:2048 + (q2 + 1) * 512], self.ONE2F[:, 0:1],
                                           XSQ[:, i % 2, :], start=(dt == 0), stop=(dt == NDT - 1))
                    self.s_tp.inc(ins)
            t_stats = self.s_tp.v
            nc.vector.wait_ge(self.s_tp.h, t_stats)
            nc.vector.tensor_scalar(SM0[0:1, 0:1024], PS[0:1, 0:1024], 1.0 / D, None, op0=OP.mult)
            nc.vector.tensor_scalar(SM0[0:1, 1024:2048], PS[0:1, 2048:3072], 1.0 / D, None, op0=OP.mult)
            nc.vector.drain()
            nc.vector.tensor_tensor(PS[0:1, 0:1024], SM0[0:1, 0:1024], SM0[0:1, 0:1024], op=OP.mult)
            nc.vector.drain()
            v_var = self.s_vc.inc(nc.vector.tensor_tensor(
                SM0[0:1, 1024:2048], SM0[0:1, 1024:2048], PS[0:1, 0:1024], op=OP.subtract))
            nc.scalar.wait_ge(self.s_vc.h, v_var)
            s_sqrt = self.s_sc.inc(nc.scalar.activation(SM0[0:1, 1024:2048], SM0[0:1, 1024:2048],
                                                        AF.Sqrt, bias=self.EPSC[0:1, 0:1]))
            nc.vector.wait_ge(self.s_sc.h, s_sqrt)
            v_rstd = self.s_vc.inc(nc.vector.reciprocal(SM0[0:1, 1024:2048], SM0[0:1, 1024:2048]))
            nc.tensor.wait_ge(self.s_vc.h, v_rstd)
            t_bc = None
            for off in (0, 512):
                nc.tensor.matmul(PS[:, off:off + 512], self.ONES1[0:1, 0:128],
                                 SM0[0:1, off:off + 512], start=True, stop=True)
                ins = nc.tensor.matmul(PS[:, 1024 + off:1536 + off], self.ONES1[0:1, 0:128],
                                       SM0[0:1, 1024 + off:1536 + off], start=True, stop=True)
                t_bc = self.s_tp.inc(ins)
            nc.vector.wait_ge(self.s_tp.h, t_bc)
            lastv = None
            for dt in range(NDT):
                for q2 in range(2):
                    sl = slice(base + q2 * 512, base + (q2 + 1) * 512)
                    nc.vector.tensor_tensor(LNTS[:, q2, :], X[:, dt, sl],
                                            PS[:, q2 * 512:(q2 + 1) * 512], op=OP.subtract)
                    nc.vector.drain()
                    ins = nc.vector.tensor_tensor(XN[:, dt, sl], LNTS[:, q2, :],
                                                  PS[:, 1024 + q2 * 512:1536 + q2 * 512], op=OP.mult)
                    lastv = self.s_vc.inc(ins)
            nc.tensor.wait_ge(self.s_vc.h, lastv)
        self.pfree(2)

    def p_attn(self, layer):
        nc, PS, XN = self.nc, self.PS, self.XN
        O = self.palloc(f"O{layer}", [128, NDT, T], BF16)
        QT = self.palloc(f"QT{layer}", [128, 4, T], BF16)
        KT = self.palloc(f"KT{layer}", [128, 4, T], BF16)
        V = self.palloc(f"V{layer}", [128, NTT, 512], BF16)
        EB = self.palloc(f"EB{layer}", [128, 2, 1024], BF16)
        SQB = EB[:, :, :].rearrange("p a b -> p (a b)")
        self.DIBS = self.T1[:, 1, :]
        if layer == 0:
            WR = self.palloc("WR", [128, 2, 8, 128], BF16)
        else:
            self.QI = self.palloc("QIa", [128, 256], I32)
            WQ = self.palloc("WQ", [128, 8, 512], BF16)
            STG = self.palloc("STG", [128, 2, 512], F32)
        self.o_tok = [[None] * NQC for _ in range(NDT)]

        slot_free = [None, None]
        for g in range(NG):
            nc.tensor.wait_ge(self.s_sc.h, self.s_sc.v)
            nc.tensor.wait_ge(self.s_vc.h, self.s_vc.v)
            if layer == 0:
                chunks = []
                for m in range(3):
                    for ftl in range(4):
                        chunks.append((m, ftl, m * 8 + g * 4 + ftl))
                for ci, (m, ftl, fc) in enumerate(chunks):
                    sl = ci % 2
                    if slot_free[sl] is not None:
                        nc.gpsimd.wait_ge(self.s_tp.h, slot_free[sl])
                    sem = self.s_wr[sl]
                    gw = (sem, sem.inc(nc.gpsimd.dma_start(WR[:, sl, :, :], self.qkvt[fc]), 16))
                    self.emit_qkv_chunk(layer, g, m, ftl, WR[:, sl, :, :], gw, QT, KT, V)
                    slot_free[sl] = self.s_tp.v
            else:
                for m in range(3):
                    self.quant_l1(m, g, WQ, STG)
                    self.emit_qkv_chunk(layer, g, m, None, WQ, None, QT, KT, V)
            self.emit_mhat(QT, KT, SQB)
            for lt in range(4):
                for qc in range(NQC):
                    self.emit_attn_pair(layer, g, lt, qc, QT, KT, V, EB, O)
        if self.dump in ("QK", "V"):
            pass
        else:
            self.pfree(len(self.pool) - 1)

    def emit_qkv_chunk(self, layer, g, m, ftl, W, gw_tok, QT, KT, V):
        nc, PS, XN = self.nc, self.PS, self.XN
        ftls = [ftl] if layer == 0 else list(range(4))
        last_tok = None
        for lf in ftls:
            wsl = (lambda dt: W[:, dt, lf * 128:(lf + 1) * 128]) if layer == 1 else \
                  (lambda dt: W[:, dt, :])
            fglob = (m * 8 if layer == 0 else m * 8) + g * 4 + lf
            if m < 2:
                dst = QT if m == 0 else KT
                bias = self.QKVB[:, fglob:fglob + 1] if layer == 0 else \
                    self.L1BS[:, m * 8 + g * 4 + lf: m * 8 + g * 4 + lf + 1]
                for qc in range(NQC):
                    bk = self.bank(qc % 2)
                    if self.qkv_bank_tok[qc % 2] is not None:
                        nc.tensor.wait_ge(self.s_vc.h, self.qkv_bank_tok[qc % 2])
                    if gw_tok is not None and qc == 0:
                        nc.tensor.wait_ge(gw_tok[0].h, gw_tok[1])
                    for dt in range(NDT):
                        ins = nc.tensor.matmul(bk, wsl(dt), XN[:, dt, qc * 512:(qc + 1) * 512],
                                               start=(dt == 0), stop=(dt == NDT - 1))
                    t = self.s_tp.inc(ins)
                    nc.vector.wait_ge(self.s_tp.h, t)
                    ins = nc.vector.tensor_scalar(dst[:, lf, qc * 512:(qc + 1) * 512], bk,
                                                  bias, None, op0=OP.add)
                    self.qkv_bank_tok[qc % 2] = self.s_vc.inc(ins)
                    last_tok = self.qkv_bank_tok[qc % 2]
            else:
                width = 128 if layer == 0 else 512
                fcol = lf * 128 if layer == 0 else 0
                mov = (lambda dt: W[:, dt, :]) if layer == 0 else (lambda dt: W[:, dt, 0:512])
                for tt in range(NTT):
                    bk = self.bank(2 + tt % 2)[:, 0:width]
                    if self.qkv_bank_tok[2 + tt % 2] is not None:
                        nc.tensor.wait_ge(self.s_vc.h, self.qkv_bank_tok[2 + tt % 2])
                    if gw_tok is not None and tt == 0:
                        nc.tensor.wait_ge(gw_tok[0].h, gw_tok[1])
                    for dt in range(NDT):
                        ins = nc.tensor.matmul(bk, XN[:, dt, tt * 128:(tt + 1) * 128], mov(dt),
                                               start=(dt == 0), stop=(dt == NDT - 1))
                    t = self.s_tp.inc(ins)
                    nc.vector.wait_ge(self.s_tp.h, t)
                    ins = nc.vector.tensor_copy(V[:, tt, fcol:fcol + width], bk)
                    self.qkv_bank_tok[2 + tt % 2] = self.s_vc.inc(ins)
                    last_tok = self.qkv_bank_tok[2 + tt % 2]
                if layer == 1:
                    break
        return last_tok

    def quant_l1(self, m, g, WQ, STG):
        nc, PS = self.nc, self.PS
        nc.vector.wait_ge(self.s_tp.h, self.s_tp.v)
        nc.vector.wait_ge(self.s_gr.h, self.s_gr.v)
        for ft in range(4):
            fr = (g * 4 + ft) * 128
            stgv = STG[:, :, :].rearrange("p a b -> p (a b)")
            nc.sync.wait_ge(self.s_vc.h, self.s_vc.v)
            ld = self.s_q[0].inc(nc.sync.dma_start(stgv, self.l1w[m, fr:fr + 128, :]), 16)
            nc.vector.wait_ge(self.s_q[0].h, ld)
            self.s_vc.inc(nc.vector.reduce_max(self.SCC[:, ft:ft + 1], stgv, axis=AX.X,
                                               apply_absolute_value=True))
        nc.vector.drain()
        nc.vector.tensor_scalar(self.SCC[:, 0:4], self.SCC[:, 0:4], 1.0 / 127.0, FQ_EPS,
                                op0=OP.mult, op1=OP.max)
        nc.vector.drain()
        v0 = self.s_vc.inc(nc.vector.reciprocal(self.SCC[:, 4:8], self.SCC[:, 0:4]))
        nc.tensor.wait_ge(self.s_vc.h, v0)
        t0 = self.s_tp.inc(nc.tensor.transpose(PS[0:8, 1024:1152], self.SCC[:, 0:8], self.EYE[:]))
        nc.scalar.wait_ge(self.s_tp.h, t0)
        s0 = self.s_sc.inc(nc.scalar.copy(self.SCT[:], PS[0:8, 1024:1152]))
        nc.sync.wait_ge(self.s_sc.h, s0)
        nc.sync.wait_ge(self.s_tp.h, self.s_tp.v)
        srow = self.s_gr.inc(nc.sync.dma_start(self.SM0[0:1, 1024:2048], self.SCT[0:8, :]), 16)
        nc.tensor.wait_ge(self.s_gr.h, srow)
        nc.tensor.matmul(self.bank(4), self.ONES1[0:1, 0:128],
                         self.SM0[0:1, 1024:1536], start=True, stop=True)
        ins = nc.tensor.matmul(self.bank(5), self.ONES1[0:1, 0:128],
                               self.SM0[0:1, 1536:2048], start=True, stop=True)
        t1 = self.s_tp.inc(ins)
        nc.vector.wait_ge(self.s_tp.h, t1)
        qtok = [None, None]
        v_base = self.s_vc.v
        for dt in range(NDT):
            sl = dt % 2
            nc.sync.wait_ge(self.s_vc.h, qtok[sl] if qtok[sl] is not None else v_base)
            nc.sync.wait_ge(self.s_q[sl].h, self.s_q[sl].v)
            ld = self.s_q[sl].inc(nc.sync.dma_start(STG[:, sl, 0:512],
                                                 self.l1wt[m, dt * 128:(dt + 1) * 128, g * 512:(g + 1) * 512]), 16)
            nc.vector.wait_ge(self.s_q[sl].h, ld)
            for hc in range(2):
                hs = slice(hc * 256, (hc + 1) * 256)
                nc.vector.tensor_tensor(self.T1[:, 0, hs], STG[:, dt % 2, hs], self.bank(5)[:, hs], op=OP.mult)
                nc.vector.drain()
                nc.vector.tensor_copy(self.QI[:], self.T1[:, 0, hs])
                nc.vector.drain()
                nc.vector.tensor_scalar(self.T1[:, 1, hs], self.QI[:], -127.0, 127.0, op0=OP.max, op1=OP.min)
                nc.vector.drain()
                ins = nc.vector.tensor_tensor(WQ[:, dt, hs], self.T1[:, 1, hs], self.bank(4)[:, hs], op=OP.mult)
            qtok[sl] = self.s_vc.inc(ins)
        nc.tensor.wait_ge(self.s_vc.h, self.s_vc.v)

    def emit_mhat(self, QT, KT, SQB):
        nc, PS, SM0 = self.nc, self.PS, self.SM0
        for si, SRC in ((0, QT), (1, KT)):
            for lt in range(4):
                vs = self.s_vc.inc(nc.vector.tensor_tensor(SQB[:], SRC[:, lt, :], SRC[:, lt, :], op=OP.mult))
                nc.tensor.wait_ge(self.s_vc.h, vs)
                for half in range(2):
                    lh = lt * 2 + half
                    pb = (lh % 2) * 4
                    if self.mh_tok[lh % 2] is not None:
                        nc.tensor.wait_ge(self.s_vc.h, self.mh_tok[lh % 2])
                    for qc in range(NQC):
                        ins = nc.tensor.matmul(PS[0:1, (pb + qc) * 512:(pb + qc + 1) * 512],
                                               self.ONESB[half * 64:half * 64 + 64, 0:1],
                                               SQB[half * 64:half * 64 + 64, qc * 512:(qc + 1) * 512],
                                               start=True, stop=True)
                    t = self.s_tp.inc(ins)
                    nc.vector.wait_ge(self.s_tp.h, t)
                    ins = nc.vector.reduce_max(SM0[0:1, si * 8 + lh:si * 8 + lh + 1],
                                               PS[0:1, pb * 512:(pb + 4) * 512], axis=AX.X)
                    self.mh_tok[lh % 2] = self.s_vc.inc(ins)
        nc.vector.drain()
        v1 = self.s_vc.inc(nc.vector.tensor_tensor(SM0[0:1, 16:24], SM0[0:1, 0:8], SM0[0:1, 8:16], op=OP.mult))
        nc.scalar.wait_ge(self.s_vc.h, v1)
        s1 = self.s_sc.inc(nc.scalar.activation(SM0[0:1, 24:32], SM0[0:1, 16:24], AF.Sqrt))
        nc.vector.wait_ge(self.s_sc.h, s1)
        nc.vector.tensor_scalar(SM0[0:1, 32:40], SM0[0:1, 24:32], -0.125, -1.0, op0=OP.mult, op1=OP.add)
        ev = SM0[0:1, 32:40].rearrange("p (a b) -> p a b", b=2)
        nc.vector.drain()
        v2 = self.s_vc.inc(nc.vector.tensor_tensor(SM0[0:1, 40:44], ev[:, :, 0], ev[:, :, 1], op=OP.min))
        nc.tensor.wait_ge(self.s_vc.h, v2)
        t2 = self.s_tp.inc(nc.tensor.matmul(PS[:, 1024:1028], self.ONES1[0:1, 0:128],
                                            SM0[0:1, 40:44], start=True, stop=True))
        nc.scalar.wait_ge(self.s_tp.h, t2)
        sc = self.s_sc.inc(nc.scalar.copy(self.NEGM8[:], PS[:, 1024:1028]))
        nc.tensor.wait_ge(self.s_sc.h, sc)

    def emit_attn_pair(self, layer, g, lt, qc, QT, KT, V, EB, O):
        nc, PS = self.nc, self.PS
        ha, hb = 2 * lt, 2 * lt + 1
        qsl = slice(qc * 512, (qc + 1) * 512)
        score_tok = [None] * 16
        exp_tok = [None] * 16
        av_tok = [None] * 16
        for kt in range(17):
            if kt < 16:
                r = kt % 2
                sview = PS[:, r * 1024:(r + 1) * 1024]
                if self.sb_tok[r] is not None:
                    nc.tensor.wait_ge(self.s_sc.h, self.sb_tok[r])
                ksl = slice(kt * 128, (kt + 1) * 128)
                nc.tensor.matmul(sview[:, 0:512], KT[0:64, lt, ksl], QT[0:64, lt, qsl], start=True, stop=True)
                ins = nc.tensor.matmul(sview[:, 512:1024], KT[64:128, lt, ksl], QT[64:128, lt, qsl],
                                       start=True, stop=True)
                score_tok[kt] = self.s_tp.inc(ins)
                nc.scalar.wait_ge(self.s_tp.h, score_tok[kt])
                if kt >= 2:
                    nc.scalar.wait_ge(self.s_tp.h, av_tok[kt - 2])
                ins = nc.scalar.activation(EB[:, kt % 2, :], sview, AF.Exp,
                                           bias=self.NEGM8[:, lt:lt + 1], scale=0.125)
                exp_tok[kt] = self.s_sc.inc(ins)
                self.sb_tok[r] = exp_tok[kt]
            if kt >= 1:
                j = kt - 1
                er = j % 2
                nc.tensor.wait_ge(self.s_sc.h, exp_tok[j])
                if j == 0 and self.av_free is not None:
                    nc.tensor.wait_ge(self.s_vc.h, self.av_free)
                st, sp = (j == 0), (j == 15)
                vsl = V[:, j, :]
                nc.tensor.matmul(self.bank(4)[0:64, :], vsl[:, ha * 64:ha * 64 + 64], EB[:, er, 0:512],
                                 start=st, stop=sp, tile_position=(0, 0))
                nc.tensor.matmul(self.bank(4)[64:128, :], vsl[:, hb * 64:hb * 64 + 64], EB[:, er, 512:1024],
                                 start=st, stop=sp, tile_position=(0, 64))
                nc.tensor.matmul(PS[0:1, 3072:3584], self.ONESB[:, 0:1], EB[:, er, 0:512],
                                 start=st, stop=sp, tile_position=(0, 0))
                ins = nc.tensor.matmul(PS[32:33, 3072:3584], self.ONESB[:, 0:1], EB[:, er, 512:1024],
                                       start=st, stop=sp, tile_position=(0, 32))
                av_tok[j] = self.s_tp.inc(ins)
        nc.scalar.wait_ge(self.s_tp.h, av_tok[15])
        nc.scalar.copy(self.SM0[0:1, 1024:1536], PS[0:1, 3072:3584])
        sden = self.s_sc.inc(nc.scalar.copy(self.SM0[32:33, 1536:2048], PS[32:33, 3072:3584]))
        nc.sync.wait_ge(self.s_sc.h, sden)
        self.s_gr.inc(nc.sync.dma_start(self.SCC[:, 0:4], self.SM0[0:1, 1024:1536]), 16)
        g1 = self.s_gr.inc(nc.sync.dma_start(self.SCC[:, 4:8], self.SM0[32:33, 1536:2048]), 16)
        nc.vector.wait_ge(self.s_gr.h, g1)
        vrc = self.s_vc.inc(nc.vector.reciprocal(self.SCC[:, 0:8], self.SCC[:, 0:8]))
        nc.sync.wait_ge(self.s_vc.h, vrc)
        self.s_gr.inc(nc.sync.dma_start(self.SM0[0:1, 1024:1536], self.SCC[:, 0:4]), 16)
        vr = self.s_gr.inc(nc.sync.dma_start(self.SM0[0:1, 1536:2048], self.SCC[:, 4:8]), 16)
        nc.tensor.wait_ge(self.s_gr.h, vr)
        nc.tensor.matmul(self.bank(7)[0:64, :], self.ONES1[0:1, 0:64], self.SM0[0:1, 1024:1536],
                         start=True, stop=True, tile_position=(0, 0))
        tb = self.s_tp.inc(nc.tensor.matmul(self.bank(7)[64:128, :], self.ONES1[0:1, 0:64],
                                            self.SM0[0:1, 1536:2048], start=True, stop=True,
                                            tile_position=(0, 64)))
        nc.vector.wait_ge(self.s_tp.h, tb)
        nc.vector.tensor_copy(self.DIBS[:], self.bank(7))
        nc.vector.drain()
        ins = nc.vector.tensor_tensor(O[:, g * 4 + lt, qsl], self.bank(4), self.DIBS[:], op=OP.mult)
        self.av_free = self.s_vc.inc(ins)
        self.o_tok[g * 4 + lt][qc] = self.av_free

    def p_outproj(self, layer):
        nc, PS = self.nc, self.PS
        O = self.pool_handles[f"O{layer}"]
        if layer == 0:
            WR = self.palloc("WOR", [128, 2, 8, 128], BF16)
        else:
            self.QI = self.palloc("QIp", [128, 256], I32)
            WQ = self.palloc("WPQ", [128, 8, 512], BF16)
            STG = self.palloc("STGP", [128, 2, 512], F32)
        bv16 = self.QKVBV if layer == 0 else self.L1BV
        slot_free = [None, None]
        for g in range(NG):
            if layer == 1:
                self.quant_l1(3, g, WQ, STG)
            for lf in range(4):
                ft = g * 4 + lf
                if layer == 0:
                    sl = ft % 2
                    if slot_free[sl] is not None:
                        nc.gpsimd.wait_ge(self.s_tp.h, slot_free[sl])
                    sem = self.s_wr[sl]
                    gw = (sem, sem.inc(nc.gpsimd.dma_start(WR[:, sl, :, :], self.wot[ft]), 16))
                    wsl = lambda dt: WR[:, sl, dt, :]
                else:
                    gw = None
                    wsl = lambda dt: WQ[:, dt, lf * 128:(lf + 1) * 128]
                if gw is not None:
                    nc.tensor.wait_ge(gw[0].h, gw[1])
                for dt in range(NDT):
                    ins = nc.tensor.matmul(PS[:, 1024 + ft:1024 + ft + 1], wsl(dt), bv16[:, dt:dt + 1],
                                           start=(dt == 0), stop=(dt == NDT - 1))
                tb = self.s_tp.inc(ins)
                nc.vector.wait_ge(self.s_tp.h, tb)
                base_b = self.WOBS[:, ft:ft + 1] if layer == 0 else self.L1BS[:, 24 + ft:25 + ft]
                vb = self.s_vc.inc(nc.vector.scalar_tensor_tensor(
                    self.WEB[:, ft:ft + 1], PS[:, 1024 + ft:1024 + ft + 1], 1.0, base_b,
                    op0=OP.mult, op1=OP.add))
                for qc in range(NQC):
                    r = qc % 2
                    if self.op_bank_tok[r] is not None:
                        nc.tensor.wait_ge(self.s_vc.h, self.op_bank_tok[r])
                    for dt in range(NDT):
                        nc.tensor.wait_ge(self.s_vc.h, self.o_tok[dt][qc] or 0)
                        ins = nc.tensor.matmul(self.bank(r), wsl(dt), O[:, dt, qc * 512:(qc + 1) * 512],
                                               start=(dt == 0), stop=(dt == NDT - 1))
                    t = self.s_tp.inc(ins)
                    nc.vector.wait_ge(self.s_tp.h, t)
                    nc.vector.wait_ge(self.s_vc.h, vb)
                    nc.vector.tensor_scalar(self.T1[:, r, :], self.bank(r), self.WEB[:, ft:ft + 1],
                                            None, op0=OP.add)
                    nc.vector.drain()
                    ins = nc.vector.tensor_tensor(self.X[:, ft, qc * 512:(qc + 1) * 512],
                                                  self.X[:, ft, qc * 512:(qc + 1) * 512],
                                                  self.T1[:, r, :], op=OP.add)
                    self.op_bank_tok[r] = self.s_vc.inc(ins)
                slot_free_tok = self.s_tp.v
                if layer == 0:
                    slot_free[ft % 2] = slot_free_tok
        self.pfree()

    def p_router(self):
        nc, PS = self.nc, self.PS
        self.LGT = self.palloc("LGT", [128, 16, 8], F32)
        for tt in range(NTT):
            if tt >= 1:
                nc.tensor.wait_ge(self.s_sc.h, self.rt_tok[0])
            for dt in range(NDT):
                ins = nc.tensor.matmul(PS[:, 0:8], self.XN[:, dt, tt * 128:(tt + 1) * 128],
                                       self.RW[:, dt, :], start=(dt == 0), stop=(dt == NDT - 1))
            t = self.s_tp.inc(ins)
            nc.scalar.wait_ge(self.s_tp.h, t)
            self.rt_tok[0] = self.s_sc.inc(nc.scalar.copy(self.LGT[:, tt, :], PS[:, 0:8]))
        for tt in range(NTT):
            L = self.LGT[:, tt, :]
            nc.vector.wait_ge(self.s_sc.h, self.rt_tok[0])
            v = self.s_vc.inc(nc.vector.reduce_max(self.SMLL[:, 0:1], L, axis=AX.X, negate=True))
            nc.scalar.wait_ge(self.s_vc.h, v)
            s = self.s_sc.inc(nc.scalar.activation(L, L, AF.Exp, bias=self.SMLL[:, 0:1],
                                                   accum_out=self.SMLL[:, 1:2]))
            nc.vector.wait_ge(self.s_sc.h, s)
            nc.vector.reciprocal(self.SMLL[:, 1:2], self.SMLL[:, 1:2])
            nc.vector.drain()
            nc.vector.tensor_scalar(L, L, self.SMLL[:, 1:2], None, op0=OP.mult)
            nc.vector.drain()
            nc.vector.reduce_max(self.SMLL[:, 0:1], L, axis=AX.X)
            nc.vector.drain()
            nc.vector.tensor_scalar(self.T1[:, 0, 0:8], L, self.SMLL[:, 0:1], None, op0=OP.is_ge)
            nc.vector.drain()
            nc.vector.scalar_tensor_tensor(self.T1[:, 1, 0:8], self.T1[:, 0, 0:8], -BIG, L,
                                           op0=OP.mult, op1=OP.add)
            nc.vector.drain()
            nc.vector.reduce_max(self.SMLL[:, 2:3], self.T1[:, 1, 0:8], axis=AX.X)
            nc.vector.drain()
            nc.vector.tensor_scalar(self.T1[:, 0, 0:8], L, self.SMLL[:, 2:3], None, op0=OP.is_ge)
            nc.vector.drain()
            vg = self.s_vc.inc(nc.vector.tensor_tensor(L, L, self.T1[:, 0, 0:8], op=OP.mult))
            nc.tensor.wait_ge(self.s_vc.h, vg)
            tt2 = self.s_tp.inc(nc.tensor.transpose(PS[0:8, 512:640], L, self.EYE[:]))
            nc.vector.wait_ge(self.s_tp.h, tt2)
            self.s_vc.inc(nc.vector.tensor_copy(self.SM0[0:8, tt * 128:(tt + 1) * 128], PS[0:8, 512:640]))
        self.pfree(1)

    def p_moe(self):
        nc, PS, XN, X = self.nc, self.PS, self.XN, self.X
        HG = self.palloc("HG", [128, 32, 1024], BF16)
        W1R = self.palloc("W1R", [128, 3, 8, 256], BF16)
        W2R = self.palloc("W2R", [128, 4, 2, 1024], BF16)
        self.GB = self.palloc("GBm", [128, 1024], F32)
        w1_free = [None] * 3
        w2_free = [None] * 4
        bank_free = [None] * 8
        gb_tok = None
        gelu_tok = [None] * 32
        for c in range(2):
            csl = slice(c * 1024, (c + 1) * 1024)
            for e in range(MOE_E):
                if gb_tok is not None:
                    nc.sync.wait_ge(self.s_vc.h, gb_tok)
                grow = self.s_gr.inc(nc.sync.dma_start(
                    self.GB[0:1, 0:1024], self.SM0[e:e + 1, c * 1024:(c + 1) * 1024]), 16)
                for b in (4, 5):
                    if bank_free[b] is not None:
                        nc.tensor.wait_ge(self.s_vc.h, bank_free[b])
                nc.tensor.wait_ge(self.s_gr.h, grow)
                ins = None
                for half in range(2):
                    ins = nc.tensor.matmul(self.bank(4 + half), self.ONES1[0:1, 0:128],
                                           self.GB[0:1, half * 512:(half + 1) * 512],
                                           start=True, stop=True)
                tgb = self.s_tp.inc(ins)
                nc.scalar.wait_ge(self.s_tp.h, tgb)
                if gb_tok is not None:
                    nc.scalar.wait_ge(self.s_vc.h, gb_tok)
                sgb = self.s_sc.inc(nc.scalar.copy(self.GB[:], self.bank(4, 2)))
                for ht in range(32):
                    sl = (ht // 2) % 3
                    if ht % 2 == 0:
                        if w1_free[sl] is not None:
                            nc.gpsimd.wait_ge(self.s_tp.h, w1_free[sl])
                        self.w1_tok[sl] = self.s_w1[sl].inc(
                            nc.gpsimd.dma_start(W1R[:, sl, :, :], self.w1t[e, ht // 2]), 16)
                    pb = (ht % 2) * 2
                    if ht >= 2 and gelu_tok[ht - 2] is not None:
                        nc.tensor.wait_ge(self.s_sc.h, gelu_tok[ht - 2])
                    for b in (pb, pb + 1):
                        if bank_free[b] is not None:
                            nc.tensor.wait_ge(self.s_vc.h, bank_free[b])
                            bank_free[b] = None
                    nc.tensor.wait_ge(self.s_w1[sl].h, self.w1_tok[sl])
                    for nh in range(2):
                        for dt in range(NDT):
                            ins = nc.tensor.matmul(self.bank(pb + nh),
                                                   W1R[:, sl, dt, (ht % 2) * 128:(ht % 2) * 128 + 128],
                                                   XN[:, dt, c * 1024 + nh * 512: c * 1024 + (nh + 1) * 512],
                                                   start=(dt == 0), stop=(dt == NDT - 1))
                    t = self.s_tp.inc(ins)
                    if ht % 2 == 1:
                        w1_free[sl] = t
                    nc.scalar.wait_ge(self.s_tp.h, t)
                    ins = nc.scalar.activation(HG[:, ht, :], self.bank(pb, 2), AF.Gelu)
                    gelu_tok[ht] = self.s_sc.inc(ins)
                hg_done = gelu_tok[31]
                for nh in range(2):
                    for ht in range(32):
                        sl2 = (ht // 2) % 4
                        if ht % 2 == 0:
                            if w2_free[sl2] is not None:
                                nc.gpsimd.wait_ge(self.s_tp.h, w2_free[sl2])
                            hq = ht // 4
                            self.w2_tok[sl2] = self.s_w2[sl2].inc(
                                nc.gpsimd.dma_start(W2R[:, sl2, :, :],
                                                    self.w2t[e, hq, :, (ht % 4) // 2 * 2:(ht % 4) // 2 * 2 + 2, :]), 16)
                        if ht == 0:
                            nc.tensor.wait_ge(self.s_sc.h, hg_done)
                            nc.tensor.wait_ge(self.s_sc.h, sgb)
                            for b in range(8):
                                if bank_free[b] is not None:
                                    nc.tensor.wait_ge(self.s_vc.h, bank_free[b])
                                    bank_free[b] = None
                            nc.tensor.wait_ge(self.s_w2[sl2].h, self.w2_tok[sl2])
                        elif ht % 2 == 0:
                            nc.tensor.wait_ge(self.s_w2[sl2].h, self.w2_tok[sl2])
                        for dt in range(NDT):
                            ins = nc.tensor.matmul(self.bank(dt), W2R[:, sl2, ht % 2, dt * 128:(dt + 1) * 128],
                                                   HG[:, ht, nh * 512:(nh + 1) * 512],
                                                   start=(ht == 0), stop=(ht == 31))
                            if ht == 31:
                                tok = self.s_tp.inc(ins)
                                nc.vector.wait_ge(self.s_tp.h, tok)
                                nc.vector.tensor_tensor(self.T1[:, dt % 2, :], self.bank(dt),
                                                        self.GB[:, nh * 512:(nh + 1) * 512], op=OP.mult)
                                nc.vector.drain()
                                xsl = slice(c * 1024 + nh * 512, c * 1024 + (nh + 1) * 512)
                                ins2 = nc.vector.tensor_tensor(X[:, dt, xsl], X[:, dt, xsl],
                                                               self.T1[:, dt % 2, :], op=OP.add)
                                bank_free[dt] = self.s_vc.inc(ins2)
                        if ht == 31:
                            w2_free[sl2] = self.s_tp.v
                        elif ht % 2 == 1:
                            w2_free[sl2] = self.s_tp.inc(ins)
                gb_tok = bank_free[7]
        self.pfree()

    def p_ffn1(self):
        nc, PS, XN, X = self.nc, self.PS, self.XN, self.X
        WQ1 = self.palloc("WQ1", [128, 8, 4096], FP8)
        WQ2 = self.palloc("WQ2", [128, 32, 1024], FP8)
        self.RBS = self.palloc("RBSf", [128, 256], F32)
        self.MBS = self.palloc("MBSf", [128, 256], F32)
        LNC = self.palloc("LNC", [128, 32, 2], BF16)
        G = self.palloc("G", [128, 32, 256], BF16)
        SQGF = self.palloc("SQGF", [128, 4, 256], BF16)
        QSC = self.palloc("QSC", [128, 2, 512], F32)
        STGF = self.palloc("STGF", [128, 2, 1024], F32)
        nc.vector.tensor_copy(LNC[:, :, 0], self.LNBB16[:])
        nc.vector.tensor_copy(LNC[:, :, 1], self.LNWB16[:])
        slot_tok = [None, None]
        self.wq1_co = {}
        for wi, (src, ntile, width, WQ) in ((1, (self.wprt, 32, 1024, WQ2)),
                                            (0, (self.wfct, 8, 4096, WQ1))):
            nc.vector.memset(self.SMLL[:, 3:4], 0.0)
            wu = min(1024, width)
            units = [(it, co) for co in range(0, width, wu) for it in range(ntile)]
            nhalf = len(units)
            for ih, (it, co) in enumerate(units):
                sl = ih % 2
                if slot_tok[sl] is not None:
                    nc.sync.wait_ge(self.s_vc.h, slot_tok[sl])
                ld = self.s_q[sl].inc(nc.sync.dma_start(
                    STGF[:, sl, 0:wu], src[it * 128:it * 128 + 128, co:co + wu]), 16)
                nc.vector.wait_ge(self.s_q[sl].h, ld)
                nc.vector.reduce_sum(self.SMLL[:, 2:3], STGF[:, sl, 0:wu], axis=AX.X,
                                     apply_absolute_value=True)
                nc.vector.drain()
                v = nc.vector.tensor_tensor(self.SMLL[:, 3:4], self.SMLL[:, 3:4],
                                            self.SMLL[:, 2:3], op=OP.add)
                slot_tok[sl] = self.s_vc.inc(v)
                nc.vector.drain()
            vsum = slot_tok[(nhalf - 1) % 2]
            nc.tensor.wait_ge(self.s_vc.h, vsum)
            tg = self.s_tp.inc(nc.tensor.matmul(PS[0:1, 1024:1025], self.ONE2F[:, 0:1],
                                                self.SMLL[:, 3:4], start=True, stop=True))
            nc.vector.wait_ge(self.s_tp.h, tg)
            nc.vector.tensor_scalar(self.SM0[0:1, 48:49], PS[0:1, 1024:1025], 1.0 / (4096.0 * 1024.0),
                                    AM_EPS, op0=OP.mult, op1=OP.add)
            nc.vector.drain()
            vg = self.s_vc.inc(nc.vector.reciprocal(self.SM0[0:1, 48:49], self.SM0[0:1, 48:49]))
            nc.tensor.wait_ge(self.s_vc.h, vg)
            tb = self.s_tp.inc(nc.tensor.matmul(PS[:, 1024:1025], self.ONES1[0:1, 0:128],
                                                self.SM0[0:1, 48:49], start=True, stop=True))
            nc.scalar.wait_ge(self.s_tp.h, tb)
            sg = self.s_sc.inc(nc.scalar.copy(self.INVGC[:, wi:wi + 1], PS[:, 1024:1025]))
            nc.vector.wait_ge(self.s_sc.h, sg)
            for ih, (it, co) in enumerate(units):
                sl = ih % 2
                if slot_tok[sl] is not None:
                    nc.sync.wait_ge(self.s_vc.h, slot_tok[sl])
                ld = self.s_q[sl].inc(nc.sync.dma_start(
                    STGF[:, sl, 0:wu], src[it * 128:it * 128 + 128, co:co + wu]), 16)
                nc.vector.wait_ge(self.s_q[sl].h, ld)
                for ch in range(wu // 512):
                    cs = slice(ch * 512, (ch + 1) * 512)
                    nc.vector.drain()
                    nc.vector.tensor_scalar(QSC[:, 0, :], STGF[:, sl, cs], self.INVGC[:, wi:wi + 1], 0.5,
                                            op0=OP.mult, op1=OP.is_gt)
                    nc.vector.tensor_scalar(QSC[:, 1, :], STGF[:, sl, cs], self.INVGC[:, wi:wi + 1], -0.5,
                                            op0=OP.mult, op1=OP.is_lt)
                    nc.vector.drain()
                    col = co + ch * 512
                    v = nc.vector.tensor_tensor(WQ[:, it, col:col + 512], QSC[:, 0, :], QSC[:, 1, :],
                                                op=OP.subtract)
                    slot_tok[sl] = self.s_vc.inc(v)
                if ih == nhalf - 1 or (wi == 0 and ih == nhalf // 2 - 1):
                    self.wq1_co[(wi, ih)] = self.s_vc.v
            if wi == 1:
                wq2_done = self.s_vc.v
        vq_done = self.s_vc.v
        wq1_half = self.wq1_co.get((0, len(units) // 2 - 1), vq_done)
        nc.tensor.wait_ge(self.s_vc.h, wq2_done)
        for dt in range(NDT):
            for ht in range(32):
                ins = nc.tensor.matmul(PS[:, 1536 + 2 * dt:1538 + 2 * dt],
                                       WQ2[:, ht, dt * 128:(dt + 1) * 128], LNC[:, ht, :],
                                       start=(ht == 0), stop=(ht == 31))
        tpb = self.s_tp.inc(ins)
        nc.vector.wait_ge(self.s_tp.h, tpb)
        vpb = self.s_vc.inc(nc.vector.tensor_copy(self.PBIA[:], PS[:, 1536:1552]))
        self.pfree(2)
        W = 256
        gelu_tok = [None] * 32
        sq_tok = [None] * 32
        fold_tok = [None] * 32
        bank_free = [None] * 8
        for c in range(T // W):
            csl = slice(c * W, (c + 1) * W)
            stat_free = [None] * 4
            if c == 0:
                nc.tensor.wait_ge(self.s_vc.h, wq1_half)
            def emit_stats(ht):
                nc.tensor.wait_ge(self.s_vc.h, sq_tok[ht])
                if ht == 0:
                    for b in (4, 5):
                        if bank_free[b] is not None:
                            nc.tensor.wait_ge(self.s_vc.h, bank_free[b])
                            bank_free[b] = None
                nc.tensor.matmul(PS[0:1, 2048:2048 + W], self.ONESB[:, 0:1], G[:, ht, :],
                                 start=(ht == 0), stop=(ht == 31))
                ins = nc.tensor.matmul(PS[0:1, 2560:2560 + W], self.ONESB[:, 0:1], SQGF[:, ht % 4, :],
                                       start=(ht == 0), stop=(ht == 31))
                st2 = self.s_tp.inc(ins)
                stat_free[ht % 4] = st2
                nc.vector.wait_ge(self.s_tp.h, st2)
                fold_tok[ht] = self.s_vc.inc(nc.vector.tensor_scalar(
                    G[:, ht, :], G[:, ht, :], self.LNW[:, ht:ht + 1], None, op0=OP.mult))
            for ht in range(32):
                pb = ht % 2
                if ht >= 2 and gelu_tok[ht - 2] is not None:
                    nc.tensor.wait_ge(self.s_sc.h, gelu_tok[ht - 2])
                if bank_free[pb] is not None:
                    nc.tensor.wait_ge(self.s_vc.h, bank_free[pb])
                    bank_free[pb] = None
                if c == 0 and ht == 16:
                    nc.tensor.wait_ge(self.s_vc.h, vq_done)
                for dt in range(NDT):
                    ins = nc.tensor.matmul(self.bank(pb)[:, 0:W], WQ1[:, dt, ht * 128:(ht + 1) * 128],
                                           XN[:, dt, csl], start=(dt == 0), stop=(dt == NDT - 1))
                t = self.s_tp.inc(ins)
                nc.scalar.wait_ge(self.s_tp.h, t)
                gelu_tok[ht] = self.s_sc.inc(nc.scalar.activation(G[:, ht, :], self.bank(pb)[:, 0:W], AF.Gelu))
                nc.vector.wait_ge(self.s_sc.h, gelu_tok[ht])
                if stat_free[ht % 4] is not None:
                    nc.vector.wait_ge(self.s_tp.h, stat_free[ht % 4])
                sq_tok[ht] = self.s_vc.inc(nc.vector.tensor_tensor(
                    SQGF[:, ht % 4, :], G[:, ht, :], G[:, ht, :], op=OP.mult))
                if ht >= 2:
                    emit_stats(ht - 2)
            emit_stats(30)
            emit_stats(31)
            nc.vector.tensor_scalar(self.SM0[0:1, 0:W], PS[0:1, 2048:2048 + W], 1.0 / 4096.0, None, op0=OP.mult)
            nc.vector.tensor_scalar(self.SM0[0:1, 512:512 + W], PS[0:1, 2560:2560 + W], 1.0 / 4096.0,
                                    None, op0=OP.mult)
            nc.vector.drain()
            nc.vector.tensor_tensor(PS[0:1, 2048:2048 + W], self.SM0[0:1, 0:W], self.SM0[0:1, 0:W], op=OP.mult)
            nc.vector.drain()
            vv = self.s_vc.inc(nc.vector.tensor_tensor(self.SM0[0:1, 512:512 + W], self.SM0[0:1, 512:512 + W],
                                                       PS[0:1, 2048:2048 + W], op=OP.subtract))
            nc.scalar.wait_ge(self.s_vc.h, vv)
            ss = self.s_sc.inc(nc.scalar.activation(self.SM0[0:1, 512:512 + W], self.SM0[0:1, 512:512 + W],
                                                    AF.Sqrt, bias=self.EPSC[0:1, 0:1]))
            nc.vector.wait_ge(self.s_sc.h, ss)
            nc.vector.reciprocal(self.SM0[0:1, 512:512 + W], self.SM0[0:1, 512:512 + W])
            nc.vector.drain()
            vr = self.s_vc.inc(nc.vector.scalar_tensor_tensor(self.SM0[0:1, 1024:1024 + W], self.SM0[0:1, 0:W],
                                                              -1.0, self.SM0[0:1, 512:512 + W],
                                                              op0=OP.mult, op1=OP.mult))
            nc.tensor.wait_ge(self.s_vc.h, vr)
            for b in (6, 7):
                if bank_free[b] is not None:
                    nc.tensor.wait_ge(self.s_vc.h, bank_free[b])
                    bank_free[b] = None
            nc.tensor.matmul(self.bank(6)[:, 0:W], self.ONES1[0:1, 0:128], self.SM0[0:1, 512:512 + W],
                             start=True, stop=True)
            tb2 = self.s_tp.inc(nc.tensor.matmul(self.bank(7)[:, 0:W], self.ONES1[0:1, 0:128],
                                                 self.SM0[0:1, 1024:1024 + W], start=True, stop=True))
            nc.scalar.wait_ge(self.s_tp.h, tb2)
            nc.scalar.copy(self.RBS[:, 0:W], self.bank(6)[:, 0:W])
            srb = self.s_sc.inc(nc.scalar.copy(self.MBS[:, 0:W], self.bank(7)[:, 0:W]))
            for ht in range(32):
                if ht == 0:
                    nc.tensor.wait_ge(self.s_vc.h, fold_tok[31])
                    nc.tensor.wait_ge(self.s_vc.h, vpb)
                    nc.tensor.wait_ge(self.s_sc.h, srb)
                    for b in range(8):
                        if bank_free[b] is not None:
                            nc.tensor.wait_ge(self.s_vc.h, bank_free[b])
                            bank_free[b] = None
                for dt in range(NDT):
                    ins = nc.tensor.matmul(self.bank(dt)[:, 0:W], WQ2[:, ht, dt * 128:(dt + 1) * 128],
                                           G[:, ht, :], start=(ht == 0), stop=(ht == 31))
                    if ht == 31:
                        tok = self.s_tp.inc(ins)
                        nc.vector.wait_ge(self.s_tp.h, tok)
                        nc.vector.tensor_tensor(self.T1[:, dt % 2, 0:W], self.bank(dt)[:, 0:W],
                                                self.RBS[:, 0:W], op=OP.mult)
                        nc.vector.drain()
                        nc.vector.scalar_tensor_tensor(self.T1[:, dt % 2, 256:256 + W], self.MBS[:, 0:W],
                                                       self.PBIA[:, 2 * dt + 1:2 * dt + 2], self.T1[:, dt % 2, 0:W],
                                                       op0=OP.mult, op1=OP.add)
                        nc.vector.drain()
                        nc.vector.tensor_scalar(self.T1[:, dt % 2, 0:W], self.T1[:, dt % 2, 256:256 + W],
                                                self.PBIA[:, 2 * dt:2 * dt + 1], None, op0=OP.add)
                        nc.vector.drain()
                        ins2 = nc.vector.tensor_tensor(X[:, dt, csl], X[:, dt, csl],
                                                       self.T1[:, dt % 2, 0:W], op=OP.add)
                        bank_free[dt] = self.s_vc.inc(ins2)
            self.mm2_tok = self.s_tp.v
        self.pfree()

    def p_store(self):
        nc = self.nc
        for se in ([self.s_tp, self.s_sc, self.s_vc, self.s_gw, self.s_ld, self.s_gr]
                   + self.s_wr + self.s_w1 + self.s_w2 + self.s_q):
            nc.sync.wait_ge(se.h, se.v)
            nc.gpsimd.wait_ge(se.h, se.v)
        if self.dump is None:
            for dt in range(NDT):
                self.s_st.inc(nc.sync.dma_start(self.outT[dt * 128:(dt + 1) * 128, :], self.X[:, dt, :]), 16)
            nc.sync.wait_ge(self.s_st.h, self.s_st.v)
        elif self.dump == "QK":
            for i, src in enumerate((self.pool_handles[f"QT{self.dump_layer}"], self.pool_handles[f"KT{self.dump_layer}"])):
                for lt in range(4):
                    self.s_gw.inc(nc.gpsimd.dma_start(
                        self.outT[(i * 4 + lt) * 128:(i * 4 + lt + 1) * 128, :], src[:, lt, :]), 16)
            nc.gpsimd.wait_ge(self.s_gw.h, self.s_gw.v)
            nc.sync.wait_ge(self.s_gw.h, self.s_gw.v)
        elif self.dump == "V":
            V = self.pool_handles[f"V{self.dump_layer}"]
            for i in range(4):
                for j in range(4):
                    self.s_gw.inc(nc.gpsimd.dma_start(
                        self.outT[i * 128:(i + 1) * 128, j * 512:(j + 1) * 512], V[:, i * 4 + j, :]), 16)
            nc.gpsimd.wait_ge(self.s_gw.h, self.s_gw.v)
            nc.sync.wait_ge(self.s_gw.h, self.s_gw.v)
        else:
            src = {"XN": self.XN}.get(self.dump) if self.dump in ("XN",) else self.pool_handles[self.dump]
            for dt in range(NDT):
                self.s_gw.inc(nc.gpsimd.dma_start(self.outT[dt * 128:(dt + 1) * 128, :], src[:, dt, :]), 16)
            nc.gpsimd.wait_ge(self.s_gw.h, self.s_gw.v)
            nc.sync.wait_ge(self.s_gw.h, self.s_gw.v)


def build_program(phases=None, dump=None):
    mk = MK(phases, dump)
    mk.qkv_bank_tok = [None] * 4
    mk.mh_tok = [None, None]
    mk.sb_tok = [None, None]
    mk.av_free = None
    mk.op_bank_tok = [None, None]
    mk.rt_tok = [None]
    mk.w1_tok = [None] * 3
    mk.w2_tok = [None] * 4
    mk.pool_handles = {}
    mk.mm2_tok = None
    _orig = mk.palloc

    def palloc2(name, shape, dtype):
        h = _orig(name, shape, dtype)
        mk.pool_handles[name] = h
        return h
    mk.palloc = palloc2
    nc = mk.build()
    return nc


def host_inputs(b_x, ins):
    d = {}
    d["xT"] = np.ascontiguousarray(b_x.T)
    qkv = ins["L0_attn_qkv_w"]
    d["qkvt"] = np.ascontiguousarray(qkv.reshape(24, 128, 8, 128).transpose(0, 3, 2, 1))
    d["qkvb"] = np.ascontiguousarray(ins["L0_attn_qkv_b"].reshape(24, 128).T)
    wo = ins["L0_attn_out_w"]
    d["wot"] = np.ascontiguousarray(wo.reshape(8, 128, 8, 128).transpose(0, 3, 2, 1))
    d["wob"] = np.ascontiguousarray(ins["L0_attn_out_b"].reshape(8, 128).T)
    d["rwt"] = np.ascontiguousarray(ins["L0_ffn_router_w"].T.reshape(8, 128, 8).transpose(1, 0, 2))
    w1 = ins["L0_ffn_w1"]
    d["w1t"] = np.ascontiguousarray(w1.transpose(0, 2, 1).reshape(8, 8, 128, 16, 256).transpose(0, 3, 2, 1, 4))
    w2 = ins["L0_ffn_w2"]
    d["w2t"] = np.ascontiguousarray(w2.transpose(0, 2, 1).reshape(8, 8, 4, 128, 1024).transpose(0, 1, 3, 2, 4))
    l1 = [ins["L1_attn_wq"], ins["L1_attn_wk"], ins["L1_attn_wv"], ins["L1_attn_wp"]]
    d["l1w"] = np.ascontiguousarray(np.stack(l1))
    d["l1wt"] = np.ascontiguousarray(np.stack([w.T for w in l1]))
    bb = [ins["L1_attn_bq"], ins["L1_attn_bk"], ins["L1_attn_bv"], ins["L1_attn_bp"]]
    d["l1b"] = np.ascontiguousarray(np.concatenate([b.reshape(8, 128).T for b in bb], axis=1))
    d["wfct"] = np.ascontiguousarray(ins["L1_ffn_W_fc"].T)
    d["lnwb"] = np.ascontiguousarray(np.concatenate(
        [ins["L1_ffn_ln_w"].reshape(32, 128).T, ins["L1_ffn_ln_b"].reshape(32, 128).T], axis=1))
    d["wprt"] = np.ascontiguousarray(ins["L1_ffn_W_proj"].T)
    d["eye"] = np.eye(128, dtype=np.float32)
    return {k: v.astype(np.float32, copy=False) for k, v in d.items()}


def kernel(**inputs):
    x = inputs["L0_attn_x"]
    nc = build_program()
    in_maps = [host_inputs(x[b], inputs) for b in range(8)]
    res = run_bass_kernel_spmd(nc, in_maps, core_ids=list(range(8)))
    out = np.stack([np.ascontiguousarray(r["outT"].T) for r in res.results])
    return out.astype(np.float32)


# version 17
# speedup vs baseline: 1.0182x; 1.0138x over previous
import numpy as np
from contextlib import ExitStack

import concourse.bass as bass
import concourse.mybir as mybir
from concourse.bass_utils import run_bass_kernel_spmd

F32 = mybir.dt.float32
BF16 = mybir.dt.bfloat16
FP8 = mybir.dt.float8e4
I32 = mybir.dt.int32
AF = mybir.ActivationFunctionType
OP = mybir.AluOpType
AX = mybir.AxisListType

D = 1024
T = 2048
NDT = 8
NTT = 16
NQC = 4
H = 16
DH = 64
NG = 2
MOE_E = 8
LN_EPS = 1e-5
FQ_EPS = float(np.finfo(np.float32).eps)
AM_EPS = 1e-6
BIG = 1e30

ALL_PHASES = ["ln1", "attn0", "proj0", "ln2", "router", "moe",
              "ln3", "attn1", "proj1", "ln4", "ffn1"]


class Sem:
    def __init__(self, nc, stk, name):
        self.h = stk.enter_context(nc.semaphore(name))
        self.v = 0

    def inc(self, instr, n=1):
        instr.then_inc(self.h, n)
        self.v += n
        return self.v


class MK:
    def __init__(self, phases=None, dump=None):
        self.dump = dump
        self.dump_layer = 0
        self.phases = ALL_PHASES if phases is None else phases
        self.nc = bass.Bass("TRN2", target_bir_lowering=False)
        self.stk = ExitStack()
        self.pool = []

    def sb(self, name, shape, dtype):
        return self.stk.enter_context(self.nc.sbuf_tensor(name, shape, dtype))

    def palloc(self, name, shape, dtype):
        cm = self.nc.sbuf_tensor(name, shape, dtype)
        h = cm.__enter__()
        self.pool.append(cm)
        return h

    def pfree(self, n=None):
        k = len(self.pool) if n is None else n
        for _ in range(k):
            self.pool.pop().__exit__(None, None, None)

    def barrier(self):
        nc, s = self.nc, self.s_bar
        for eng in (nc.tensor, nc.vector, nc.scalar, nc.sync):
            eng.drain()
            eng.sem_inc(s.h, 1)
            s.v += 1
        nc.gpsimd.sem_inc(s.h, 1)
        s.v += 1
        for eng in (nc.tensor, nc.vector, nc.scalar, nc.gpsimd, nc.sync):
            eng.wait_ge(s.h, s.v)

    def bank(self, i, n=1):
        return self.PS[:, 512 * i:512 * (i + n)]

    def build(self):
        nc = self.nc
        s = self.stk
        di = lambda name, shape: nc.dram_tensor(name, shape, F32, kind="ExternalInput").ap()
        self.xT = di("xT", [D, T])
        self.qkvt = di("qkvt", [24, 128, 8, 128])
        self.qkvb = di("qkvb", [128, 24])
        self.wot = di("wot", [8, 128, 8, 128])
        self.wob = di("wob", [128, 8])
        self.rwt = di("rwt", [128, 8, 8])
        self.w1t = di("w1t", [MOE_E, 16, 128, 8, 256])
        self.w2t = di("w2t", [MOE_E, 8, 128, 4, 1024])
        self.l1w = di("l1w", [4, D, D])
        self.l1wt = di("l1wt", [4, D, D])
        self.l1b = di("l1b", [128, 32])
        self.wfct = di("wfct", [D, 4096])
        self.lnwb = di("lnwb", [128, 64])
        self.wprt = di("wprt", [4096, D])
        self.eye_in = di("eye", [128, 128])
        self.outT = nc.dram_tensor("outT", [D, T], F32, kind="ExternalOutput").ap()

        self.X = self.sb("X", [128, NDT, T], F32)
        self.XN = self.sb("XN", [128, NDT, T], BF16)
        self.SM0 = self.sb("SM0", [128, 2048], F32)
        self.ONES1 = self.sb("ONES1", [128, 128], F32)
        self.ONESB = self.sb("ONESB", [128, 2], BF16)
        self.ONE2F = self.sb("ONE2F", [128, 2], F32)
        self.EYE = self.sb("EYE", [128, 128], F32)
        self.QKVB = self.sb("QKVB", [128, 24], F32)
        self.QKVBV = self.sb("QKVBV", [128, 8], BF16)
        self.WOBS = self.sb("WOBS", [128, 8], F32)
        self.L1BS = self.sb("L1BS", [128, 32], F32)
        self.L1BV = self.sb("L1BV", [128, 8], BF16)
        self.LNW = self.sb("LNW", [128, 32], F32)
        self.LNWB16 = self.sb("LNWB16", [128, 32], BF16)
        self.LNBB16 = self.sb("LNBB16", [128, 32], BF16)
        self.NEGM8 = self.sb("NEGM8", [128, 4], F32)
        self.RW = self.sb("RW", [128, 8, 8], BF16)
        self.SCT = self.sb("SCT", [8, 128], F32)
        self.SCC = self.sb("SCC", [128, 8], F32)
        self.SMLL = self.sb("SMLL", [128, 4], F32)
        self.WEB = self.sb("WEB", [128, 8], F32)
        self.INVGC = self.sb("INVGC", [128, 2], F32)
        self.PBIA = self.sb("PBIA", [128, 16], F32)
        self.T1 = self.sb("T1", [128, 2, 512], F32)
        self.EPSC = self.sb("EPSC", [128, 1], F32)
        self.SQG = None
        self.PS = s.enter_context(nc.psum_tensor("PS", [128, 4096], F32))

        self.s_ld = Sem(nc, s, "s_ld")
        self.s_gw = Sem(nc, s, "s_gw")
        self.s_st = Sem(nc, s, "s_st")
        self.s_tp = Sem(nc, s, "s_tp")
        self.s_sc = Sem(nc, s, "s_sc")
        self.s_vc = Sem(nc, s, "s_vc")
        self.s_bar = Sem(nc, s, "s_bar")
        self.s_wr = [Sem(nc, s, f"s_wr{i}") for i in range(2)]
        self.s_w1 = [Sem(nc, s, f"s_w1{i}") for i in range(3)]
        self.s_w2 = [Sem(nc, s, f"s_w2{i}") for i in range(4)]
        self.s_q = [Sem(nc, s, f"s_q{i}") for i in range(2)]
        self.s_gr = Sem(nc, s, "s_gr")

        ph = self.phases
        self.p_load()
        self.barrier()
        steps = [("ln1", self.p_ln, ()), ("attn0", self.p_attn, (0,)),
                 ("proj0", self.p_outproj, (0,)), ("ln2", self.p_ln, ()),
                 ("router", self.p_router, ()), ("moe", self.p_moe, ()),
                 ("ln3", self.p_ln, ()), ("attn1", self.p_attn, (1,)),
                 ("proj1", self.p_outproj, (1,)), ("ln4", self.p_ln, ()),
                 ("ffn1", self.p_ffn1, ())]
        for name, fn, args in steps:
            if name in ph:
                fn(*args)
                self.barrier()
        self.p_store()
        self.pfree()
        self.stk.close()
        return self.nc

    def p_load(self):
        nc = self.nc
        for dt in range(NDT):
            self.s_ld.inc(nc.sync.dma_start(self.X[:, dt, :], self.xT[dt * 128:(dt + 1) * 128, :]), 16)
        for dst, src in ((self.QKVB, self.qkvb[:]), (self.WOBS, self.wob[:]),
                         (self.L1BS, self.l1b[:]), (self.LNW, self.lnwb[:, 0:32]),
                         (self.EYE, self.eye_in[:])):
            self.s_ld.inc(nc.sync.dma_start(dst[:], src), 16)
        self.s_gw.inc(nc.gpsimd.dma_start(self.RW[:], self.rwt[:]), 16)
        self.s_gw.inc(nc.gpsimd.dma_start(self.LNWB16[:], self.lnwb[:, 0:32]), 16)
        self.s_gw.inc(nc.gpsimd.dma_start(self.LNBB16[:], self.lnwb[:, 32:64]), 16)
        nc.vector.wait_ge(self.s_ld.h, self.s_ld.v)
        nc.vector.wait_ge(self.s_gw.h, self.s_gw.v)
        nc.vector.memset(self.ONES1[:], 1.0)
        nc.vector.memset(self.EPSC[:], LN_EPS)
        nc.vector.memset(self.ONESB[:], 1.0)
        nc.vector.memset(self.ONE2F[:], 1.0)
        nc.vector.tensor_copy(self.QKVBV[:], self.QKVB[:, 16:24])
        nc.vector.tensor_copy(self.L1BV[:], self.L1BS[:, 16:24])

    def p_ln(self):
        self._ln_n = getattr(self, "_ln_n", 0) + 1
        """XN = (X - mu) * rsqrt(var+eps) in bf16, two 1024-token half-passes.
        PSUM per half: sum(x) banks0-1, sum(x^2) banks4-5; then MUB banks0-1,
        RB banks2-3 broadcasts."""
        nc, X, XN, SM0, PS = self.nc, self.X, self.XN, self.SM0, self.PS
        XSQ = self.palloc(f"XSQ{self._ln_n}", [128, 2, 512], F32)
        LNTS = self.palloc(f"LNTS{self._ln_n}", [128, 2, 512], F32)
        for half in range(2):
            base = half * 1024
            t_sq0 = self.s_tp.v
            sq_tok = []
            for q2 in range(2):
                for dt in range(NDT):
                    i = q2 * NDT + dt
                    if i >= 2:
                        nc.scalar.wait_ge(self.s_tp.h, t_sq0 + (i - 1))
                    ins = nc.scalar.activation(XSQ[:, i % 2, :],
                                               X[:, dt, base + q2 * 512: base + (q2 + 1) * 512], AF.Square)
                    sq_tok.append(self.s_sc.inc(ins))
            for q2 in range(2):
                for dt in range(NDT):
                    i = q2 * NDT + dt
                    nc.tensor.matmul(PS[0:1, q2 * 512:(q2 + 1) * 512], self.ONE2F[:, 0:1],
                                     X[:, dt, base + q2 * 512: base + (q2 + 1) * 512],
                                     start=(dt == 0), stop=(dt == NDT - 1))
                    nc.tensor.wait_ge(self.s_sc.h, sq_tok[i])
                    ins = nc.tensor.matmul(PS[0:1, 2048 + q2 * 512:2048 + (q2 + 1) * 512], self.ONE2F[:, 0:1],
                                           XSQ[:, i % 2, :], start=(dt == 0), stop=(dt == NDT - 1))
                    self.s_tp.inc(ins)
            t_stats = self.s_tp.v
            nc.vector.wait_ge(self.s_tp.h, t_stats)
            nc.vector.tensor_scalar(SM0[0:1, 0:1024], PS[0:1, 0:1024], 1.0 / D, None, op0=OP.mult)
            nc.vector.tensor_scalar(SM0[0:1, 1024:2048], PS[0:1, 2048:3072], 1.0 / D, None, op0=OP.mult)
            nc.vector.drain()
            nc.vector.tensor_tensor(PS[0:1, 0:1024], SM0[0:1, 0:1024], SM0[0:1, 0:1024], op=OP.mult)
            nc.vector.drain()
            v_var = self.s_vc.inc(nc.vector.tensor_tensor(
                SM0[0:1, 1024:2048], SM0[0:1, 1024:2048], PS[0:1, 0:1024], op=OP.subtract))
            nc.scalar.wait_ge(self.s_vc.h, v_var)
            s_sqrt = self.s_sc.inc(nc.scalar.activation(SM0[0:1, 1024:2048], SM0[0:1, 1024:2048],
                                                        AF.Sqrt, bias=self.EPSC[0:1, 0:1]))
            nc.vector.wait_ge(self.s_sc.h, s_sqrt)
            v_rstd = self.s_vc.inc(nc.vector.reciprocal(SM0[0:1, 1024:2048], SM0[0:1, 1024:2048]))
            nc.tensor.wait_ge(self.s_vc.h, v_rstd)
            t_bc = None
            for off in (0, 512):
                nc.tensor.matmul(PS[:, off:off + 512], self.ONES1[0:1, 0:128],
                                 SM0[0:1, off:off + 512], start=True, stop=True)
                ins = nc.tensor.matmul(PS[:, 1024 + off:1536 + off], self.ONES1[0:1, 0:128],
                                       SM0[0:1, 1024 + off:1536 + off], start=True, stop=True)
                t_bc = self.s_tp.inc(ins)
            nc.vector.wait_ge(self.s_tp.h, t_bc)
            lastv = None
            for dt in range(NDT):
                for q2 in range(2):
                    sl = slice(base + q2 * 512, base + (q2 + 1) * 512)
                    nc.vector.tensor_tensor(LNTS[:, q2, :], X[:, dt, sl],
                                            PS[:, q2 * 512:(q2 + 1) * 512], op=OP.subtract)
                    nc.vector.drain()
                    ins = nc.vector.tensor_tensor(XN[:, dt, sl], LNTS[:, q2, :],
                                                  PS[:, 1024 + q2 * 512:1536 + q2 * 512], op=OP.mult)
                    lastv = self.s_vc.inc(ins)
            nc.tensor.wait_ge(self.s_vc.h, lastv)
        self.pfree(2)

    def p_attn(self, layer):
        nc, PS, XN = self.nc, self.PS, self.XN
        O = self.palloc(f"O{layer}", [128, NDT, T], BF16)
        QT = self.palloc(f"QT{layer}", [128, 4, T], BF16)
        KT = self.palloc(f"KT{layer}", [128, 4, T], BF16)
        V = self.palloc(f"V{layer}", [128, NTT, 512], BF16)
        EB = self.palloc(f"EB{layer}", [128, 2, 1024], BF16)
        SQB = EB[:, :, :].rearrange("p a b -> p (a b)")
        self.DIBS = self.T1[:, 1, :]
        if layer == 0:
            WR = self.palloc("WR", [128, 2, 8, 128], BF16)
        else:
            self.QI = self.palloc("QIa", [128, 256], I32)
            WQ = self.palloc("WQ", [128, 8, 512], BF16)
            STG = self.palloc("STG", [128, 2, 512], F32)
        self.o_tok = [[None] * NQC for _ in range(NDT)]

        slot_free = [None, None]
        for g in range(NG):
            nc.tensor.wait_ge(self.s_sc.h, self.s_sc.v)
            nc.tensor.wait_ge(self.s_vc.h, self.s_vc.v)
            if layer == 0:
                chunks = []
                for m in range(3):
                    for ftl in range(4):
                        chunks.append((m, ftl, m * 8 + g * 4 + ftl))
                for ci, (m, ftl, fc) in enumerate(chunks):
                    sl = ci % 2
                    if slot_free[sl] is not None:
                        nc.gpsimd.wait_ge(self.s_tp.h, slot_free[sl])
                    sem = self.s_wr[sl]
                    gw = (sem, sem.inc(nc.gpsimd.dma_start(WR[:, sl, :, :], self.qkvt[fc]), 16))
                    self.emit_qkv_chunk(layer, g, m, ftl, WR[:, sl, :, :], gw, QT, KT, V)
                    slot_free[sl] = self.s_tp.v
            else:
                for m in range(3):
                    self.quant_l1(m, g, WQ, STG)
                    self.emit_qkv_chunk(layer, g, m, None, WQ, None, QT, KT, V)
            self.emit_mhat(QT, KT, SQB)
            for lt in range(4):
                for qc in range(NQC):
                    self.emit_attn_pair(layer, g, lt, qc, QT, KT, V, EB, O)
        if self.dump in ("QK", "V"):
            pass
        else:
            self.pfree(len(self.pool) - 1)

    def emit_qkv_chunk(self, layer, g, m, ftl, W, gw_tok, QT, KT, V):
        nc, PS, XN = self.nc, self.PS, self.XN
        ftls = [ftl] if layer == 0 else list(range(4))
        last_tok = None
        for lf in ftls:
            wsl = (lambda dt: W[:, dt, lf * 128:(lf + 1) * 128]) if layer == 1 else \
                  (lambda dt: W[:, dt, :])
            fglob = (m * 8 if layer == 0 else m * 8) + g * 4 + lf
            if m < 2:
                dst = QT if m == 0 else KT
                bias = self.QKVB[:, fglob:fglob + 1] if layer == 0 else \
                    self.L1BS[:, m * 8 + g * 4 + lf: m * 8 + g * 4 + lf + 1]
                for qc in range(NQC):
                    bk = self.bank(qc % 2)
                    if self.qkv_bank_tok[qc % 2] is not None:
                        nc.tensor.wait_ge(self.s_vc.h, self.qkv_bank_tok[qc % 2])
                    if gw_tok is not None and qc == 0:
                        nc.tensor.wait_ge(gw_tok[0].h, gw_tok[1])
                    for dt in range(NDT):
                        ins = nc.tensor.matmul(bk, wsl(dt), XN[:, dt, qc * 512:(qc + 1) * 512],
                                               start=(dt == 0), stop=(dt == NDT - 1))
                    t = self.s_tp.inc(ins)
                    nc.vector.wait_ge(self.s_tp.h, t)
                    ins = nc.vector.tensor_scalar(dst[:, lf, qc * 512:(qc + 1) * 512], bk,
                                                  bias, None, op0=OP.add)
                    self.qkv_bank_tok[qc % 2] = self.s_vc.inc(ins)
                    last_tok = self.qkv_bank_tok[qc % 2]
            else:
                width = 128 if layer == 0 else 512
                fcol = lf * 128 if layer == 0 else 0
                mov = (lambda dt: W[:, dt, :]) if layer == 0 else (lambda dt: W[:, dt, 0:512])
                for tt in range(NTT):
                    bk = self.bank(2 + tt % 2)[:, 0:width]
                    if self.qkv_bank_tok[2 + tt % 2] is not None:
                        nc.tensor.wait_ge(self.s_vc.h, self.qkv_bank_tok[2 + tt % 2])
                    if gw_tok is not None and tt == 0:
                        nc.tensor.wait_ge(gw_tok[0].h, gw_tok[1])
                    for dt in range(NDT):
                        ins = nc.tensor.matmul(bk, XN[:, dt, tt * 128:(tt + 1) * 128], mov(dt),
                                               start=(dt == 0), stop=(dt == NDT - 1))
                    t = self.s_tp.inc(ins)
                    nc.vector.wait_ge(self.s_tp.h, t)
                    ins = nc.vector.tensor_copy(V[:, tt, fcol:fcol + width], bk)
                    self.qkv_bank_tok[2 + tt % 2] = self.s_vc.inc(ins)
                    last_tok = self.qkv_bank_tok[2 + tt % 2]
                if layer == 1:
                    break
        return last_tok

    def quant_l1(self, m, g, WQ, STG):
        nc, PS = self.nc, self.PS
        nc.vector.wait_ge(self.s_tp.h, self.s_tp.v)
        nc.vector.wait_ge(self.s_gr.h, self.s_gr.v)
        for ft in range(4):
            fr = (g * 4 + ft) * 128
            stgv = STG[:, :, :].rearrange("p a b -> p (a b)")
            nc.sync.wait_ge(self.s_vc.h, self.s_vc.v)
            ld = self.s_q[0].inc(nc.sync.dma_start(stgv, self.l1w[m, fr:fr + 128, :]), 16)
            nc.vector.wait_ge(self.s_q[0].h, ld)
            self.s_vc.inc(nc.vector.reduce_max(self.SCC[:, ft:ft + 1], stgv, axis=AX.X,
                                               apply_absolute_value=True))
        nc.vector.drain()
        nc.vector.tensor_scalar(self.SCC[:, 0:4], self.SCC[:, 0:4], 1.0 / 127.0, FQ_EPS,
                                op0=OP.mult, op1=OP.max)
        nc.vector.drain()
        v0 = self.s_vc.inc(nc.vector.reciprocal(self.SCC[:, 4:8], self.SCC[:, 0:4]))
        nc.tensor.wait_ge(self.s_vc.h, v0)
        t0 = self.s_tp.inc(nc.tensor.transpose(PS[0:8, 1024:1152], self.SCC[:, 0:8], self.EYE[:]))
        nc.scalar.wait_ge(self.s_tp.h, t0)
        s0 = self.s_sc.inc(nc.scalar.copy(self.SCT[:], PS[0:8, 1024:1152]))
        nc.sync.wait_ge(self.s_sc.h, s0)
        nc.sync.wait_ge(self.s_tp.h, self.s_tp.v)
        srow = self.s_gr.inc(nc.sync.dma_start(self.SM0[0:1, 1024:2048], self.SCT[0:8, :]), 16)
        nc.tensor.wait_ge(self.s_gr.h, srow)
        nc.tensor.matmul(self.bank(4), self.ONES1[0:1, 0:128],
                         self.SM0[0:1, 1024:1536], start=True, stop=True)
        ins = nc.tensor.matmul(self.bank(5), self.ONES1[0:1, 0:128],
                               self.SM0[0:1, 1536:2048], start=True, stop=True)
        t1 = self.s_tp.inc(ins)
        nc.vector.wait_ge(self.s_tp.h, t1)
        qtok = [None, None]
        v_base = self.s_vc.v
        for dt in range(NDT):
            sl = dt % 2
            nc.sync.wait_ge(self.s_vc.h, qtok[sl] if qtok[sl] is not None else v_base)
            nc.sync.wait_ge(self.s_q[sl].h, self.s_q[sl].v)
            ld = self.s_q[sl].inc(nc.sync.dma_start(STG[:, sl, 0:512],
                                                 self.l1wt[m, dt * 128:(dt + 1) * 128, g * 512:(g + 1) * 512]), 16)
            nc.vector.wait_ge(self.s_q[sl].h, ld)
            for hc in range(2):
                hs = slice(hc * 256, (hc + 1) * 256)
                nc.vector.tensor_tensor(self.T1[:, 0, hs], STG[:, dt % 2, hs], self.bank(5)[:, hs], op=OP.mult)
                nc.vector.drain()
                nc.vector.tensor_copy(self.QI[:], self.T1[:, 0, hs])
                nc.vector.drain()
                nc.vector.tensor_scalar(self.T1[:, 1, hs], self.QI[:], -127.0, 127.0, op0=OP.max, op1=OP.min)
                nc.vector.drain()
                ins = nc.vector.tensor_tensor(WQ[:, dt, hs], self.T1[:, 1, hs], self.bank(4)[:, hs], op=OP.mult)
            qtok[sl] = self.s_vc.inc(ins)
        nc.tensor.wait_ge(self.s_vc.h, self.s_vc.v)

    def emit_mhat(self, QT, KT, SQB):
        nc, PS, SM0 = self.nc, self.PS, self.SM0
        for si, SRC in ((0, QT), (1, KT)):
            for lt in range(4):
                vs = self.s_vc.inc(nc.vector.tensor_tensor(SQB[:], SRC[:, lt, :], SRC[:, lt, :], op=OP.mult))
                nc.tensor.wait_ge(self.s_vc.h, vs)
                for half in range(2):
                    lh = lt * 2 + half
                    pb = (lh % 2) * 4
                    if self.mh_tok[lh % 2] is not None:
                        nc.tensor.wait_ge(self.s_vc.h, self.mh_tok[lh % 2])
                    for qc in range(NQC):
                        ins = nc.tensor.matmul(PS[0:1, (pb + qc) * 512:(pb + qc + 1) * 512],
                                               self.ONESB[half * 64:half * 64 + 64, 0:1],
                                               SQB[half * 64:half * 64 + 64, qc * 512:(qc + 1) * 512],
                                               start=True, stop=True)
                    t = self.s_tp.inc(ins)
                    nc.vector.wait_ge(self.s_tp.h, t)
                    ins = nc.vector.reduce_max(SM0[0:1, si * 8 + lh:si * 8 + lh + 1],
                                               PS[0:1, pb * 512:(pb + 4) * 512], axis=AX.X)
                    self.mh_tok[lh % 2] = self.s_vc.inc(ins)
        nc.vector.drain()
        v1 = self.s_vc.inc(nc.vector.tensor_tensor(SM0[0:1, 16:24], SM0[0:1, 0:8], SM0[0:1, 8:16], op=OP.mult))
        nc.scalar.wait_ge(self.s_vc.h, v1)
        s1 = self.s_sc.inc(nc.scalar.activation(SM0[0:1, 24:32], SM0[0:1, 16:24], AF.Sqrt))
        nc.vector.wait_ge(self.s_sc.h, s1)
        nc.vector.tensor_scalar(SM0[0:1, 32:40], SM0[0:1, 24:32], -0.125, -1.0, op0=OP.mult, op1=OP.add)
        ev = SM0[0:1, 32:40].rearrange("p (a b) -> p a b", b=2)
        nc.vector.drain()
        v2 = self.s_vc.inc(nc.vector.tensor_tensor(SM0[0:1, 40:44], ev[:, :, 0], ev[:, :, 1], op=OP.min))
        nc.tensor.wait_ge(self.s_vc.h, v2)
        t2 = self.s_tp.inc(nc.tensor.matmul(PS[:, 1024:1028], self.ONES1[0:1, 0:128],
                                            SM0[0:1, 40:44], start=True, stop=True))
        nc.scalar.wait_ge(self.s_tp.h, t2)
        sc = self.s_sc.inc(nc.scalar.copy(self.NEGM8[:], PS[:, 1024:1028]))
        nc.tensor.wait_ge(self.s_sc.h, sc)

    def emit_attn_pair(self, layer, g, lt, qc, QT, KT, V, EB, O):
        nc, PS = self.nc, self.PS
        ha, hb = 2 * lt, 2 * lt + 1
        qsl = slice(qc * 512, (qc + 1) * 512)
        score_tok = [None] * 16
        exp_tok = [None] * 16
        av_tok = [None] * 16
        for kt in range(17):
            if kt < 16:
                r = kt % 2
                sview = PS[:, r * 1024:(r + 1) * 1024]
                if self.sb_tok[r] is not None:
                    nc.tensor.wait_ge(self.s_sc.h, self.sb_tok[r])
                ksl = slice(kt * 128, (kt + 1) * 128)
                nc.tensor.matmul(sview[:, 0:512], KT[0:64, lt, ksl], QT[0:64, lt, qsl], start=True, stop=True)
                ins = nc.tensor.matmul(sview[:, 512:1024], KT[64:128, lt, ksl], QT[64:128, lt, qsl],
                                       start=True, stop=True)
                score_tok[kt] = self.s_tp.inc(ins)
                nc.scalar.wait_ge(self.s_tp.h, score_tok[kt])
                if kt >= 2:
                    nc.scalar.wait_ge(self.s_tp.h, av_tok[kt - 2])
                ins = nc.scalar.activation(EB[:, kt % 2, :], sview, AF.Exp,
                                           bias=self.NEGM8[:, lt:lt + 1], scale=0.125)
                exp_tok[kt] = self.s_sc.inc(ins)
                self.sb_tok[r] = exp_tok[kt]
            if kt >= 1:
                j = kt - 1
                er = j % 2
                nc.tensor.wait_ge(self.s_sc.h, exp_tok[j])
                if j == 0 and self.av_free is not None:
                    nc.tensor.wait_ge(self.s_vc.h, self.av_free)
                st, sp = (j == 0), (j == 15)
                vsl = V[:, j, :]
                nc.tensor.matmul(self.bank(4)[0:64, :], vsl[:, ha * 64:ha * 64 + 64], EB[:, er, 0:512],
                                 start=st, stop=sp, tile_position=(0, 0))
                nc.tensor.matmul(self.bank(4)[64:128, :], vsl[:, hb * 64:hb * 64 + 64], EB[:, er, 512:1024],
                                 start=st, stop=sp, tile_position=(0, 64))
                nc.tensor.matmul(PS[0:1, 3072:3584], self.ONESB[:, 0:1], EB[:, er, 0:512],
                                 start=st, stop=sp, tile_position=(0, 0))
                ins = nc.tensor.matmul(PS[32:33, 3072:3584], self.ONESB[:, 0:1], EB[:, er, 512:1024],
                                       start=st, stop=sp, tile_position=(0, 32))
                av_tok[j] = self.s_tp.inc(ins)
        nc.scalar.wait_ge(self.s_tp.h, av_tok[15])
        nc.scalar.copy(self.SM0[0:1, 1024:1536], PS[0:1, 3072:3584])
        sden = self.s_sc.inc(nc.scalar.copy(self.SM0[32:33, 1536:2048], PS[32:33, 3072:3584]))
        nc.sync.wait_ge(self.s_sc.h, sden)
        self.s_gr.inc(nc.sync.dma_start(self.SCC[:, 0:4], self.SM0[0:1, 1024:1536]), 16)
        g1 = self.s_gr.inc(nc.sync.dma_start(self.SCC[:, 4:8], self.SM0[32:33, 1536:2048]), 16)
        nc.vector.wait_ge(self.s_gr.h, g1)
        vrc = self.s_vc.inc(nc.vector.reciprocal(self.SCC[:, 0:8], self.SCC[:, 0:8]))
        nc.sync.wait_ge(self.s_vc.h, vrc)
        self.s_gr.inc(nc.sync.dma_start(self.SM0[0:1, 1024:1536], self.SCC[:, 0:4]), 16)
        vr = self.s_gr.inc(nc.sync.dma_start(self.SM0[0:1, 1536:2048], self.SCC[:, 4:8]), 16)
        nc.tensor.wait_ge(self.s_gr.h, vr)
        nc.tensor.matmul(self.bank(7)[0:64, :], self.ONES1[0:1, 0:64], self.SM0[0:1, 1024:1536],
                         start=True, stop=True, tile_position=(0, 0))
        tb = self.s_tp.inc(nc.tensor.matmul(self.bank(7)[64:128, :], self.ONES1[0:1, 0:64],
                                            self.SM0[0:1, 1536:2048], start=True, stop=True,
                                            tile_position=(0, 64)))
        nc.vector.wait_ge(self.s_tp.h, tb)
        nc.vector.tensor_copy(self.DIBS[:], self.bank(7))
        nc.vector.drain()
        ins = nc.vector.tensor_tensor(O[:, g * 4 + lt, qsl], self.bank(4), self.DIBS[:], op=OP.mult)
        self.av_free = self.s_vc.inc(ins)
        self.o_tok[g * 4 + lt][qc] = self.av_free

    def p_outproj(self, layer):
        nc, PS = self.nc, self.PS
        O = self.pool_handles[f"O{layer}"]
        if layer == 0:
            WR = self.palloc("WOR", [128, 2, 8, 128], BF16)
        else:
            self.QI = self.palloc("QIp", [128, 256], I32)
            WQ = self.palloc("WPQ", [128, 8, 512], BF16)
            STG = self.palloc("STGP", [128, 2, 512], F32)
        bv16 = self.QKVBV if layer == 0 else self.L1BV
        slot_free = [None, None]
        for g in range(NG):
            if layer == 1:
                self.quant_l1(3, g, WQ, STG)
            for lf in range(4):
                ft = g * 4 + lf
                if layer == 0:
                    sl = ft % 2
                    if slot_free[sl] is not None:
                        nc.gpsimd.wait_ge(self.s_tp.h, slot_free[sl])
                    sem = self.s_wr[sl]
                    gw = (sem, sem.inc(nc.gpsimd.dma_start(WR[:, sl, :, :], self.wot[ft]), 16))
                    wsl = lambda dt: WR[:, sl, dt, :]
                else:
                    gw = None
                    wsl = lambda dt: WQ[:, dt, lf * 128:(lf + 1) * 128]
                if gw is not None:
                    nc.tensor.wait_ge(gw[0].h, gw[1])
                for dt in range(NDT):
                    ins = nc.tensor.matmul(PS[:, 1024 + ft:1024 + ft + 1], wsl(dt), bv16[:, dt:dt + 1],
                                           start=(dt == 0), stop=(dt == NDT - 1))
                tb = self.s_tp.inc(ins)
                nc.vector.wait_ge(self.s_tp.h, tb)
                base_b = self.WOBS[:, ft:ft + 1] if layer == 0 else self.L1BS[:, 24 + ft:25 + ft]
                vb = self.s_vc.inc(nc.vector.scalar_tensor_tensor(
                    self.WEB[:, ft:ft + 1], PS[:, 1024 + ft:1024 + ft + 1], 1.0, base_b,
                    op0=OP.mult, op1=OP.add))
                for qc in range(NQC):
                    r = qc % 2
                    if self.op_bank_tok[r] is not None:
                        nc.tensor.wait_ge(self.s_vc.h, self.op_bank_tok[r])
                    for dt in range(NDT):
                        nc.tensor.wait_ge(self.s_vc.h, self.o_tok[dt][qc] or 0)
                        ins = nc.tensor.matmul(self.bank(r), wsl(dt), O[:, dt, qc * 512:(qc + 1) * 512],
                                               start=(dt == 0), stop=(dt == NDT - 1))
                    t = self.s_tp.inc(ins)
                    nc.vector.wait_ge(self.s_tp.h, t)
                    nc.vector.wait_ge(self.s_vc.h, vb)
                    nc.vector.tensor_scalar(self.T1[:, r, :], self.bank(r), self.WEB[:, ft:ft + 1],
                                            None, op0=OP.add)
                    nc.vector.drain()
                    ins = nc.vector.tensor_tensor(self.X[:, ft, qc * 512:(qc + 1) * 512],
                                                  self.X[:, ft, qc * 512:(qc + 1) * 512],
                                                  self.T1[:, r, :], op=OP.add)
                    self.op_bank_tok[r] = self.s_vc.inc(ins)
                slot_free_tok = self.s_tp.v
                if layer == 0:
                    slot_free[ft % 2] = slot_free_tok
        self.pfree()

    def p_router(self):
        nc, PS = self.nc, self.PS
        self.LGT = self.palloc("LGT", [128, 16, 8], F32)
        rt_rot = [None, None]
        for tt in range(NTT):
            r = tt % 2
            if rt_rot[r] is not None:
                nc.tensor.wait_ge(self.s_sc.h, rt_rot[r])
            for dt in range(NDT):
                ins = nc.tensor.matmul(PS[:, r * 512:r * 512 + 8], self.XN[:, dt, tt * 128:(tt + 1) * 128],
                                       self.RW[:, dt, :], start=(dt == 0), stop=(dt == NDT - 1))
            t = self.s_tp.inc(ins)
            nc.scalar.wait_ge(self.s_tp.h, t)
            rt_rot[r] = self.s_sc.inc(nc.scalar.copy(self.LGT[:, tt, :], PS[:, r * 512:r * 512 + 8]))
        self.rt_tok[0] = rt_rot[(NTT - 1) % 2]
        for tt in range(NTT):
            L = self.LGT[:, tt, :]
            nc.vector.wait_ge(self.s_sc.h, self.rt_tok[0])
            v = self.s_vc.inc(nc.vector.reduce_max(self.SMLL[:, 0:1], L, axis=AX.X, negate=True))
            nc.scalar.wait_ge(self.s_vc.h, v)
            s = self.s_sc.inc(nc.scalar.activation(L, L, AF.Exp, bias=self.SMLL[:, 0:1],
                                                   accum_out=self.SMLL[:, 1:2]))
            nc.vector.wait_ge(self.s_sc.h, s)
            nc.vector.reciprocal(self.SMLL[:, 1:2], self.SMLL[:, 1:2])
            nc.vector.drain()
            nc.vector.tensor_scalar(L, L, self.SMLL[:, 1:2], None, op0=OP.mult)
            nc.vector.drain()
            nc.vector.reduce_max(self.SMLL[:, 0:1], L, axis=AX.X)
            nc.vector.drain()
            nc.vector.tensor_scalar(self.T1[:, 0, 0:8], L, self.SMLL[:, 0:1], None, op0=OP.is_ge)
            nc.vector.drain()
            nc.vector.scalar_tensor_tensor(self.T1[:, 1, 0:8], self.T1[:, 0, 0:8], -BIG, L,
                                           op0=OP.mult, op1=OP.add)
            nc.vector.drain()
            nc.vector.reduce_max(self.SMLL[:, 2:3], self.T1[:, 1, 0:8], axis=AX.X)
            nc.vector.drain()
            nc.vector.tensor_scalar(self.T1[:, 0, 0:8], L, self.SMLL[:, 2:3], None, op0=OP.is_ge)
            nc.vector.drain()
            vg = self.s_vc.inc(nc.vector.tensor_tensor(L, L, self.T1[:, 0, 0:8], op=OP.mult))
            nc.tensor.wait_ge(self.s_vc.h, vg)
            tt2 = self.s_tp.inc(nc.tensor.transpose(PS[0:8, 512:640], L, self.EYE[:]))
            nc.vector.wait_ge(self.s_tp.h, tt2)
            self.s_vc.inc(nc.vector.tensor_copy(self.SM0[0:8, tt * 128:(tt + 1) * 128], PS[0:8, 512:640]))
        self.pfree(1)

    def p_moe(self):
        nc, PS, XN, X = self.nc, self.PS, self.XN, self.X
        HG = self.palloc("HG", [128, 32, 1024], BF16)
        W1R = self.palloc("W1R", [128, 3, 8, 256], BF16)
        W2R = self.palloc("W2R", [128, 4, 2, 1024], BF16)
        self.GB = self.palloc("GBm", [128, 1024], F32)
        w1_free = [None] * 3
        w2_free = [None] * 4
        bank_free = [None] * 8
        gb_tok = None
        gelu_tok = [None] * 32
        for c in range(2):
            csl = slice(c * 1024, (c + 1) * 1024)
            for e in range(MOE_E):
                if gb_tok is not None:
                    nc.sync.wait_ge(self.s_vc.h, gb_tok)
                grow = self.s_gr.inc(nc.sync.dma_start(
                    self.GB[0:1, 0:1024], self.SM0[e:e + 1, c * 1024:(c + 1) * 1024]), 16)
                for b in (4, 5):
                    if bank_free[b] is not None:
                        nc.tensor.wait_ge(self.s_vc.h, bank_free[b])
                nc.tensor.wait_ge(self.s_gr.h, grow)
                ins = None
                for half in range(2):
                    ins = nc.tensor.matmul(self.bank(4 + half), self.ONES1[0:1, 0:128],
                                           self.GB[0:1, half * 512:(half + 1) * 512],
                                           start=True, stop=True)
                tgb = self.s_tp.inc(ins)
                nc.scalar.wait_ge(self.s_tp.h, tgb)
                if gb_tok is not None:
                    nc.scalar.wait_ge(self.s_vc.h, gb_tok)
                sgb = self.s_sc.inc(nc.scalar.copy(self.GB[:], self.bank(4, 2)))
                for ht in range(32):
                    sl = (ht // 2) % 3
                    if ht % 2 == 0:
                        if w1_free[sl] is not None:
                            nc.gpsimd.wait_ge(self.s_tp.h, w1_free[sl])
                        self.w1_tok[sl] = self.s_w1[sl].inc(
                            nc.gpsimd.dma_start(W1R[:, sl, :, :], self.w1t[e, ht // 2]), 16)
                    pb = (ht % 2) * 2
                    if ht >= 2 and gelu_tok[ht - 2] is not None:
                        nc.tensor.wait_ge(self.s_sc.h, gelu_tok[ht - 2])
                    for b in (pb, pb + 1):
                        if bank_free[b] is not None:
                            nc.tensor.wait_ge(self.s_vc.h, bank_free[b])
                            bank_free[b] = None
                    nc.tensor.wait_ge(self.s_w1[sl].h, self.w1_tok[sl])
                    for nh in range(2):
                        for dt in range(NDT):
                            ins = nc.tensor.matmul(self.bank(pb + nh),
                                                   W1R[:, sl, dt, (ht % 2) * 128:(ht % 2) * 128 + 128],
                                                   XN[:, dt, c * 1024 + nh * 512: c * 1024 + (nh + 1) * 512],
                                                   start=(dt == 0), stop=(dt == NDT - 1))
                    t = self.s_tp.inc(ins)
                    if ht % 2 == 1:
                        w1_free[sl] = t
                    nc.scalar.wait_ge(self.s_tp.h, t)
                    ins = nc.scalar.activation(HG[:, ht, :], self.bank(pb, 2), AF.Gelu)
                    gelu_tok[ht] = self.s_sc.inc(ins)
                hg_done = gelu_tok[31]
                for nh in range(2):
                    for ht in range(32):
                        sl2 = (ht // 2) % 4
                        if ht % 2 == 0:
                            if w2_free[sl2] is not None:
                                nc.gpsimd.wait_ge(self.s_tp.h, w2_free[sl2])
                            hq = ht // 4
                            self.w2_tok[sl2] = self.s_w2[sl2].inc(
                                nc.gpsimd.dma_start(W2R[:, sl2, :, :],
                                                    self.w2t[e, hq, :, (ht % 4) // 2 * 2:(ht % 4) // 2 * 2 + 2, :]), 16)
                        if ht == 0:
                            nc.tensor.wait_ge(self.s_sc.h, hg_done)
                            nc.tensor.wait_ge(self.s_sc.h, sgb)
                            for b in range(8):
                                if bank_free[b] is not None:
                                    nc.tensor.wait_ge(self.s_vc.h, bank_free[b])
                                    bank_free[b] = None
                            nc.tensor.wait_ge(self.s_w2[sl2].h, self.w2_tok[sl2])
                        elif ht % 2 == 0:
                            nc.tensor.wait_ge(self.s_w2[sl2].h, self.w2_tok[sl2])
                        for dt in range(NDT):
                            ins = nc.tensor.matmul(self.bank(dt), W2R[:, sl2, ht % 2, dt * 128:(dt + 1) * 128],
                                                   HG[:, ht, nh * 512:(nh + 1) * 512],
                                                   start=(ht == 0), stop=(ht == 31))
                            if ht == 31:
                                tok = self.s_tp.inc(ins)
                                nc.vector.wait_ge(self.s_tp.h, tok)
                                nc.vector.tensor_tensor(self.T1[:, dt % 2, :], self.bank(dt),
                                                        self.GB[:, nh * 512:(nh + 1) * 512], op=OP.mult)
                                nc.vector.drain()
                                xsl = slice(c * 1024 + nh * 512, c * 1024 + (nh + 1) * 512)
                                ins2 = nc.vector.tensor_tensor(X[:, dt, xsl], X[:, dt, xsl],
                                                               self.T1[:, dt % 2, :], op=OP.add)
                                bank_free[dt] = self.s_vc.inc(ins2)
                        if ht == 31:
                            w2_free[sl2] = self.s_tp.v
                        elif ht % 2 == 1:
                            w2_free[sl2] = self.s_tp.inc(ins)
                gb_tok = bank_free[7]
        self.pfree()

    def p_ffn1(self):
        nc, PS, XN, X = self.nc, self.PS, self.XN, self.X
        WQ1 = self.palloc("WQ1", [128, 8, 4096], FP8)
        WQ2 = self.palloc("WQ2", [128, 32, 1024], FP8)
        self.RBS = self.palloc("RBSf", [128, 256], F32)
        self.MBS = self.palloc("MBSf", [128, 256], F32)
        LNC = self.palloc("LNC", [128, 32, 2], BF16)
        G = self.palloc("G", [128, 32, 256], BF16)
        SQGF = self.palloc("SQGF", [128, 4, 256], BF16)
        QSC = self.palloc("QSC", [128, 2, 512], F32)
        STGF = self.palloc("STGF", [128, 2, 1024], F32)
        nc.vector.tensor_copy(LNC[:, :, 0], self.LNBB16[:])
        nc.vector.tensor_copy(LNC[:, :, 1], self.LNWB16[:])
        slot_tok = [None, None]
        self.wq1_co = {}
        for wi, (src, ntile, width, WQ) in ((1, (self.wprt, 32, 1024, WQ2)),
                                            (0, (self.wfct, 8, 4096, WQ1))):
            nc.vector.memset(self.SMLL[:, 3:4], 0.0)
            wu = min(1024, width)
            units = [(it, co) for co in range(0, width, wu) for it in range(ntile)]
            nhalf = len(units)
            for ih, (it, co) in enumerate(units):
                sl = ih % 2
                if slot_tok[sl] is not None:
                    nc.sync.wait_ge(self.s_vc.h, slot_tok[sl])
                ld = self.s_q[sl].inc(nc.sync.dma_start(
                    STGF[:, sl, 0:wu], src[it * 128:it * 128 + 128, co:co + wu]), 16)
                nc.vector.wait_ge(self.s_q[sl].h, ld)
                nc.vector.reduce_sum(self.SMLL[:, 2:3], STGF[:, sl, 0:wu], axis=AX.X,
                                     apply_absolute_value=True)
                nc.vector.drain()
                v = nc.vector.tensor_tensor(self.SMLL[:, 3:4], self.SMLL[:, 3:4],
                                            self.SMLL[:, 2:3], op=OP.add)
                slot_tok[sl] = self.s_vc.inc(v)
                nc.vector.drain()
            vsum = slot_tok[(nhalf - 1) % 2]
            nc.tensor.wait_ge(self.s_vc.h, vsum)
            tg = self.s_tp.inc(nc.tensor.matmul(PS[0:1, 1024:1025], self.ONE2F[:, 0:1],
                                                self.SMLL[:, 3:4], start=True, stop=True))
            nc.vector.wait_ge(self.s_tp.h, tg)
            nc.vector.tensor_scalar(self.SM0[0:1, 48:49], PS[0:1, 1024:1025], 1.0 / (4096.0 * 1024.0),
                                    AM_EPS, op0=OP.mult, op1=OP.add)
            nc.vector.drain()
            vg = self.s_vc.inc(nc.vector.reciprocal(self.SM0[0:1, 48:49], self.SM0[0:1, 48:49]))
            nc.tensor.wait_ge(self.s_vc.h, vg)
            tb = self.s_tp.inc(nc.tensor.matmul(PS[:, 1024:1025], self.ONES1[0:1, 0:128],
                                                self.SM0[0:1, 48:49], start=True, stop=True))
            nc.scalar.wait_ge(self.s_tp.h, tb)
            sg = self.s_sc.inc(nc.scalar.copy(self.INVGC[:, wi:wi + 1], PS[:, 1024:1025]))
            nc.vector.wait_ge(self.s_sc.h, sg)
            for ih, (it, co) in enumerate(units):
                sl = ih % 2
                if slot_tok[sl] is not None:
                    nc.sync.wait_ge(self.s_vc.h, slot_tok[sl])
                ld = self.s_q[sl].inc(nc.sync.dma_start(
                    STGF[:, sl, 0:wu], src[it * 128:it * 128 + 128, co:co + wu]), 16)
                nc.vector.wait_ge(self.s_q[sl].h, ld)
                for ch in range(wu // 512):
                    cs = slice(ch * 512, (ch + 1) * 512)
                    nc.vector.drain()
                    nc.vector.tensor_scalar(QSC[:, 0, :], STGF[:, sl, cs], self.INVGC[:, wi:wi + 1], 0.5,
                                            op0=OP.mult, op1=OP.is_gt)
                    nc.vector.tensor_scalar(QSC[:, 1, :], STGF[:, sl, cs], self.INVGC[:, wi:wi + 1], -0.5,
                                            op0=OP.mult, op1=OP.is_lt)
                    nc.vector.drain()
                    col = co + ch * 512
                    v = nc.vector.tensor_tensor(WQ[:, it, col:col + 512], QSC[:, 0, :], QSC[:, 1, :],
                                                op=OP.subtract)
                    slot_tok[sl] = self.s_vc.inc(v)
                if ih == nhalf - 1 or (wi == 0 and ih == nhalf // 2 - 1):
                    self.wq1_co[(wi, ih)] = self.s_vc.v
            if wi == 1:
                wq2_done = self.s_vc.v
        vq_done = self.s_vc.v
        wq1_half = self.wq1_co.get((0, len(units) // 2 - 1), vq_done)
        nc.tensor.wait_ge(self.s_vc.h, wq2_done)
        for dt in range(NDT):
            for ht in range(32):
                ins = nc.tensor.matmul(PS[:, 1536 + 2 * dt:1538 + 2 * dt],
                                       WQ2[:, ht, dt * 128:(dt + 1) * 128], LNC[:, ht, :],
                                       start=(ht == 0), stop=(ht == 31))
        tpb = self.s_tp.inc(ins)
        nc.vector.wait_ge(self.s_tp.h, tpb)
        vpb = self.s_vc.inc(nc.vector.tensor_copy(self.PBIA[:], PS[:, 1536:1552]))
        self.pfree(2)
        W = 256
        gelu_tok = [None] * 32
        sq_tok = [None] * 32
        fold_tok = [None] * 32
        bank_free = [None] * 8
        for c in range(T // W):
            csl = slice(c * W, (c + 1) * W)
            stat_free = [None] * 4
            if c == 0:
                nc.tensor.wait_ge(self.s_vc.h, wq1_half)
            def emit_stats(ht):
                nc.tensor.wait_ge(self.s_vc.h, sq_tok[ht])
                if ht == 0:
                    for b in (4, 5):
                        if bank_free[b] is not None:
                            nc.tensor.wait_ge(self.s_vc.h, bank_free[b])
                            bank_free[b] = None
                nc.tensor.matmul(PS[0:1, 2048:2048 + W], self.ONESB[:, 0:1], G[:, ht, :],
                                 start=(ht == 0), stop=(ht == 31))
                ins = nc.tensor.matmul(PS[0:1, 2560:2560 + W], self.ONESB[:, 0:1], SQGF[:, ht % 4, :],
                                       start=(ht == 0), stop=(ht == 31))
                st2 = self.s_tp.inc(ins)
                stat_free[ht % 4] = st2
                nc.vector.wait_ge(self.s_tp.h, st2)
                fold_tok[ht] = self.s_vc.inc(nc.vector.tensor_scalar(
                    G[:, ht, :], G[:, ht, :], self.LNW[:, ht:ht + 1], None, op0=OP.mult))
            for ht in range(32):
                pb = ht % 2
                if ht >= 2 and gelu_tok[ht - 2] is not None:
                    nc.tensor.wait_ge(self.s_sc.h, gelu_tok[ht - 2])
                if bank_free[pb] is not None:
                    nc.tensor.wait_ge(self.s_vc.h, bank_free[pb])
                    bank_free[pb] = None
                if c == 0 and ht == 16:
                    nc.tensor.wait_ge(self.s_vc.h, vq_done)
                for dt in range(NDT):
                    ins = nc.tensor.matmul(self.bank(pb)[:, 0:W], WQ1[:, dt, ht * 128:(ht + 1) * 128],
                                           XN[:, dt, csl], start=(dt == 0), stop=(dt == NDT - 1))
                t = self.s_tp.inc(ins)
                nc.scalar.wait_ge(self.s_tp.h, t)
                gelu_tok[ht] = self.s_sc.inc(nc.scalar.activation(G[:, ht, :], self.bank(pb)[:, 0:W], AF.Gelu))
                nc.vector.wait_ge(self.s_sc.h, gelu_tok[ht])
                if stat_free[ht % 4] is not None:
                    nc.vector.wait_ge(self.s_tp.h, stat_free[ht % 4])
                sq_tok[ht] = self.s_vc.inc(nc.vector.tensor_tensor(
                    SQGF[:, ht % 4, :], G[:, ht, :], G[:, ht, :], op=OP.mult))
                if ht >= 2:
                    emit_stats(ht - 2)
            emit_stats(30)
            emit_stats(31)
            nc.vector.tensor_scalar(self.SM0[0:1, 0:W], PS[0:1, 2048:2048 + W], 1.0 / 4096.0, None, op0=OP.mult)
            nc.vector.tensor_scalar(self.SM0[0:1, 512:512 + W], PS[0:1, 2560:2560 + W], 1.0 / 4096.0,
                                    None, op0=OP.mult)
            nc.vector.drain()
            nc.vector.tensor_tensor(PS[0:1, 2048:2048 + W], self.SM0[0:1, 0:W], self.SM0[0:1, 0:W], op=OP.mult)
            nc.vector.drain()
            vv = self.s_vc.inc(nc.vector.tensor_tensor(self.SM0[0:1, 512:512 + W], self.SM0[0:1, 512:512 + W],
                                                       PS[0:1, 2048:2048 + W], op=OP.subtract))
            nc.scalar.wait_ge(self.s_vc.h, vv)
            ss = self.s_sc.inc(nc.scalar.activation(self.SM0[0:1, 512:512 + W], self.SM0[0:1, 512:512 + W],
                                                    AF.Sqrt, bias=self.EPSC[0:1, 0:1]))
            nc.vector.wait_ge(self.s_sc.h, ss)
            nc.vector.reciprocal(self.SM0[0:1, 512:512 + W], self.SM0[0:1, 512:512 + W])
            nc.vector.drain()
            vr = self.s_vc.inc(nc.vector.scalar_tensor_tensor(self.SM0[0:1, 1024:1024 + W], self.SM0[0:1, 0:W],
                                                              -1.0, self.SM0[0:1, 512:512 + W],
                                                              op0=OP.mult, op1=OP.mult))
            nc.tensor.wait_ge(self.s_vc.h, vr)
            for b in (6, 7):
                if bank_free[b] is not None:
                    nc.tensor.wait_ge(self.s_vc.h, bank_free[b])
                    bank_free[b] = None
            nc.tensor.matmul(self.bank(6)[:, 0:W], self.ONES1[0:1, 0:128], self.SM0[0:1, 512:512 + W],
                             start=True, stop=True)
            tb2 = self.s_tp.inc(nc.tensor.matmul(self.bank(7)[:, 0:W], self.ONES1[0:1, 0:128],
                                                 self.SM0[0:1, 1024:1024 + W], start=True, stop=True))
            nc.scalar.wait_ge(self.s_tp.h, tb2)
            nc.scalar.copy(self.RBS[:, 0:W], self.bank(6)[:, 0:W])
            srb = self.s_sc.inc(nc.scalar.copy(self.MBS[:, 0:W], self.bank(7)[:, 0:W]))
            for ht in range(32):
                if ht == 0:
                    nc.tensor.wait_ge(self.s_vc.h, fold_tok[31])
                    nc.tensor.wait_ge(self.s_vc.h, vpb)
                    nc.tensor.wait_ge(self.s_sc.h, srb)
                    for b in range(8):
                        if bank_free[b] is not None:
                            nc.tensor.wait_ge(self.s_vc.h, bank_free[b])
                            bank_free[b] = None
                for dt in range(NDT):
                    ins = nc.tensor.matmul(self.bank(dt)[:, 0:W], WQ2[:, ht, dt * 128:(dt + 1) * 128],
                                           G[:, ht, :], start=(ht == 0), stop=(ht == 31))
                    if ht == 31:
                        tok = self.s_tp.inc(ins)
                        nc.vector.wait_ge(self.s_tp.h, tok)
                        nc.vector.tensor_tensor(self.T1[:, dt % 2, 0:W], self.bank(dt)[:, 0:W],
                                                self.RBS[:, 0:W], op=OP.mult)
                        nc.vector.drain()
                        nc.vector.scalar_tensor_tensor(self.T1[:, dt % 2, 256:256 + W], self.MBS[:, 0:W],
                                                       self.PBIA[:, 2 * dt + 1:2 * dt + 2], self.T1[:, dt % 2, 0:W],
                                                       op0=OP.mult, op1=OP.add)
                        nc.vector.drain()
                        nc.vector.tensor_scalar(self.T1[:, dt % 2, 0:W], self.T1[:, dt % 2, 256:256 + W],
                                                self.PBIA[:, 2 * dt:2 * dt + 1], None, op0=OP.add)
                        nc.vector.drain()
                        ins2 = nc.vector.tensor_tensor(X[:, dt, csl], X[:, dt, csl],
                                                       self.T1[:, dt % 2, 0:W], op=OP.add)
                        bank_free[dt] = self.s_vc.inc(ins2)
            self.mm2_tok = self.s_tp.v
        self.pfree()

    def p_store(self):
        nc = self.nc
        for se in ([self.s_tp, self.s_sc, self.s_vc, self.s_gw, self.s_ld, self.s_gr]
                   + self.s_wr + self.s_w1 + self.s_w2 + self.s_q):
            nc.sync.wait_ge(se.h, se.v)
            nc.gpsimd.wait_ge(se.h, se.v)
        if self.dump is None:
            for dt in range(NDT):
                self.s_st.inc(nc.sync.dma_start(self.outT[dt * 128:(dt + 1) * 128, :], self.X[:, dt, :]), 16)
            nc.sync.wait_ge(self.s_st.h, self.s_st.v)
        elif self.dump == "QK":
            for i, src in enumerate((self.pool_handles[f"QT{self.dump_layer}"], self.pool_handles[f"KT{self.dump_layer}"])):
                for lt in range(4):
                    self.s_gw.inc(nc.gpsimd.dma_start(
                        self.outT[(i * 4 + lt) * 128:(i * 4 + lt + 1) * 128, :], src[:, lt, :]), 16)
            nc.gpsimd.wait_ge(self.s_gw.h, self.s_gw.v)
            nc.sync.wait_ge(self.s_gw.h, self.s_gw.v)
        elif self.dump == "V":
            V = self.pool_handles[f"V{self.dump_layer}"]
            for i in range(4):
                for j in range(4):
                    self.s_gw.inc(nc.gpsimd.dma_start(
                        self.outT[i * 128:(i + 1) * 128, j * 512:(j + 1) * 512], V[:, i * 4 + j, :]), 16)
            nc.gpsimd.wait_ge(self.s_gw.h, self.s_gw.v)
            nc.sync.wait_ge(self.s_gw.h, self.s_gw.v)
        else:
            src = {"XN": self.XN}.get(self.dump) if self.dump in ("XN",) else self.pool_handles[self.dump]
            for dt in range(NDT):
                self.s_gw.inc(nc.gpsimd.dma_start(self.outT[dt * 128:(dt + 1) * 128, :], src[:, dt, :]), 16)
            nc.gpsimd.wait_ge(self.s_gw.h, self.s_gw.v)
            nc.sync.wait_ge(self.s_gw.h, self.s_gw.v)


def build_program(phases=None, dump=None):
    mk = MK(phases, dump)
    mk.qkv_bank_tok = [None] * 4
    mk.mh_tok = [None, None]
    mk.sb_tok = [None, None]
    mk.av_free = None
    mk.op_bank_tok = [None, None]
    mk.rt_tok = [None]
    mk.w1_tok = [None] * 3
    mk.w2_tok = [None] * 4
    mk.pool_handles = {}
    mk.mm2_tok = None
    _orig = mk.palloc

    def palloc2(name, shape, dtype):
        h = _orig(name, shape, dtype)
        mk.pool_handles[name] = h
        return h
    mk.palloc = palloc2
    nc = mk.build()
    return nc


def host_inputs(b_x, ins):
    d = {}
    d["xT"] = np.ascontiguousarray(b_x.T)
    qkv = ins["L0_attn_qkv_w"]
    d["qkvt"] = np.ascontiguousarray(qkv.reshape(24, 128, 8, 128).transpose(0, 3, 2, 1))
    d["qkvb"] = np.ascontiguousarray(ins["L0_attn_qkv_b"].reshape(24, 128).T)
    wo = ins["L0_attn_out_w"]
    d["wot"] = np.ascontiguousarray(wo.reshape(8, 128, 8, 128).transpose(0, 3, 2, 1))
    d["wob"] = np.ascontiguousarray(ins["L0_attn_out_b"].reshape(8, 128).T)
    d["rwt"] = np.ascontiguousarray(ins["L0_ffn_router_w"].T.reshape(8, 128, 8).transpose(1, 0, 2))
    w1 = ins["L0_ffn_w1"]
    d["w1t"] = np.ascontiguousarray(w1.transpose(0, 2, 1).reshape(8, 8, 128, 16, 256).transpose(0, 3, 2, 1, 4))
    w2 = ins["L0_ffn_w2"]
    d["w2t"] = np.ascontiguousarray(w2.transpose(0, 2, 1).reshape(8, 8, 4, 128, 1024).transpose(0, 1, 3, 2, 4))
    l1 = [ins["L1_attn_wq"], ins["L1_attn_wk"], ins["L1_attn_wv"], ins["L1_attn_wp"]]
    d["l1w"] = np.ascontiguousarray(np.stack(l1))
    d["l1wt"] = np.ascontiguousarray(np.stack([w.T for w in l1]))
    bb = [ins["L1_attn_bq"], ins["L1_attn_bk"], ins["L1_attn_bv"], ins["L1_attn_bp"]]
    d["l1b"] = np.ascontiguousarray(np.concatenate([b.reshape(8, 128).T for b in bb], axis=1))
    d["wfct"] = np.ascontiguousarray(ins["L1_ffn_W_fc"].T)
    d["lnwb"] = np.ascontiguousarray(np.concatenate(
        [ins["L1_ffn_ln_w"].reshape(32, 128).T, ins["L1_ffn_ln_b"].reshape(32, 128).T], axis=1))
    d["wprt"] = np.ascontiguousarray(ins["L1_ffn_W_proj"].T)
    d["eye"] = np.eye(128, dtype=np.float32)
    return {k: v.astype(np.float32, copy=False) for k, v in d.items()}


def kernel(**inputs):
    x = inputs["L0_attn_x"]
    nc = build_program()
    in_maps = [host_inputs(x[b], inputs) for b in range(8)]
    res = run_bass_kernel_spmd(nc, in_maps, core_ids=list(range(8)))
    out = np.stack([np.ascontiguousarray(r["outT"].T) for r in res.results])
    return out.astype(np.float32)
